# version 11
# baseline (speedup 1.0000x reference)
import math
from contextlib import ExitStack

import numpy as np
import ml_dtypes

import concourse.bass as bass
import concourse.mybir as mybir
from concourse.bass_utils import run_bass_kernel_spmd

F32 = mybir.dt.float32
BF16 = mybir.dt.bfloat16
AF = mybir.ActivationFunctionType
ALU = mybir.AluOpType
AX = mybir.AxisListType
BFD = ml_dtypes.bfloat16

NCORES = 8
NB = 2
T = 2048
D = 1024
H = 16
DH = 64
NT = T // 128
NI = D // 128
NG = NB * NT
NC2 = NB * NT // 4
E0, H0 = 32, 256
E1, DFF = 8, 2816
BIG = 1.0e30
SM_SHIFT = 16.0
LN_EPS = 1e-5
XNT_B = NB * T


def _alibi_slopes(n_heads):
    start = 2 ** (-(2 ** (-(math.log2(n_heads) - 3))))
    return [start * start**i for i in range(n_heads)]


def _host_prep(inputs):
    f32 = np.float32
    c = np.ascontiguousarray
    W = {}
    W["l0_wqkv"] = c(np.asarray(inputs["L0_attn_W_qkv"], f32))
    W["l0_wf"] = c(np.asarray(inputs["L0_attn_W_f"], f32))
    bf = np.asarray(inputs["L0_attn_b_f"], f32)
    W["l0_bf_row"] = c(np.tile(bf, H)[None, :])
    W["l0_bf_col"] = c(np.concatenate([bf, bf])[:, None])
    W["l0_wout"] = c(np.asarray(inputs["L0_attn_W_out"], f32))
    W["l0_bout_row"] = c(np.asarray(inputs["L0_attn_b_out"], f32)[None, :])
    W["l0_rwT"] = c(np.asarray(inputs["L0_ffn_router_w"], f32).T)
    W["l0_rb_row"] = c(np.asarray(inputs["L0_ffn_router_b"], f32)[None, :])
    W["l0_w1"] = c(np.asarray(inputs["L0_ffn_weights1"], f32))
    W["l0_b1"] = c(np.asarray(inputs["L0_ffn_bias1"], f32)
                   .reshape(E0, 2, 128).transpose(2, 0, 1).reshape(128, 2 * E0))
    W["l0_w2"] = c(np.asarray(inputs["L0_ffn_weights2"], f32))
    W["l0_b2"] = c(np.asarray(inputs["L0_ffn_bias2"], f32))
    for nm, k in (("q", "Wq"), ("k", "Wk"), ("v", "Wv"), ("o", "Wo")):
        W[f"l1_w{nm}T"] = c(np.asarray(inputs[f"L1_attn_{k}"], f32).T)
    W["l1_bqT"] = c(np.asarray(inputs["L1_attn_bq"], f32).reshape(NI, 128).T)
    W["l1_bkT"] = c(np.asarray(inputs["L1_attn_bk"], f32).reshape(NI, 128).T)
    W["l1_bv_row"] = c(np.asarray(inputs["L1_attn_bv"], f32)[None, :])
    W["l1_bo_row"] = c(np.asarray(inputs["L1_attn_bo"], f32)[None, :])
    W["l1_w1"] = c(np.asarray(inputs["L1_ffn_w1"], f32))
    W["l1_w3"] = c(np.asarray(inputs["L1_ffn_w3"], f32))
    W["l1_w2"] = c(np.asarray(inputs["L1_ffn_w2"], f32))

    W["c_ident"] = np.eye(128, dtype=f32)
    slopes = _alibi_slopes(H)
    abias = np.zeros((128, H * 16), np.float64)
    p = np.arange(128)
    for h in range(H):
        for m2 in range(16):
            abias[:, h * 16 + m2] = slopes[h] * (p - m2 * 128 - 64)
    W["c_abias"] = abias.astype(f32)
    abias2 = np.zeros((128, 3 * 32), np.float64)
    for h in range(3):
        for j in range(32):
            abias2[:, h * 32 + j] = slopes[h] * (p - 64 * j - 32)
    W["c_abias2"] = abias2.astype(f32)
    woff = [0, 4, 12, 24]
    abw = np.zeros((128, 11 * 40), np.float64)
    for h in range(5, H):
        for w in range(4):
            for kt in range(4 * w + 4):
                abw[:, (h - 5) * 40 + woff[w] + kt] = slopes[h] * (kt * 128 + p - (w * 512 + 384))
    W["c_abiasw"] = abw.astype(f32)
    pen = np.full((E0, E0, 128), -BIG, f32)
    for e in range(E0):
        pen[e, e, :] = 0.0
    W["c_pen"] = np.ascontiguousarray(pen.reshape(E0, E0 * 128)).astype(BFD)
    kk = np.arange(128)[:, None]
    qr = np.arange(128)[None, :]
    tri = np.where(kk > qr, -BIG, 0.0).astype(f32)
    W["c_maskm"] = tri.astype(BFD)
    mw = np.zeros((128, 896), f32)
    mw[:, 0:384] = -BIG
    mw[:, 384:512] = tri
    W["c_maskw"] = mw.astype(BFD)
    return W


class Ctr:
    __slots__ = ("h", "n", "name")

    def __init__(self, stack, nc, name):
        self.h = stack.enter_context(nc.semaphore(name))
        self.n = 0
        self.name = name


class Prog:
    ENGS = ("tensor", "vector", "scalar", "gpsimd", "sync")

    def __init__(self):
        self.q = {e: [] for e in self.ENGS}
        self.last_wait = {e: {} for e in self.ENGS}

    def op(self, eng, fn, inc=None, k=1, waits=()):
        lw = self.last_wait[eng]
        wl = []
        for ctr, v in waits:
            if v <= 0 or lw.get(ctr.name, -1) >= v:
                continue
            lw[ctr.name] = v
            wl.append((ctr.h, v))
        tgt = None
        if inc is not None:
            inc.n += k
            tgt = (inc.h, k)

        def emit(e, wl=wl, fn=fn, tgt=tgt):
            for h, v in wl:
                e.wait_ge(h, v)
            ins = fn(e)
            if tgt is not None:
                ins.then_inc(tgt[0], tgt[1])
        self.q[eng].append(emit)

    def raw(self, eng, fn):
        self.q[eng].append(fn)



    def realize(self, nc):
        with nc.Block() as blk:
            for eng in self.ENGS:
                ops = self.q[eng]
                if not ops:
                    continue
                def make(ops):
                    def run(e):
                        for f in ops:
                            f(e)
                    return run
                getattr(blk, eng)(make(ops))


def build_program(layers=(0, 1)):
    nc = bass.Bass("TRN2", target_bir_lowering=False)
    P = Prog()
    dram = {}

    def din(name, shape, dt=F32):
        dram[name] = nc.dram_tensor(name, list(shape), dt, kind="ExternalInput").ap()
        return dram[name]

    def dint(name, shape, dt=F32):
        dram[name] = nc.dram_tensor(name, list(shape), dt, kind="Internal").ap()
        return dram[name]

    def dout(name, shape, dt=F32):
        dram[name] = nc.dram_tensor(name, list(shape), dt, kind="ExternalOutput").ap()
        return dram[name]

    if 0 in layers:
        x0 = din("x0", (NB, T, D))
        din("l0_wqkv", (D, 3 * D)); din("l0_wf", (DH, DH))
        din("l0_bf_row", (1, D)); din("l0_bf_col", (128, 1))
        din("l0_wout", (D, D)); din("l0_bout_row", (1, D))
        din("l0_rwT", (D, E0)); din("l0_rb_row", (1, E0))
        din("l0_w1", (E0, D, H0)); din("l0_b1", (128, 2 * E0))
        din("l0_w2", (E0, H0, D)); din("l0_b2", (E0, D))
        din("c_pen", (E0, E0 * 128), BF16)
        hb = dint("hbounce", (NG, 128, 2 * E0, 128), BF16)
    if 1 in layers:
        for nm in "qkvo":
            din(f"l1_w{nm}T", (D, D))
        din("l1_bqT", (128, NI)); din("l1_bkT", (128, NI))
        din("l1_bv_row", (1, D)); din("l1_bo_row", (1, D))
        din("l1_w1", (E1, D, DFF)); din("l1_w3", (E1, D, DFF))
        din("l1_w2", (E1, DFF, D))
        din("c_abias", (128, H * 16)); din("c_abias2", (128, 96))
        din("c_abiasw", (128, 440))
        din("c_maskm", (128, 128), BF16); din("c_maskw", (128, 896), BF16)
    din("c_ident", (128, 128))

    if layers == (0,):
        x_mid = dout("x_mid", (NB, T, D))
    elif 0 in layers:
        x_mid = dint("x_mid", (NB, T, D))
    else:
        x_mid = din("x_mid", (NB, T, D))
    if 1 in layers:
        y_out = dout("y", (NB, T, D))
        x_att = dint("x_att", (NB, T, D))
    xn_hbm = dint("xn_hbm", (NB, T, D), BF16)

    st = ExitStack()
    with st:
        E = st.enter_context
        ct = Ctr(st, nc, "ct")
        cv = Ctr(st, nc, "cv")
        cs = Ctr(st, nc, "cs")
        cg = Ctr(st, nc, "cg")
        ch = Ctr(st, nc, "ch")
        cq = Ctr(st, nc, "cq")
        ALLC = [ct, cv, cs, cg, ch, cq]
        dctrs = {}

        def dctr(name):
            if name not in dctrs:
                dctrs[name] = Ctr(st, nc, "d_" + name)
                ALLC.append(dctrs[name])
            return dctrs[name]

        def dmadone(name):
            c = dctr(name)
            return [(c, c.n)]

        def allw():
            return [(c, c.n) for c in ALLC]

        pool = [E(nc.psum_tensor(f"ps{i}", [128, 1024], F32)) for i in range(4)]

        def slot(i, rows=128, cols=512, r0=0):
            return pool[i // 2][r0:r0 + rows, (i % 2) * 512:(i % 2) * 512 + cols]

        ident_sb = E(nc.sbuf_tensor("ident_sb", [128, 128], F32))
        ones_sb = E(nc.sbuf_tensor("ones_sb", [1, 128], BF16))
        P.op("gpsimd", lambda g: g.dma_start(ident_sb[:, :], dram["c_ident"][:, :]),
             inc=cg, k=16)
        P.op("vector", lambda v: v.memset(ones_sb[:, :], 1.0), inc=cv)
        cst = E(nc.sbuf_tensor("cst", [128, 4], F32))
        P.op("vector", lambda v: v.memset(cst[:, 0:1], LN_EPS))
        P.op("vector", lambda v: v.memset(cst[:, 1:2], -0.5))
        P.op("vector", lambda v: v.memset(cst[:, 2:3], 1e-6))
        P.op("vector", lambda v: v.memset(cst[:, 3:4], -1.0), inc=cv)
        wst = [E(nc.sbuf_tensor(f"wst{i}", [128, 1024], F32)) for i in range(2)]
        wsti = {}

        def emit_wload(pairs, waits=(), dst_free=(), ring=None, rkey="wst", queue="scalar"):
            bufs = wst if ring is None else ring
            nb = len(bufs)
            groups, cur, acc = [], [], 0
            for pr in pairs:
                if acc + pr[2] > 1024:
                    groups.append(cur); cur, acc = [], 0
                cur.append(pr); acc += pr[2]
            if cur:
                groups.append(cur)
            firstall = True
            for grp in groups:
                i = wsti.get(rkey, 0) % nb
                wsti[rkey] = wsti.get(rkey, 0) + 1
                wc = dctr(f"{rkey}{i}")
                off = 0
                first = True
                for dst, srcap, cols in grp:
                    w = list(free.get((rkey, i), []))
                    if firstall and first:
                        w = list(waits) + w
                    P.op(queue, lambda s, i=i, off=off, srcap=srcap, cols=cols: s.dma_start(
                        bufs[i][:, off:off + cols], srcap), inc=wc, k=16,
                        waits=(w if first else ()))
                    first = False
                    off += cols
                ms = [(wc, wc.n)]
                off = 0
                first = True
                for dst, srcap, cols in grp:
                    w = (ms + (list(dst_free) if firstall else [])) if first else ()
                    P.op("vector", lambda v, i=i, off=off, dst=dst, cols=cols: v.tensor_copy(
                        dst, bufs[i][:, off:off + cols]), inc=cv, waits=w)
                    first = False
                    off += cols
                free[(rkey, i)] = [(cv, cv.n)]
                firstall = False
            return [(cv, cv.n)]

        xres = [E(nc.sbuf_tensor(f"xres{i}", [128, D], F32)) for i in range(2)]
        res_sb = [E(nc.sbuf_tensor(f"res{i}", [128, D], F32)) for i in range(2)]
        sln = ExitStack()
        L = sln.enter_context
        xin = [L(nc.sbuf_tensor(f"xin{i}", [128, D], F32)) for i in range(2)]
        xn_sb = [L(nc.sbuf_tensor(f"xnsb{i}", [128, D], BF16)) for i in range(2)]
        bn6 = [L(nc.sbuf_tensor(f"bn6_{i}", [128, 12], F32)) for i in range(2)]
        mvb = [L(nc.sbuf_tensor(f"mv{i}", [128, 2], F32)) for i in range(2)]
        rsd = [L(nc.sbuf_tensor(f"rstd{i}", [128, 2], F32)) for i in range(2)]

        free = {}

        def emit_layernorm(src_dram, xnT):
            base = allw()
            for i in range(2):
                for key in (("xin", i), ("bn", i), ("mv", i), ("rstd", i), ("xn", i)):
                    free.pop(key, None)
            for g in range(NG):
                b, t = divmod(g, NT)
                i = g % 2
                xi, xo, b6, m2, rs = xin[i], xn_sb[i], bn6[i], mvb[i], rsd[i]
                P.op("sync", lambda s, xi=xi, b=b, t=t: s.dma_start(
                    xi[:, :], src_dram[b, t * 128:(t + 1) * 128, :]),
                    inc=dctr(f"xin{i}"), k=16, waits=free.get(("xin", i), base))
                ld = dmadone(f"xin{i}")
                P.op("vector", lambda v, b6=b6, xi=xi: v.bn_stats(b6[:, 0:6], xi[:, 0:512]),
                     waits=ld + free.get(("bn", i), base))
                P.op("vector", lambda v, b6=b6, xi=xi: v.bn_stats(b6[:, 6:12], xi[:, 512:1024]),
                     inc=cv)
                P.op("vector", lambda v, b6=b6, m2=m2: v.bn_aggr(m2[:, :], b6[:, :]),
                     inc=cv, waits=[(cv, cv.n)] + free.get(("mv", i), base))
                free[("bn", i)] = [(cv, cv.n)]
                P.op("scalar", lambda s, m2=m2, rs=rs: s.activation(
                    rs[:, 0:1], m2[:, 1:2], AF.Ln, bias=cst[:, 0:1]), inc=cs,
                    waits=[(cv, cv.n)] + free.get(("rstd", i), base))
                P.op("scalar", lambda s, rs=rs: s.activation(
                    rs[:, 1:2], rs[:, 0:1], AF.Exp, scale=cst[:, 1:2]), inc=cs,
                    waits=[(cs, cs.n)])
                P.op("vector", lambda v, xo=xo, xi=xi, m2=m2, rs=rs: v.tensor_scalar(
                    xo[:, :], xi[:, :], m2[:, 0:1], rs[:, 1:2], ALU.subtract, ALU.mult),
                    inc=cv, waits=[(cs, cs.n)] + free.get(("xn", i), base))
                free[("xin", i)] = [(cv, cv.n)]
                free[("mv", i)] = [(cv, cv.n), (cs, cs.n)]
                free[("rstd", i)] = [(cv, cv.n), (cs, cs.n)]
                P.op("sync", lambda s, xo=xo, b=b, t=t: s.dma_start(
                    xn_hbm[b, t * 128:(t + 1) * 128, :], xo[:, :]),
                    inc=dctr(f"xnst{i}"), k=16, waits=[(cv, cv.n)])
                free[("xn", i)] = dmadone(f"xnst{i}")
            stored = dmadone("xnst0") + dmadone("xnst1") + base
            for it in range(NI):
                for b in range(NB):
                    off = it * XNT_B + b * T
                    P.op("sync", lambda s, it=it, b=b, off=off: s.dma_start_transpose(
                        xnT[:, off:off + T], xn_hbm[b, :, it * 128:(it + 1) * 128]),
                        inc=dctr("tp"), k=16, waits=stored)
            return dmadone("tp")

        def xnt_col(it, g):
            return it * XNT_B + g * 128

        if 0 in layers:
            sx = ExitStack()
            xnT = sx.enter_context(nc.sbuf_tensor("xnT", [128, NI * XNT_B], BF16, side="right"))
            sa0 = ExitStack()
            A = sa0.enter_context
            wf_sb = A(nc.sbuf_tensor("wf_sb", [128, DH], BF16, side="right"))
            bfrow_sb = A(nc.sbuf_tensor("bfrow_sb", [1, D], BF16, side="right"))
            bfcol_sb = A(nc.sbuf_tensor("bfcol_sb", [128, 1], F32, side="right"))
            boutrow_sb = A(nc.sbuf_tensor("boutrow_sb", [1, D], BF16, side="right"))
            kt_sb = A(nc.sbuf_tensor("kt_sb", [128, 8 * 512], BF16, side="right"))
            fk_sb = [A(nc.sbuf_tensor(f"fk{i}", [128, D], BF16, side="right")) for i in range(2)]
            vext = [A(nc.sbuf_tensor(f"vex0_{i}", [128, H * 66], BF16, side="right")) for i in range(2)]
            kv_sb = [A(nc.sbuf_tensor(f"kv{i}", [128, 8 * 65], BF16, side="right")) for i in range(2)]
            kvacc = A(nc.sbuf_tensor("kvacc", [128, 2 * 260], F32, side="right"))
            qt_sb = [A(nc.sbuf_tensor(f"qt{i}", [128, 512], BF16, side="right")) for i in range(2)]
            fq_sb = [A(nc.sbuf_tensor(f"fq{i}", [128, 512], BF16, side="right")) for i in range(2)]
            oe_sb = [A(nc.sbuf_tensor(f"oe{i}", [128, 512], BF16, side="right")) for i in range(2)]
            tln = [A(nc.sbuf_tensor(f"tln{i}", [1, 512], F32, side="right")) for i in range(2)]
            rrow = [A(nc.sbuf_tensor(f"rrow{i}", [1, 512], BF16, side="right")) for i in range(2)]
            apT = A(nc.sbuf_tensor("apT", [128, NI * T], BF16, side="right"))

            P.op("gpsimd", lambda g: g.dma_start(wf_sb[0:64, :], dram["l0_wf"][:, :]),
                 inc=cg, k=16)
            P.op("gpsimd", lambda g: g.dma_start(wf_sb[64:128, :], dram["l0_wf"][:, :]),
                 inc=cg, k=16)
            P.op("gpsimd", lambda g: g.dma_start(bfrow_sb[:, :], dram["l0_bf_row"][:, :]),
                 inc=cg, k=16)
            P.op("gpsimd", lambda g: g.dma_start(bfcol_sb[:, :], dram["l0_bf_col"][:, :]),
                 inc=cg, k=16)
            P.op("gpsimd", lambda g: g.dma_start(boutrow_sb[:, :], dram["l0_bout_row"][:, :]),
                 inc=cg, k=16)
            for i in range(2):
                P.op("vector", lambda v, i=i: v.memset(
                    vext[i][:, :].rearrange("p (h c) -> p h c", c=66)[:, :, 64:66], 1.0),
                    inc=cv)

            sa1 = ExitStack()
            wk_sb = sa1.enter_context(nc.sbuf_tensor("wk_sb", [128, NI * D], BF16, side="right"))
            wv_sb = sa1.enter_context(nc.sbuf_tensor("wv_sb", [128, NI * D], BF16, side="right"))
            for it in range(NI):
                wload = emit_wload([
                    (wk_sb[:, it * D:(it + 1) * D],
                     dram["l0_wqkv"][it * 128:(it + 1) * 128, D:2 * D], D),
                    (wv_sb[:, it * D:(it + 1) * D],
                     dram["l0_wqkv"][it * 128:(it + 1) * 128, 2 * D:3 * D], D)])

            wload = wload + [(cg, cg.n)]
            xnT_ready = emit_layernorm(x0, xnT)

            for b in range(NB):
                P.op("vector", lambda v: v.memset(kvacc[:, :], 0.0), inc=cv,
                     waits=free.get("kvacc", []))
                for c in range(4):
                    for kt in range(8):
                        ps = slot(kt % 2)
                        pw = free.get(("ps01", kt % 2), [])
                        for it in range(NI):
                            P.op("tensor", lambda t, ps=ps, it=it, kt=kt, c=c, b=b: t.matmul(
                                ps, wk_sb[:, it * D + kt * 128: it * D + (kt + 1) * 128],
                                xnT[:, xnt_col(it, b * NT) + c * 512: xnt_col(it, b * NT) + (c + 1) * 512],
                                start=(it == 0), stop=(it == NI - 1)),
                                inc=(ct if it == NI - 1 else None),
                                waits=(xnT_ready + wload + pw) if it == 0 else ())
                        P.op("scalar", lambda s, ps=ps, kt=kt: s.copy(
                            kt_sb[:, kt * 512:(kt + 1) * 512], ps), inc=cs,
                            waits=[(ct, ct.n)] + free.get(("ktsb", kt), []))
                        free[("ps01", kt % 2)] = [(cs, cs.n)]
                    ktev = [(cs, cs.n)]
                    for ti in range(4):
                        g = b * NT + c * 4 + ti
                        i = ti % 2
                        fkps0 = slot(2, cols=512)
                        fkps1 = slot(3, cols=512)
                        pw = free.get("fkps", [])
                        for h in range(H):
                            ps = fkps0 if h < 8 else fkps1
                            co = (h % 8) * 64
                            hb0 = (h % 2) * 64
                            P.op("tensor", lambda t, ps=ps, co=co, h=h, ti=ti, hb0=hb0: t.matmul(
                                ps[:, co:co + 64],
                                kt_sb[hb0:hb0 + 64,
                                      (h // 2) * 512 + ti * 128:(h // 2) * 512 + (ti + 1) * 128],
                                wf_sb[hb0:hb0 + 64, :], start=True, stop=False),
                                waits=(ktev + pw + wload) if h == 0 else ())
                            P.op("tensor", lambda t, ps=ps, co=co, h=h: t.matmul(
                                ps[:, co:co + 64], ones_sb[:, :],
                                bfrow_sb[:, h * 64:(h + 1) * 64], start=False, stop=True),
                                inc=(ct if h == H - 1 else None))
                        P.op("scalar", lambda s, i=i: s.activation(
                            fk_sb[i][:, :], pool[1][:, :], AF.Relu), inc=cs,
                            waits=[(ct, ct.n)] + free.get(("fksb", i), []))
                        free["fkps"] = [(cs, cs.n)]
                        fkdone = [(cs, cs.n)]
                        vps = pool[2]
                        pw = free.get("vps", [])
                        for half in range(2):
                            for it in range(NI):
                                P.op("tensor", lambda t, vps=vps, half=half, it=it, g=g: t.matmul(
                                    vps[:, half * 512:(half + 1) * 512],
                                    xnT[:, xnt_col(it, g): xnt_col(it, g) + 128],
                                    wv_sb[:, it * D + half * 512: it * D + (half + 1) * 512],
                                    start=(it == 0), stop=(it == NI - 1)),
                                    inc=(ct if it == NI - 1 and half == 1 else None),
                                    waits=pw if (half == 0 and it == 0) else ())
                        P.op("vector", lambda v, i=i: v.tensor_copy(
                            vext[i][:, 0:528].rearrange("p (h c) -> p h c", c=66)[:, :, 0:64],
                            pool[2][:, 0:512].rearrange("p (h c) -> p h c", c=64)),
                            waits=[(ct, ct.n)] + free.get(("vext", i), []))
                        P.op("vector", lambda v, i=i: v.tensor_copy(
                            vext[i][:, 528:1056].rearrange("p (h c) -> p h c", c=66)[:, :, 0:64],
                            pool[2][:, 512:1024].rearrange("p (h c) -> p h c", c=64)),
                            inc=cv)
                        free["vps"] = [(cv, cv.n)]
                        vdone = [(cv, cv.n)]
                        for h in range(H):
                            kvp = pool[3][(h % 2) * 64:(h % 2) * 64 + 64,
                                          (h // 2 % 4) * 65 + (h // 8) * 512:
                                          (h // 2 % 4) * 65 + (h // 8) * 512 + 65]
                            P.op("tensor", lambda t, kvp=kvp, h=h, i=i: t.matmul(
                                kvp, fk_sb[i][:, h * 64:(h + 1) * 64],
                                vext[i][:, h * 66:h * 66 + 65],
                                start=True, stop=True),
                                inc=(ct if h == H - 1 else None),
                                waits=(fkdone + vdone + free.get("kvps", [])) if h == 0 else ())
                        free[("fksb", i)] = [(ct, ct.n)]
                        free[("vext", i)] = [(ct, ct.n)]
                        P.op("vector", lambda v: v.tensor_tensor(
                            kvacc[:, 0:260], kvacc[:, 0:260], pool[3][:, 0:260], ALU.add),
                            waits=[(ct, ct.n), (cv, cv.n)])
                        P.op("vector", lambda v: v.tensor_tensor(
                            kvacc[:, 260:520], kvacc[:, 260:520], pool[3][:, 512:772], ALU.add),
                            inc=cv)
                        free["kvps"] = [(cv, cv.n)]
                    for kt in range(8):
                        free[("ktsb", kt)] = [(ct, ct.n)]
                P.op("vector", lambda v, b=b: v.tensor_copy(
                    kv_sb[b][:, :], kvacc[:, :]),
                    inc=cv, waits=[(cv, cv.n)] + free.get(("kvsb", b), []))
                free["kvacc"] = [(cv, cv.n)]
            kvdone = [(cv, cv.n)]

            sa1.close()
            sa2 = ExitStack()
            wq_sb = sa2.enter_context(nc.sbuf_tensor("wq_sb", [128, NI * D], BF16, side="right"))
            wout_sb = sa2.enter_context(nc.sbuf_tensor("wout_sb", [128, NI * D], BF16, side="right"))
            pa_done = [(ct, ct.n)]
            for it in range(NI):
                wload2 = emit_wload([
                    (wq_sb[:, it * D:(it + 1) * D],
                     dram["l0_wqkv"][it * 128:(it + 1) * 128, 0:D], D),
                    (wout_sb[:, it * D:(it + 1) * D],
                     dram["l0_wout"][it * 128:(it + 1) * 128, :], D)],
                    dst_free=pa_done)

            free[("fqps", 0)] = kvdone
            free[("fqps", 1)] = kvdone
            for b in range(NB):
                for c in range(4):
                    for p in range(8):
                        i = p % 2
                        ps = slot(i)
                        for it in range(NI):
                            P.op("tensor", lambda t, ps=ps, it=it, p=p, c=c, b=b: t.matmul(
                                ps, wq_sb[:, it * D + p * 128: it * D + (p + 1) * 128],
                                xnT[:, xnt_col(it, b * NT) + c * 512: xnt_col(it, b * NT) + (c + 1) * 512],
                                start=(it == 0), stop=(it == NI - 1)),
                                inc=(ct if it == NI - 1 else None),
                                waits=(kvdone + wload2 + free.get(("ps01", i), [])) if it == 0 else ())
                        P.op("scalar", lambda s, ps=ps, i=i: s.copy(qt_sb[i][:, :], ps),
                             inc=cs, waits=[(ct, ct.n)] + free.get(("qtsb", i), []))
                        free[("ps01", i)] = [(cs, cs.n)]
                        qtdone = [(cs, cs.n)]
                        fqp = slot(2 + i)
                        P.op("tensor", lambda t, fqp=fqp, i=i: t.matmul(
                            fqp[0:64, :], wf_sb[0:64, :], qt_sb[i][0:64, :],
                            start=True, stop=True),
                            waits=qtdone + free.get(("fqps", i), []))
                        P.op("tensor", lambda t, fqp=fqp, i=i: t.matmul(
                            fqp[64:128, :], wf_sb[64:128, :], qt_sb[i][64:128, :],
                            start=True, stop=True), inc=ct)
                        free[("qtsb", i)] = [(ct, ct.n)]
                        P.op("scalar", lambda s, fqp=fqp, i=i: s.activation(
                            fq_sb[i][:, :], fqp, AF.Relu, bias=bfcol_sb[:, 0:1]),
                            inc=cs, waits=[(ct, ct.n)] + free.get(("fqsb", i), []))
                        free[("fqps", i)] = [(cs, cs.n)]
                        fqdone = [(cs, cs.n)]
                        oep = slot(4)
                        n0 = slot(5, rows=1)
                        n1 = slot(6, rows=1)
                        pw = free.get("oeps", [])
                        for hh in range(2):
                            h = 2 * p + hh
                            kcol = (h // 2 % 4) * 65 + (h // 8) * 260
                            kbase = (h % 2) * 64
                            P.op("tensor", lambda t, oep=oep, hh=hh, kcol=kcol, kbase=kbase, i=i, b=b: t.matmul(
                                oep[hh * 64:(hh + 1) * 64, :],
                                kv_sb[b][kbase:kbase + 64, kcol:kcol + 64],
                                fq_sb[i][hh * 64:(hh + 1) * 64, :],
                                start=True, stop=True),
                                waits=(fqdone + pw) if hh == 0 else ())
                            nps = n0 if hh == 0 else n1
                            P.op("tensor", lambda t, nps=nps, hh=hh, kcol=kcol, kbase=kbase, i=i, b=b: t.matmul(
                                nps, kv_sb[b][kbase:kbase + 64, kcol + 64:kcol + 65],
                                fq_sb[i][hh * 64:(hh + 1) * 64, :],
                                start=True, stop=True),
                                inc=(ct if hh == 1 else None))
                        oedone = [(ct, ct.n)]
                        free[("fqsb", i)] = oedone
                        rbp = slot(7)
                        for hh in range(2):
                            nps = n0 if hh == 0 else n1
                            P.op("scalar", lambda s, nps=nps, hh=hh: s.activation(
                                tln[hh][:, :], nps, AF.Ln, bias=cst[0:1, 2:3]), inc=cs,
                                waits=oedone + free.get(("tln", hh), []))
                            P.op("scalar", lambda s, hh=hh: s.activation(
                                rrow[hh][:, :], tln[hh][:, :], AF.Exp, scale=cst[0:1, 3:4]),
                                inc=cs, waits=[(cs, cs.n)] + free.get(("rrow", hh), []))
                            free[("tln", hh)] = [(cs, cs.n)]
                        rdone = [(cs, cs.n)]
                        pwrb = free.get("rbps", [])
                        for hh in range(2):
                            P.op("tensor", lambda t, rbp=rbp, hh=hh: t.matmul(
                                rbp[hh * 64:(hh + 1) * 64, :], ones_sb[0:1, 0:64],
                                rrow[hh][:, :], start=True, stop=True),
                                inc=(ct if hh == 1 else None),
                                waits=(rdone + pwrb) if hh == 0 else ())
                        free[("rrow", 0)] = [(ct, ct.n)]
                        free[("rrow", 1)] = [(ct, ct.n)]
                        P.op("scalar", lambda s, oep=oep, i=i: s.copy(oe_sb[i][:, :], oep),
                             inc=cs, waits=[(ct, ct.n)] + free.get(("oesb", i), []))
                        free["oeps"] = [(cs, cs.n)]
                        P.op("vector", lambda v, i=i, p=p, c=c, rbp=rbp: v.tensor_tensor(
                            apT[:, p * T + c * 512:p * T + (c + 1) * 512],
                            oe_sb[i][:, :], rbp[:, :], ALU.mult),
                            inc=cv, waits=[(cs, cs.n)] + free.get(("apT", p, c), []))
                        free[("oesb", i)] = [(cv, cv.n)]
                        free["rbps"] = [(cv, cv.n)]
                    apdone = [(cv, cv.n)]
                    for ti in range(4):
                        i = ti % 2
                        yp = pool[1]
                        P.op("sync", lambda s, i=i, b=b, c=c, ti=ti: s.dma_start(
                            xres[i][:, :], x0[b, (c * 4 + ti) * 128:(c * 4 + ti + 1) * 128, :]),
                            inc=dctr(f"xres{i}"), k=16, waits=free.get(("xres", i), []))
                        xrdy = dmadone(f"xres{i}")
                        pw = free.get("wops", [])
                        for half in range(2):
                            for it in range(NI):
                                P.op("tensor", lambda t, yp=yp, half=half, it=it, c=c, ti=ti: t.matmul(
                                    yp[:, half * 512:(half + 1) * 512],
                                    apT[:, it * T + (c * 4 + ti) * 128: it * T + (c * 4 + ti + 1) * 128],
                                    wout_sb[:, it * D + half * 512: it * D + (half + 1) * 512],
                                    start=(it == 0), stop=False),
                                    waits=(apdone + pw + wload2) if (half == 0 and it == 0) else ())
                            P.op("tensor", lambda t, yp=yp, half=half: t.matmul(
                                yp[:, half * 512:(half + 1) * 512], ones_sb[:, :],
                                boutrow_sb[:, half * 512:(half + 1) * 512],
                                start=False, stop=True),
                                inc=(ct if half == 1 else None))
                        for half in range(2):
                            P.op("vector", lambda v, i=i, half=half, yp=yp: v.tensor_tensor(
                                res_sb[i][:, half * 512:(half + 1) * 512],
                                yp[:, half * 512:(half + 1) * 512],
                                xres[i][:, half * 512:(half + 1) * 512], ALU.add),
                                inc=(cv if half == 1 else None),
                                waits=([(ct, ct.n)] + xrdy + free.get(("ressb", i), [])) if half == 0 else ())
                        free[("xres", i)] = [(cv, cv.n)]
                        free["wops"] = [(cv, cv.n)]
                        P.op("sync", lambda s, i=i, b=b, c=c, ti=ti: s.dma_start(
                            x_mid[b, (c * 4 + ti) * 128:(c * 4 + ti + 1) * 128, :],
                            res_sb[i][:, :]), inc=dctr(f"res{i}"), k=16, waits=[(cv, cv.n)])
                        free[("ressb", i)] = dmadone(f"res{i}")
                    free[("fqps", 0)] = [(cv, cv.n)]
                    free[("fqps", 1)] = [(cv, cv.n)]
                free.update({("apT", p, c): [(ct, ct.n)] for p in range(8) for c in range(4)})
            attn_done = [(ct, ct.n), (cv, cv.n), (cs, cs.n)]
            sa2.close()
            sa0.close()

            mT_sb = L(nc.sbuf_tensor("mT_sb", [E0, NG * 128], BF16))
            sm = ExitStack()
            M = sm.enter_context
            rw_sb = M(nc.sbuf_tensor("rw_sb", [128, NI * E0], BF16, side="right"))
            rb_sb = M(nc.sbuf_tensor("rb_sb", [1, E0], BF16, side="right"))
            pen_sb = M(nc.sbuf_tensor("pen_sb", [E0, E0 * 128], BF16, side="right"))
            b1_sb = M(nc.sbuf_tensor("b1_sb", [128, 2 * E0], F32, side="right"))
            lmax = [M(nc.sbuf_tensor(f"lmax{i}", [128, 1], F32, side="right")) for i in range(2)]
            mf_sb = [M(nc.sbuf_tensor(f"mf{i}", [128, E0], F32, side="right")) for i in range(2)]
            w1_sb = [M(nc.sbuf_tensor(f"w1sb{i}", [128, NI * H0], BF16, side="right")) for i in range(2)]
            hT_sb = [M(nc.sbuf_tensor(f"hT0_{i}", [128, 512], BF16, side="right")) for i in range(4)]
            w1st = [M(nc.sbuf_tensor(f"w1st{i}", [128, 1024], F32, side="right")) for i in range(2)]
            for i in range(2):
                free[("w1st", i)] = attn_done
                free[("w1", i)] = attn_done
            rw_rdy = emit_wload([(rw_sb[:, :].rearrange("p (i e) -> p i e", e=E0),
                                  dram["l0_rwT"].rearrange("(i p) e -> p i e", p=128), NI * E0)],
                                dst_free=attn_done)
            P.op("gpsimd", lambda g: g.dma_start(rb_sb[:, :], dram["l0_rb_row"][:, :]),
                 inc=dctr("rcst"), k=16, waits=attn_done)
            P.op("gpsimd", lambda g: g.dma_start(pen_sb[:, :], dram["c_pen"][:, :]),
                 inc=dctr("rcst"), k=16)
            P.op("gpsimd", lambda g: g.dma_start(b1_sb[:, :], dram["l0_b1"][:, :]),
                 inc=dctr("rcst"), k=16)
            rld = dmadone("rcst") + rw_rdy
            for g in range(NG):
                i = g % 2
                lp = slot(i, cols=E0)
                for it in range(NI):
                    P.op("tensor", lambda t, lp=lp, it=it, g=g: t.matmul(
                        lp, xnT[:, xnt_col(it, g): xnt_col(it, g) + 128],
                        rw_sb[:, it * E0:(it + 1) * E0],
                        start=(it == 0), stop=False),
                        waits=(rld + free.get(("lps", i), [])) if it == 0 else ())
                P.op("tensor", lambda t, lp=lp: t.matmul(
                    lp, ones_sb[:, :], rb_sb[:, :], start=False, stop=True), inc=ct)
                P.op("vector", lambda v, lp=lp, i=i: v.reduce_max(
                    lmax[i][:, :], lp, axis=AX.X), inc=cv,
                    waits=[(ct, ct.n)] + free.get(("lmax", i), []))
                P.op("vector", lambda v, lp=lp, i=i: v.tensor_single_scalar(
                    mf_sb[i][:, :], lp, lmax[i][:, :], ALU.is_equal), inc=cv,
                    waits=[(cv, cv.n)] + free.get(("mf", i), []))
                free[("lps", i)] = [(cv, cv.n)]
                free[("lmax", i)] = [(cv, cv.n)]
                mtp = slot(2, rows=E0, cols=128)
                P.op("tensor", lambda t, mtp=mtp, i=i: t.transpose(
                    mtp, mf_sb[i][:, :], ident_sb[:, :]), inc=ct,
                    waits=[(cv, cv.n)] + free.get("mtp", []))
                free[("mf", i)] = [(ct, ct.n)]
                P.op("vector", lambda v, mtp=mtp, g=g: v.tensor_copy(
                    mT_sb[:, g * 128:(g + 1) * 128], mtp), inc=cv,
                    waits=[(ct, ct.n)])
                free["mtp"] = [(cv, cv.n)]
            mtdone = [(cv, cv.n)]

            def load_w1(e):
                i = e % 2
                return emit_wload(
                    [(w1_sb[i][:, a * 1024:(a + 1) * 1024].rearrange("p (i f) -> p i f", f=H0),
                      dram["l0_w1"][e][a * 512:(a + 1) * 512, :].rearrange("(i p) f -> p i f", p=128), 1024)
                     for a in range(2)],
                    dst_free=free.get(("w1", i), []), ring=w1st, rkey="w1st", queue="sync")

            w1rdys = {0: load_w1(0)}
            for e in range(E0):
                i = e % 2
                if e + 1 < E0:
                    w1rdys[e + 1] = load_w1(e + 1)
                w1rdy = w1rdys[e]
                for fh in range(2):
                    for g2 in range(NC2):
                        j = (fh * NC2 + g2) % 2
                        hp = slot(j)
                        for it in range(NI):
                            P.op("tensor", lambda t, hp=hp, it=it, i=i, fh=fh, g2=g2: t.matmul(
                                hp, w1_sb[i][:, it * H0 + fh * 128: it * H0 + fh * 128 + 128],
                                xnT[:, it * XNT_B + g2 * 512: it * XNT_B + (g2 + 1) * 512],
                                start=(it == 0), stop=False),
                                waits=(w1rdy + mtdone + free.get(("hps", j), [])) if it == 0 else ())
                        P.op("tensor", lambda t, hp=hp, e=e, g2=g2: t.matmul(
                            hp, pen_sb[:, e * 128:(e + 1) * 128],
                            mT_sb[:, g2 * 512:(g2 + 1) * 512],
                            start=False, stop=True), inc=ct)
                        k = (fh * NC2 + g2) % 4
                        P.op("scalar", lambda s, hp=hp, k=k, e=e, fh=fh: s.activation(
                            hT_sb[k][:, :], hp, AF.Relu, bias=b1_sb[:, 2 * e + fh:2 * e + fh + 1]),
                            inc=cs, waits=[(ct, ct.n)] + free.get(("hT", k), []))
                        free[("hps", j)] = [(cs, cs.n)]
                        P.op("scalar", lambda s, k=k, e=e, fh=fh, g2=g2: s.dma_start(
                            hb[g2 * 4:(g2 + 1) * 4, :, 2 * e + fh, :].rearrange("a p n -> p a n"),
                            hT_sb[k][:, :].rearrange("p (ti n) -> p ti n", n=128)),
                            inc=dctr(f"hbw{k}"), k=16, waits=[(cs, cs.n)])
                        free[("hT", k)] = dmadone(f"hbw{k}")
                free[("w1", i)] = [(ct, ct.n)]
            hbdone = (dmadone("hbw0") + dmadone("hbw1") + dmadone("hbw2")
                      + dmadone("hbw3") + [(ct, ct.n)])
            sm.close()
            sx.close()

            se = ExitStack()
            Me = se.enter_context
            w2_sb = Me(nc.sbuf_tensor("w2_sb", [128, 2 * E0 * D], BF16, side="right"))
            b2_sb = Me(nc.sbuf_tensor("b2_sb", [E0, D], BF16, side="right"))
            hall = [Me(nc.sbuf_tensor(f"hall{j}", [128, 2 * E0 * 128], BF16, side="right"))
                    for j in range(2)]
            w2lds = []
            for e in range(E0):
                w2lds.append(emit_wload(
                    [(w2_sb[:, (e * 2 + fh) * D:(e * 2 + fh + 1) * D],
                      dram["l0_w2"][e][fh * 128:(fh + 1) * 128, :], D)
                     for fh in range(2)],
                    waits=(hbdone if e == 0 else ()),
                    dst_free=(hbdone if e == 0 else ())))
            P.op("gpsimd", lambda g: g.dma_start(b2_sb[:, :], dram["l0_b2"][:, :]),
                 inc=dctr("b2"), k=16, waits=hbdone)
            w2rdy = dmadone("b2")
            def load_hall(g):
                j = g % 2
                P.op("sync", lambda s, g=g, j=j: s.dma_start(
                    hall[j][:, :], hb[g].rearrange("p k n -> p (k n)")),
                    inc=dctr(f"hall{j}"), k=16,
                    waits=(hbdone if g < 2 else []) + free.get(("hall", j), []))
                return dmadone(f"hall{j}")

            hrdys = {0: load_hall(0)}
            for g in range(NG):
                i = g % 2
                if g + 1 < NG:
                    hrdys[g + 1] = load_hall(g + 1)
                hrdy = hrdys[g]
                yp = pool[0] if i == 0 else pool[1]
                pw = free.get(("yps", i), [])
                for half in range(2):
                    for k in range(2 * E0):
                        kw = w2lds[k // 2] if (k % 2 == 0 and half == 0) else []
                        P.op("tensor", lambda t, yp=yp, half=half, k=k, i=i: t.matmul(
                            yp[:, half * 512:(half + 1) * 512],
                            hall[i][:, k * 128:(k + 1) * 128],
                            w2_sb[:, k * D + half * 512: k * D + (half + 1) * 512],
                            start=(k == 0), stop=False),
                            waits=(hrdy + w2rdy + pw + kw) if (half == 0 and k == 0) else kw)
                    P.op("tensor", lambda t, yp=yp, half=half, g=g: t.matmul(
                        yp[:, half * 512:(half + 1) * 512],
                        mT_sb[:, g * 128:(g + 1) * 128],
                        b2_sb[:, half * 512:(half + 1) * 512],
                        start=False, stop=True),
                        inc=(ct if half == 1 else None))
                free[("hall", i)] = [(ct, ct.n)]
                mmdone = [(ct, ct.n)]
                b, t0 = divmod(g, NT)
                P.op("sync", lambda s, i=i, b=b, t0=t0: s.dma_start(
                    xres[i][:, :], x_mid[b, t0 * 128:(t0 + 1) * 128, :]),
                    inc=dctr(f"xres{i}"), k=16, waits=free.get(("xres", i), []))
                xrdy2 = dmadone(f"xres{i}")
                for half in range(2):
                    P.op("vector", lambda v, i=i, yp=yp, half=half: v.scalar_tensor_tensor(
                        res_sb[i][:, half * 512:(half + 1) * 512],
                        yp[:, half * 512:(half + 1) * 512], 0.0,
                        xres[i][:, half * 512:(half + 1) * 512],
                        ALU.max, ALU.add),
                        inc=(cv if half == 1 else None),
                        waits=(mmdone + xrdy2 + free.get(("ressb", i), [])) if half == 0 else ())
                free[("yps", i)] = [(cv, cv.n)]
                free[("xres", i)] = [(cv, cv.n)]
                P.op("sync", lambda s, i=i, b=b, t0=t0: s.dma_start(
                    x_mid[b, t0 * 128:(t0 + 1) * 128, :], res_sb[i][:, :]),
                    inc=dctr(f"res{i}"), k=16, waits=[(cv, cv.n)])
                free[("ressb", i)] = dmadone(f"res{i}")
                if 1 in layers:
                    b6, m2, rs, xo = bn6[i], mvb[i], rsd[i], xn_sb[i]
                    P.op("vector", lambda v, b6=b6, i=i: v.bn_stats(
                        b6[:, 0:6], res_sb[i][:, 0:512]),
                        waits=[(cv, cv.n)] + free.get(("bn", i), []))
                    P.op("vector", lambda v, b6=b6, i=i: v.bn_stats(
                        b6[:, 6:12], res_sb[i][:, 512:1024]), inc=cv)
                    P.op("vector", lambda v, b6=b6, m2=m2: v.bn_aggr(
                        m2[:, :], b6[:, :]), inc=cv,
                        waits=[(cv, cv.n)] + free.get(("mv", i), []))
                    free[("bn", i)] = [(cv, cv.n)]
                    P.op("scalar", lambda s, m2=m2, rs=rs: s.activation(
                        rs[:, 0:1], m2[:, 1:2], AF.Ln, bias=cst[:, 0:1]), inc=cs,
                        waits=[(cv, cv.n)] + free.get(("rstd", i), []))
                    P.op("scalar", lambda s, rs=rs: s.activation(
                        rs[:, 1:2], rs[:, 0:1], AF.Exp, scale=cst[:, 1:2]), inc=cs,
                        waits=[(cs, cs.n)])
                    P.op("vector", lambda v, xo=xo, i=i, m2=m2, rs=rs: v.tensor_scalar(
                        xo[:, :], res_sb[i][:, :], m2[:, 0:1], rs[:, 1:2],
                        ALU.subtract, ALU.mult),
                        inc=cv, waits=[(cs, cs.n)] + free.get(("xn", i), []))
                    free[("mv", i)] = [(cv, cv.n), (cs, cs.n)]
                    free[("rstd", i)] = [(cv, cv.n), (cs, cs.n)]
                    free[("ressb", i)] = dmadone(f"res{i}") + [(cv, cv.n)]
                    P.op("sync", lambda s, xo=xo, b=b, t0=t0: s.dma_start(
                        xn_hbm[b, t0 * 128:(t0 + 1) * 128, :], xo[:, :]),
                        inc=dctr(f"xnst{i}"), k=16, waits=[(cv, cv.n)])
                    free[("xn", i)] = dmadone(f"xnst{i}")
            se.close()

        if 1 in layers:
            if 0 in layers:
                sx1 = ExitStack()
                xnT1 = sx1.enter_context(
                    nc.sbuf_tensor("xnT1", [128, NI * XNT_B], BF16, side="right"))
                stored1 = [(ct, ct.n)] + dmadone("xnst0") + dmadone("xnst1")
                vb_ready = []
                for tb in range(NB):
                    tpc = dctr(f"tp{tb}f")
                    for it in range(NI):
                        off = it * XNT_B + tb * T
                        P.op("sync", lambda s, it=it, tb=tb, off=off: s.dma_start_transpose(
                            xnT1[:, off:off + T], xn_hbm[tb, :, it * 128:(it + 1) * 128]),
                            inc=tpc, k=16, waits=stored1)
                    vb_ready.append([(tpc, tpc.n)])
                xnT_ready1 = vb_ready[0] + vb_ready[1]
            else:
                sx1 = ExitStack()
                xnT1 = sx1.enter_context(
                    nc.sbuf_tensor("xnT1", [128, NI * XNT_B], BF16, side="right"))
                xnT_ready1 = emit_layernorm(x_mid, xnT1)
                vb_ready = [xnT_ready1, xnT_ready1]
            ln1_done = xnT_ready1 + [(ct, ct.n), (cv, cv.n), (cs, cs.n)]
            sln.close()

            sga = ExitStack()
            G = sga.enter_context
            vext1 = G(nc.sbuf_tensor("vext1", [128, NT * H * 66], BF16, side="right"))
            catT = G(nc.sbuf_tensor("catT", [128, NI * T], BF16, side="right"))
            wbuf = G(nc.sbuf_tensor("wbuf", [128, NI * D], BF16, side="right"))
            wqs = [wbuf[:, i * 1024:(i + 1) * 1024] for i in range(2)]
            wks = [wbuf[:, 2048 + i * 1024:2048 + (i + 1) * 1024] for i in range(2)]
            qtp = [G(nc.sbuf_tensor(f"qtp{i}", [128, T], BF16, side="right")) for i in range(2)]
            ktp = [G(nc.sbuf_tensor(f"ktp{i}", [128, T], BF16, side="right")) for i in range(2)]
            probs = [G(nc.sbuf_tensor(f"probs{i}", [128, 512], BF16, side="right")) for i in range(2)]
            ysb = [G(nc.sbuf_tensor("ysb0", [128, 512], BF16, side="right"))] * 2
            tl1 = [G(nc.sbuf_tensor("tl1_0", [1, 512], F32, side="right"))] * 2
            rr1 = [G(nc.sbuf_tensor("rr1_0", [1, 512], BF16, side="right"))] * 2
            abias_sb = G(nc.sbuf_tensor("abias_sb", [128, H * 16], F32, side="right"))
            abias2_sb = G(nc.sbuf_tensor("abias2_sb", [128, 96], F32, side="right"))
            abiasw_sb = G(nc.sbuf_tensor("abiasw_sb", [128, 440], F32, side="right"))
            maskw_sb = G(nc.sbuf_tensor("maskw_sb", [128, 896], BF16, side="right"))
            maskm_sb = G(nc.sbuf_tensor("maskm_sb", [128, 128], BF16, side="right"))
            bqT_sb = G(nc.sbuf_tensor("bqT_sb", [128, NI], F32, side="right"))
            bkT_sb = G(nc.sbuf_tensor("bkT_sb", [128, NI], F32, side="right"))
            bvrow_sb = G(nc.sbuf_tensor("bvrow_sb", [1, D], BF16, side="right"))
            borow_sb = G(nc.sbuf_tensor("borow_sb", [1, D], BF16, side="right"))
            P.op("gpsimd", lambda g: g.dma_start(abias_sb[:, :], dram["c_abias"][:, :]),
                 inc=dctr("l1cst"), k=16, waits=ln1_done)
            P.op("gpsimd", lambda g: g.dma_start(maskm_sb[:, :], dram["c_maskm"][:, :]),
                 inc=dctr("l1cst"), k=16)
            P.op("gpsimd", lambda g: g.dma_start(abias2_sb[:, :], dram["c_abias2"][:, :]),
                 inc=dctr("l1cst"), k=16)
            P.op("gpsimd", lambda g: g.dma_start(abiasw_sb[:, :], dram["c_abiasw"][:, :]),
                 inc=dctr("l1cst"), k=16)
            P.op("gpsimd", lambda g: g.dma_start(maskw_sb[:, :], dram["c_maskw"][:, :]),
                 inc=dctr("l1cst"), k=16)
            P.op("gpsimd", lambda g: g.dma_start(bqT_sb[:, :], dram["l1_bqT"][:, :]),
                 inc=dctr("l1cst"), k=16)
            P.op("gpsimd", lambda g: g.dma_start(bkT_sb[:, :], dram["l1_bkT"][:, :]),
                 inc=dctr("l1cst"), k=16)
            P.op("gpsimd", lambda g: g.dma_start(bvrow_sb[:, :], dram["l1_bv_row"][:, :]),
                 inc=dctr("l1cst"), k=16)
            P.op("gpsimd", lambda g: g.dma_start(borow_sb[:, :], dram["l1_bo_row"][:, :]),
                 inc=dctr("l1cst"), k=16)
            c1ld = dmadone("l1cst")
            P.op("vector", lambda v: v.memset(
                vext1[:, :].rearrange("p (t h c) -> p t h c", h=H, c=66)[:, :, :, 64:66], 1.0),
                inc=cv, waits=ln1_done)
            WOFF = [0, 4, 12, 24]

            for b in range(NB):
                for it in range(NI):
                    wvrdy = emit_wload(
                        [(wbuf[:, it * D:(it + 1) * D],
                          dram["l1_wvT"][it * 128:(it + 1) * 128, :], D)],
                        dst_free=(free.get("wbuf", c1ld) if it == 0 else ()))
                for ti in range(NT):
                    g0 = b * NT + ti
                    vps = pool[2]
                    pw = free.get("vps1", [])
                    for half in range(2):
                        for it in range(NI):
                            P.op("tensor", lambda t, half=half, it=it, g0=g0: t.matmul(
                                vps[:, half * 512:(half + 1) * 512],
                                xnT1[:, xnt_col(it, g0): xnt_col(it, g0) + 128],
                                wbuf[:, it * D + half * 512: it * D + (half + 1) * 512],
                                start=(it == 0), stop=False),
                                waits=(wvrdy + pw + vb_ready[b]) if (half == 0 and it == 0) else ())
                        P.op("tensor", lambda t, half=half: t.matmul(
                            vps[:, half * 512:(half + 1) * 512], ones_sb[:, :],
                            bvrow_sb[:, half * 512:(half + 1) * 512],
                            start=False, stop=True),
                            inc=(ct if half == 1 else None))
                    P.op("vector", lambda v, ti=ti: v.tensor_copy(
                        vext1[:, ti * 1056:ti * 1056 + 528].rearrange("p (h c) -> p h c", c=66)[:, :, 0:64],
                        pool[2][:, 0:512].rearrange("p (h c) -> p h c", c=64)),
                        waits=[(ct, ct.n)] + free.get("vext1", []))
                    P.op("vector", lambda v, ti=ti: v.tensor_copy(
                        vext1[:, ti * 1056 + 528:ti * 1056 + 1056].rearrange("p (h c) -> p h c", c=66)[:, :, 0:64],
                        pool[2][:, 512:1024].rearrange("p (h c) -> p h c", c=64)),
                        inc=cv)
                    free["vps1"] = [(cv, cv.n)]
                vdone1 = [(cv, cv.n)]

                for p in range(8):
                    i = p % 2
                    wqw = free.get(("wqs", i), []) + ([(ct, ct.n)] if p < 2 else [])
                    wrdy = emit_wload(
                        [(wqs[i].rearrange("p (i o) -> p i o", o=128),
                          dram["l1_wqT"][:, p * 128:(p + 1) * 128].rearrange("(i p) o -> p i o", p=128), 1024),
                         (wks[i].rearrange("p (i o) -> p i o", o=128),
                          dram["l1_wkT"][:, p * 128:(p + 1) * 128].rearrange("(i p) o -> p i o", p=128), 1024)],
                        dst_free=wqw)
                    for c in range(4):
                        ps = slot(c % 2)
                        for it in range(NI):
                            P.op("tensor", lambda t, ps=ps, it=it, i=i, c=c, b=b: t.matmul(
                                ps, wqs[i].rearrange("p (i o) -> p i o", o=128)[:, it, :],
                                xnT1[:, it * XNT_B + b * T + c * 512: it * XNT_B + b * T + (c + 1) * 512],
                                start=(it == 0), stop=(it == NI - 1)),
                                inc=(ct if it == NI - 1 else None),
                                waits=(wrdy + vdone1 + free.get(("ps01a", c % 2), [])) if it == 0 else ())
                        P.op("vector", lambda v, ps=ps, i=i, c=c, p=p: v.tensor_scalar(
                            qtp[i][:, c * 512:(c + 1) * 512], ps,
                            bqT_sb[:, p:p + 1], 0.125, ALU.add, ALU.mult),
                            inc=cv, waits=[(ct, ct.n)] + free.get(("qtp", i), []))
                        free[("ps01a", c % 2)] = [(cv, cv.n)]
                    for c in range(4):
                        ps = slot(c % 2)
                        for it in range(NI):
                            P.op("tensor", lambda t, ps=ps, it=it, i=i, c=c, b=b: t.matmul(
                                ps, wks[i].rearrange("p (i o) -> p i o", o=128)[:, it, :],
                                xnT1[:, it * XNT_B + b * T + c * 512: it * XNT_B + b * T + (c + 1) * 512],
                                start=(it == 0), stop=(it == NI - 1)),
                                inc=(ct if it == NI - 1 else None),
                                waits=free.get(("ps01a", c % 2), []) if it == 0 else ())
                        P.op("scalar", lambda s, ps=ps, i=i, c=c, p=p: s.add(
                            ktp[i][:, c * 512:(c + 1) * 512], ps,
                            bkT_sb[:, p:p + 1]),
                            inc=cs, waits=[(ct, ct.n)] + free.get(("ktp", i), []))
                        free[("ps01a", c % 2)] = [(cs, cs.n)]
                    prdone = [(cv, cv.n), (cs, cs.n)]
                    free[("wqs", i)] = [(ct, ct.n)]
                    for hh in range(2):
                        h = 2 * p + hh
                        hb0 = hh * 64
                        pend_div_s = []
                        pend_div_t = []
                        divstate = {}

                        def emit_div_s(w, hh=hh):
                            ysl = pool[2] if w % 2 == 0 else pool[0]
                            nsrc = (ysl[64:65, 0:512] if hh == 0
                                    else ysl[0:1, 512:1024])
                            P.op("scalar", lambda s, nsrc=nsrc: s.activation(
                                tl1[0][:, :], nsrc, AF.Ln), inc=cs,
                                waits=free.get("ydone", []) + free.get("tl1f", []))
                            P.op("scalar", lambda s: s.activation(
                                rr1[0][:, :], tl1[0][:, :], AF.Exp, scale=cst[0:1, 3:4]),
                                inc=cs, waits=[(cs, cs.n)] + free.get("rr1f", []))
                            free["tl1f"] = [(cs, cs.n)]
                            divstate[w] = [(cs, cs.n)]

                        def emit_div(w, hh=hh, hb0=hb0, p=p, i=i):
                            ysl = pool[2] if w % 2 == 0 else pool[0]
                            rdone1 = divstate.pop(w)
                            rbp = pool[3][hb0:hb0 + 64, 0:512]
                            P.op("tensor", lambda t, rbp=rbp: t.matmul(
                                rbp, ones_sb[0:1, 0:64], rr1[0][:, :],
                                start=True, stop=True), inc=ct,
                                waits=rdone1 + free.get("rbps1", []))
                            free["rr1f"] = [(ct, ct.n)]
                            ysrc = ysl[hb0:hb0 + 64, 0:512]
                            P.op("vector", lambda v, ysrc=ysrc, hb0=hb0: v.tensor_copy(
                                ysb[0][hb0:hb0 + 64, :], ysrc), inc=cv,
                                waits=[(ct, ct.n)] + free.get("ysbf", []))
                            free[("yps", w % 2)] = [(cv, cv.n)]
                            P.op("vector", lambda v, hb0=hb0, p=p, w=w, rbp=rbp: v.tensor_tensor(
                                catT[hb0:hb0 + 64, p * T + w * 512:p * T + (w + 1) * 512],
                                ysb[0][hb0:hb0 + 64, :], rbp, ALU.mult),
                                inc=cv, waits=[(cv, cv.n)])
                            free["ysbf"] = [(cv, cv.n)]
                            free["rbps1"] = [(cv, cv.n)]

                        for w in range(4):
                            nkt = 4 * w + 4
                            ysl = pool[2] if w % 2 == 0 else pool[0]
                            tierC = h >= 5
                            pdone = {}

                            def emit_se(kt, w=w, hb0=hb0, h=h, i=i, tierC=tierC, pdone=pdone):
                                sc = slot(2 + kt % 2)
                                P.op("tensor", lambda t, sc=sc, kt=kt, w=w, hb0=hb0, i=i: t.matmul(
                                    sc, ktp[i][hb0:hb0 + 64, kt * 128:(kt + 1) * 128],
                                    qtp[i][hb0:hb0 + 64, w * 512:(w + 1) * 512],
                                    start=True, stop=True),
                                    inc=ct,
                                    waits=prdone + free.get(("scps", kt % 2), []))
                                scdep = [(ct, ct.n)]
                                pfree = free.get(("probs", kt % 2), [])
                                pb = probs[kt % 2]
                                cvd = False
                                if tierC:
                                    if kt >= 4 * w:
                                        j = kt - 4 * w
                                        P.op("vector", lambda v, sc=sc, j=j: v.tensor_tensor(
                                            sc, sc, maskw_sb[:, (3 - j) * 128:(3 - j) * 128 + 512],
                                            ALU.add), inc=cv, waits=scdep)
                                        scdep = [(cv, cv.n)]
                                        cvd = True
                                    col = (h - 5) * 40 + WOFF[w] + kt
                                    P.op("scalar", lambda s, sc=sc, pb=pb, col=col: s.activation(
                                        pb[:, :], sc, AF.Exp,
                                        bias=abiasw_sb[:, col:col + 1]),
                                        inc=cs, waits=scdep + pfree)
                                else:
                                    steep = h < 3
                                    for qt in range(4):
                                        qtg = 4 * w + qt
                                        if kt > qtg:
                                            P.op("vector", lambda v, pb=pb, qt=qt: v.memset(
                                                pb[:, qt * 128:(qt + 1) * 128], 0.0),
                                                inc=cv, waits=scdep + pfree)
                                            cvd = True
                                            continue
                                        if kt == qtg:
                                            P.op("vector", lambda v, sc=sc, qt=qt: v.tensor_tensor(
                                                sc[:, qt * 128:(qt + 1) * 128],
                                                sc[:, qt * 128:(qt + 1) * 128],
                                                maskm_sb[:, :], ALU.add),
                                                inc=cv, waits=scdep)
                                            scdep2 = [(cv, cv.n)]
                                            cvd = True
                                        else:
                                            scdep2 = scdep
                                        if steep:
                                            for q6 in range(2):
                                                jj = (8 * w + 2 * qt + q6) - 2 * kt
                                                c0 = qt * 128 + q6 * 64
                                                if jj < 0:
                                                    P.op("vector", lambda v, pb=pb, c0=c0: v.memset(
                                                        pb[:, c0:c0 + 64], 0.0),
                                                        inc=cv, waits=scdep2 + pfree)
                                                    cvd = True
                                                    continue
                                                col = h * 32 + jj
                                                P.op("scalar", lambda s, sc=sc, pb=pb, c0=c0, col=col: s.activation(
                                                    pb[:, c0:c0 + 64],
                                                    sc[:, c0:c0 + 64], AF.Exp,
                                                    bias=abias2_sb[:, col:col + 1]),
                                                    inc=cs, waits=scdep2 + pfree)
                                        else:
                                            col = h * 16 + (qtg - kt)
                                            P.op("scalar", lambda s, sc=sc, pb=pb, qt=qt, col=col: s.activation(
                                                pb[:, qt * 128:(qt + 1) * 128],
                                                sc[:, qt * 128:(qt + 1) * 128], AF.Exp,
                                                bias=abias_sb[:, col:col + 1]),
                                                inc=cs, waits=scdep2 + pfree)
                                free[("scps", kt % 2)] = [(cs, cs.n)]
                                pdone[kt] = [(cs, cs.n)] + ([(cv, cv.n)] if cvd else [])

                            emit_se(0)
                            for kt in range(nkt):
                                if kt + 1 < nkt:
                                    emit_se(kt + 1)
                                elif pend_div_t:
                                    pend_div_t.pop(0)()
                                if kt == 0 and pend_div_s:
                                    pend_div_s.pop(0)()
                                w0 = (pdone[kt] + prdone + free.get(("yps", w % 2), [])
                                      ) if kt == 0 else pdone[kt]
                                if hh == 0:
                                    P.op("tensor", lambda t, kt=kt, h=h, w=w, nkt=nkt, ysl=ysl: t.matmul(
                                        ysl[0:65, 0:512],
                                        vext1[:, kt * 1056 + h * 66: kt * 1056 + h * 66 + 65],
                                        probs[kt % 2][:, :],
                                        start=(kt == 0), stop=(kt == nkt - 1)),
                                        inc=ct, waits=w0)
                                else:
                                    P.op("tensor", lambda t, kt=kt, h=h, w=w, nkt=nkt, ysl=ysl: t.matmul(
                                        ysl[64:128, 0:512],
                                        vext1[:, kt * 1056 + h * 66: kt * 1056 + h * 66 + 64],
                                        probs[kt % 2][:, :],
                                        start=(kt == 0), stop=(kt == nkt - 1)),
                                        waits=w0)
                                    P.op("tensor", lambda t, kt=kt, h=h, w=w, nkt=nkt, ysl=ysl: t.matmul(
                                        ysl[0:1, 512:1024],
                                        vext1[:, kt * 1056 + h * 66 + 64: kt * 1056 + h * 66 + 65],
                                        probs[kt % 2][:, :],
                                        start=(kt == 0), stop=(kt == nkt - 1)),
                                        inc=ct)
                                free[("probs", kt % 2)] = [(ct, ct.n)]
                            free["ydone"] = [(ct, ct.n)]
                            pend_div_s.append(lambda w=w: emit_div_s(w))
                            pend_div_t.append(lambda w=w: emit_div(w))
                        while pend_div_s:
                            pend_div_s.pop(0)()
                        while pend_div_t:
                            pend_div_t.pop(0)()
                    free[("ps01a", 0)] = [(cv, cv.n), (cs, cs.n)]
                    free[("ps01a", 1)] = [(cv, cv.n), (cs, cs.n)]
                    free[("qtp", i)] = [(ct, ct.n)]
                    free[("ktp", i)] = [(ct, ct.n)]
                catdone = [(cv, cv.n)]
                free["vext1"] = [(ct, ct.n)]

                hd_done = [(ct, ct.n)]
                for it in range(NI):
                    wordy = emit_wload(
                        [(wbuf[:, it * D:(it + 1) * D],
                          dram["l1_woT"][it * 128:(it + 1) * 128, :], D)],
                        dst_free=(hd_done if it == 0 else ()))
                for ti in range(NT):
                    i = ti % 2
                    yp = pool[1]
                    P.op("sync", lambda s, i=i, b=b, ti=ti: s.dma_start(
                        xres[i][:, :], x_mid[b, ti * 128:(ti + 1) * 128, :]),
                        inc=dctr(f"xres{i}"), k=16, waits=free.get(("xres", i), []))
                    xrdy = dmadone(f"xres{i}")
                    pw = free.get("wops1", [])
                    for half in range(2):
                        for it in range(NI):
                            P.op("tensor", lambda t, yp=yp, half=half, it=it, ti=ti: t.matmul(
                                yp[:, half * 512:(half + 1) * 512],
                                catT[:, it * T + ti * 128: it * T + (ti + 1) * 128],
                                wbuf[:, it * D + half * 512: it * D + (half + 1) * 512],
                                start=(it == 0), stop=False),
                                waits=(catdone + pw + wordy) if (half == 0 and it == 0) else ())
                        P.op("tensor", lambda t, yp=yp, half=half: t.matmul(
                            yp[:, half * 512:(half + 1) * 512], ones_sb[:, :],
                            borow_sb[:, half * 512:(half + 1) * 512],
                            start=False, stop=True),
                            inc=(ct if half == 1 else None))
                    for half in range(2):
                        P.op("vector", lambda v, i=i, half=half, yp=yp: v.tensor_tensor(
                            res_sb[i][:, half * 512:(half + 1) * 512],
                            yp[:, half * 512:(half + 1) * 512],
                            xres[i][:, half * 512:(half + 1) * 512], ALU.add),
                            inc=(cv if half == 1 else None),
                            waits=([(ct, ct.n)] + xrdy + free.get(("ressb", i), [])) if half == 0 else ())
                    free[("xres", i)] = [(cv, cv.n)]
                    free["wops1"] = [(cv, cv.n)]
                    P.op("sync", lambda s, i=i, b=b, ti=ti: s.dma_start(
                        x_att[b, ti * 128:(ti + 1) * 128, :], res_sb[i][:, :]),
                        inc=dctr(f"res{i}"), k=16, waits=[(cv, cv.n)])
                    free[("ressb", i)] = dmadone(f"res{i}")
                free["wbuf"] = [(ct, ct.n)]
            attn1_done = [(ct, ct.n), (cv, cv.n), (cs, cs.n)]
            sga.close()

            sf = ExitStack()
            FT = sf.enter_context
            NFT = DFF // 128
            hT1 = FT(nc.sbuf_tensor("hT1", [128, NFT * 512], BF16, side="right"))
            w1c = [FT(nc.sbuf_tensor(f"w1c{i}", [128, NI * 128], BF16, side="right")) for i in range(4)]
            w3c = [FT(nc.sbuf_tensor(f"w3c{i}", [128, NI * 128], BF16, side="right")) for i in range(4)]
            w2c = [FT(nc.sbuf_tensor(f"w2c{i}", [128, D], BF16, side="right")) for i in range(4)]
            gsb = [FT(nc.sbuf_tensor(f"gsb{i}", [128, 512], BF16, side="right")) for i in range(4)]
            wstf = [FT(nc.sbuf_tensor(f"wstf{i}", [128, 1024], F32, side="right")) for i in range(4)]
            wsts = [FT(nc.sbuf_tensor(f"wsts{i}", [128, 1024], F32, side="right")) for i in range(4)]
            for i in range(4):
                free[("wstf", i)] = attn1_done
                free[("wsts", i)] = attn1_done
            xnv = xnT1[:, :].rearrange("p (i b t) -> p i b t", b=NB, t=T)
            def load_ft(e, ft):
                i = (e * NFT + ft) % 4
                df = (attn1_done if (e == 0 and ft < 4) else [])
                r1 = emit_wload(
                    [(w1c[i][:, :].rearrange("p (i f) -> p i f", f=128),
                      dram["l1_w1"][e][:, ft * 128:(ft + 1) * 128].rearrange("(i p) f -> p i f", p=128), 1024)],
                    dst_free=df + free.get(("w1c", i), []),
                    ring=wstf, rkey="wstf", queue="sync")
                r3 = emit_wload(
                    [(w3c[i][:, :].rearrange("p (i f) -> p i f", f=128),
                      dram["l1_w3"][e][:, ft * 128:(ft + 1) * 128].rearrange("(i p) f -> p i f", p=128), 1024)],
                    dst_free=df + free.get(("w3c", i), []),
                    ring=wsts, rkey="wsts", queue="scalar")
                return (r1, r3)

            wrdys = {}
            for e in range(E1):
                if e == 0:
                    wrdys[0] = load_ft(0, 0)
                    wrdys[1] = load_ft(0, 1)
                for ft in range(NFT):
                    i = (e * NFT + ft) % 4
                    if ft + 2 < NFT:
                        wrdys[ft + 2] = load_ft(e, ft + 2)
                    elif e + 1 < E1:
                        wrdys[ft + 2 - NFT] = load_ft(e + 1, ft + 2 - NFT)
                    wrdy1, wrdy3 = wrdys[ft]
                    par = ft % 2
                    gp = slot(0 + 2 * par)
                    up = slot(1 + 2 * par)
                    for it in range(NI):
                        P.op("tensor", lambda t, gp=gp, it=it, i=i, e=e: t.matmul(
                            gp, w1c[i][:, it * 128:(it + 1) * 128],
                            xnv[:, it, :, e * 256:(e + 1) * 256],
                            start=(it == 0), stop=(it == NI - 1)),
                            inc=(ct if it == NI - 1 else None),
                            waits=(wrdy1 + free.get(("gup", par), [])) if it == 0 else ())
                    for it in range(NI):
                        P.op("tensor", lambda t, up=up, it=it, i=i, e=e: t.matmul(
                            up, w3c[i][:, it * 128:(it + 1) * 128],
                            xnv[:, it, :, e * 256:(e + 1) * 256],
                            start=(it == 0), stop=(it == NI - 1)),
                            inc=(ct if it == NI - 1 else None),
                            waits=wrdy3 if it == 0 else ())
                    free[("w1c", i)] = [(ct, ct.n)]
                    free[("w3c", i)] = [(ct, ct.n)]
                    P.op("scalar", lambda s, gp=gp, i=i: s.activation(
                        gsb[i][:, :], gp, AF.Silu), inc=cs,
                        waits=[(ct, ct.n)] + free.get(("gsb", i), []))
                    P.op("vector", lambda v, up=up, i=i, ft=ft: v.tensor_tensor(
                        hT1[:, ft * 512:(ft + 1) * 512], gsb[i][:, :], up, ALU.mult),
                        inc=cv, waits=[(cs, cs.n)] + free.get(("hT1", ft), []))
                    free[("gsb", i)] = [(cv, cv.n)]
                    free[("gup", par)] = [(cv, cv.n)]
                hdone1 = [(cv, cv.n)]
                ypools = [pool[0], pool[1], pool[2], pool[3]]
                pwy = free.get("ypsf", [])
                w2rdys = {}

                def load_w2ft(ft2):
                    ring, rk, qu = ((wstf, "wstf", "sync") if ft2 % 2 == 0
                                    else (wsts, "wsts", "scalar"))
                    return emit_wload(
                        [(w2c[ft2 % 4][:, :],
                          dram["l1_w2"][e][ft2 * 128:(ft2 + 1) * 128, :], D)],
                        dst_free=free.get(("w2c", ft2 % 4), []),
                        ring=ring, rkey=rk, queue=qu)

                w2rdys[0] = load_w2ft(0)
                w2rdys[1] = load_w2ft(1)
                for ft in range(NFT):
                    i = ft % 4
                    if ft + 2 < NFT:
                        w2rdys[ft + 2] = load_w2ft(ft + 2)
                    w2rdy1 = w2rdys[ft]
                    for sl in range(4):
                        for half in range(2):
                            P.op("tensor", lambda t, sl=sl, half=half, ft=ft, i=i, e=e: t.matmul(
                                ypools[sl][:, half * 512:(half + 1) * 512],
                                hT1[:, ft * 512 + sl * 128: ft * 512 + (sl + 1) * 128],
                                w2c[i][:, half * 512:(half + 1) * 512],
                                start=(ft == 0), stop=(ft == NFT - 1)),
                                inc=(ct if (sl == 3 and half == 1) else None),
                                waits=(w2rdy1 + ((hdone1 + pwy) if ft == 0 else [])) if (sl == 0 and half == 0) else ())
                    free[("w2c", i)] = [(ct, ct.n)]
                free.update({("hT1", ft): [(ct, ct.n)] for ft in range(NFT)})
                ydone1 = [(ct, ct.n)]
                for sl in range(4):
                    i = sl % 2
                    b, st2 = sl // 2, sl % 2
                    ti = e * 2 + st2
                    P.op("sync", lambda s, i=i, b=b, ti=ti: s.dma_start(
                        xres[i][:, :], x_att[b, ti * 128:(ti + 1) * 128, :]),
                        inc=dctr(f"xres{i}"), k=16, waits=free.get(("xres", i), []))
                    xrdy3 = dmadone(f"xres{i}")
                    for half in range(2):
                        P.op("vector", lambda v, i=i, half=half, sl=sl: v.tensor_tensor(
                            res_sb[i][:, half * 512:(half + 1) * 512],
                            ypools[sl][:, half * 512:(half + 1) * 512],
                            xres[i][:, half * 512:(half + 1) * 512], ALU.add),
                            inc=(cv if half == 1 else None),
                            waits=(ydone1 + xrdy3 + free.get(("ressb", i), [])) if half == 0 else ())
                    free[("xres", i)] = [(cv, cv.n)]
                    P.op("sync", lambda s, i=i, b=b, ti=ti: s.dma_start(
                        y_out[b, ti * 128:(ti + 1) * 128, :], res_sb[i][:, :]),
                        inc=dctr(f"res{i}"), k=16, waits=[(cv, cv.n)])
                    free[("ressb", i)] = dmadone(f"res{i}")
                free["ypsf"] = [(cv, cv.n)]
                free[("gup", 0)] = [(cv, cv.n)]
                free[("gup", 1)] = [(cv, cv.n)]
            sf.close()
            sx1.close()

        if 1 not in layers:
            sln.close()
        finals = [(c.h, c.n) for c in ALLC]
        P.raw("sync", lambda e, fl=finals: [e.wait_ge(h, n) for h, n in fl if n > 0])
        P.realize(nc)
    return nc


def _emit_layer1_body(*a, **k):
    raise NotImplementedError("layer 1 not yet implemented")


FUSED = True


def kernel(**inputs):
    W = _host_prep(inputs)
    x = np.ascontiguousarray(np.asarray(inputs["L0_attn_x"], np.float32))
    shards = [x[2 * c:2 * c + 2] for c in range(NCORES)]

    l0_keys = ["l0_wqkv", "l0_wf", "l0_bf_row", "l0_bf_col", "l0_wout",
               "l0_bout_row", "l0_rwT", "l0_rb_row", "l0_w1", "l0_b1",
               "l0_w2", "l0_b2", "c_pen"]
    l1_keys = ["l1_wqT", "l1_wkT", "l1_wvT", "l1_woT", "l1_bqT", "l1_bkT",
               "l1_bv_row", "l1_bo_row", "l1_w1", "l1_w3", "l1_w2",
               "c_abias", "c_abias2", "c_abiasw", "c_maskm", "c_maskw"]

    if FUSED:
        nc = build_program((0, 1))
        maps = [dict({k: W[k] for k in l0_keys + l1_keys},
                     c_ident=W["c_ident"], x0=shards[c]) for c in range(NCORES)]
        res = run_bass_kernel_spmd(nc, maps, core_ids=list(range(NCORES)))
        return np.concatenate([np.asarray(r["y"]).reshape(NB, T, D) for r in res.results], axis=0)
    nc0 = build_program((0,))
    maps = [dict({k: W[k] for k in l0_keys}, c_ident=W["c_ident"], x0=shards[c])
            for c in range(NCORES)]
    res0 = run_bass_kernel_spmd(nc0, maps, core_ids=list(range(NCORES)))
    xmid = [np.ascontiguousarray(np.asarray(r["x_mid"]).reshape(NB, T, D)) for r in res0.results]
    nc1 = build_program((1,))
    maps = [dict({k: W[k] for k in l1_keys}, c_ident=W["c_ident"], x_mid=xmid[c])
            for c in range(NCORES)]
    res1 = run_bass_kernel_spmd(nc1, maps, core_ids=list(range(NCORES)))
    return np.concatenate([np.asarray(r["y"]).reshape(NB, T, D) for r in res1.results], axis=0)


# version 12
# speedup vs baseline: 1.0207x; 1.0207x over previous
import math
from contextlib import ExitStack

import numpy as np
import ml_dtypes

import concourse.bass as bass
import concourse.mybir as mybir
from concourse.bass_utils import run_bass_kernel_spmd

F32 = mybir.dt.float32
BF16 = mybir.dt.bfloat16
AF = mybir.ActivationFunctionType
ALU = mybir.AluOpType
AX = mybir.AxisListType
BFD = ml_dtypes.bfloat16

NCORES = 8
NB = 2
T = 2048
D = 1024
H = 16
DH = 64
NT = T // 128
NI = D // 128
NG = NB * NT
NC2 = NB * NT // 4
E0, H0 = 32, 256
E1, DFF = 8, 2816
BIG = 1.0e30
SM_SHIFT = 16.0
LN_EPS = 1e-5
XNT_B = NB * T


def _alibi_slopes(n_heads):
    start = 2 ** (-(2 ** (-(math.log2(n_heads) - 3))))
    return [start * start**i for i in range(n_heads)]


def _host_prep(inputs):
    f32 = np.float32
    c = np.ascontiguousarray
    W = {}
    W["l0_wqkv"] = c(np.asarray(inputs["L0_attn_W_qkv"], f32))
    W["l0_wf"] = c(np.asarray(inputs["L0_attn_W_f"], f32))
    bf = np.asarray(inputs["L0_attn_b_f"], f32)
    W["l0_bf_row"] = c(np.tile(bf, H)[None, :])
    W["l0_bf_col"] = c(np.concatenate([bf, bf])[:, None])
    W["l0_wout"] = c(np.asarray(inputs["L0_attn_W_out"], f32))
    W["l0_bout_row"] = c(np.asarray(inputs["L0_attn_b_out"], f32)[None, :])
    W["l0_rwT"] = c(np.asarray(inputs["L0_ffn_router_w"], f32).T)
    W["l0_rb_row"] = c(np.asarray(inputs["L0_ffn_router_b"], f32)[None, :])
    W["l0_w1"] = c(np.asarray(inputs["L0_ffn_weights1"], f32))
    W["l0_b1"] = c(np.asarray(inputs["L0_ffn_bias1"], f32)
                   .reshape(E0, 2, 128).transpose(2, 0, 1).reshape(128, 2 * E0))
    W["l0_w2"] = c(np.asarray(inputs["L0_ffn_weights2"], f32))
    W["l0_b2"] = c(np.asarray(inputs["L0_ffn_bias2"], f32))
    for nm, k in (("q", "Wq"), ("k", "Wk"), ("v", "Wv"), ("o", "Wo")):
        W[f"l1_w{nm}T"] = c(np.asarray(inputs[f"L1_attn_{k}"], f32).T)
    W["l1_bqT"] = c(np.asarray(inputs["L1_attn_bq"], f32).reshape(NI, 128).T)
    W["l1_bkT"] = c(np.asarray(inputs["L1_attn_bk"], f32).reshape(NI, 128).T)
    W["l1_bv_row"] = c(np.asarray(inputs["L1_attn_bv"], f32)[None, :])
    W["l1_bo_row"] = c(np.asarray(inputs["L1_attn_bo"], f32)[None, :])
    W["l1_w1"] = c(np.asarray(inputs["L1_ffn_w1"], f32))
    W["l1_w3"] = c(np.asarray(inputs["L1_ffn_w3"], f32))
    W["l1_w2"] = c(np.asarray(inputs["L1_ffn_w2"], f32))

    W["c_ident"] = np.eye(128, dtype=f32)
    slopes = _alibi_slopes(H)
    abias = np.zeros((128, H * 16), np.float64)
    p = np.arange(128)
    for h in range(H):
        for m2 in range(16):
            abias[:, h * 16 + m2] = slopes[h] * (p - m2 * 128 - 64)
    W["c_abias"] = abias.astype(f32)
    abias2 = np.zeros((128, 3 * 32), np.float64)
    for h in range(3):
        for j in range(32):
            abias2[:, h * 32 + j] = slopes[h] * (p - 64 * j - 32)
    W["c_abias2"] = abias2.astype(f32)
    woff = [0, 4, 12, 24]
    abw = np.zeros((128, 11 * 40), np.float64)
    for h in range(5, H):
        for w in range(4):
            for kt in range(4 * w + 4):
                abw[:, (h - 5) * 40 + woff[w] + kt] = slopes[h] * (kt * 128 + p - (w * 512 + 384))
    W["c_abiasw"] = abw.astype(f32)
    pen = np.full((E0, E0, 128), -BIG, f32)
    for e in range(E0):
        pen[e, e, :] = 0.0
    W["c_pen"] = np.ascontiguousarray(pen.reshape(E0, E0 * 128)).astype(BFD)
    kk = np.arange(128)[:, None]
    qr = np.arange(128)[None, :]
    tri = np.where(kk > qr, -BIG, 0.0).astype(f32)
    W["c_maskm"] = tri.astype(BFD)
    mw = np.zeros((128, 896), f32)
    mw[:, 0:384] = -BIG
    mw[:, 384:512] = tri
    W["c_maskw"] = mw.astype(BFD)
    return W


class Ctr:
    __slots__ = ("h", "n", "name")

    def __init__(self, stack, nc, name):
        self.h = stack.enter_context(nc.semaphore(name))
        self.n = 0
        self.name = name


class Prog:
    ENGS = ("tensor", "vector", "scalar", "gpsimd", "sync")

    def __init__(self):
        self.q = {e: [] for e in self.ENGS}
        self.last_wait = {e: {} for e in self.ENGS}

    def op(self, eng, fn, inc=None, k=1, waits=()):
        lw = self.last_wait[eng]
        wl = []
        for ctr, v in waits:
            if v <= 0 or lw.get(ctr.name, -1) >= v:
                continue
            lw[ctr.name] = v
            wl.append((ctr.h, v))
        tgt = None
        if inc is not None:
            inc.n += k
            tgt = (inc.h, k)

        def emit(e, wl=wl, fn=fn, tgt=tgt):
            for h, v in wl:
                e.wait_ge(h, v)
            ins = fn(e)
            if tgt is not None:
                ins.then_inc(tgt[0], tgt[1])
        self.q[eng].append(emit)

    def raw(self, eng, fn):
        self.q[eng].append(fn)



    def realize(self, nc):
        with nc.Block() as blk:
            for eng in self.ENGS:
                ops = self.q[eng]
                if not ops:
                    continue
                def make(ops):
                    def run(e):
                        for f in ops:
                            f(e)
                    return run
                getattr(blk, eng)(make(ops))


def build_program(layers=(0, 1)):
    nc = bass.Bass("TRN2", target_bir_lowering=False)
    P = Prog()
    dram = {}

    def din(name, shape, dt=F32):
        dram[name] = nc.dram_tensor(name, list(shape), dt, kind="ExternalInput").ap()
        return dram[name]

    def dint(name, shape, dt=F32):
        dram[name] = nc.dram_tensor(name, list(shape), dt, kind="Internal").ap()
        return dram[name]

    def dout(name, shape, dt=F32):
        dram[name] = nc.dram_tensor(name, list(shape), dt, kind="ExternalOutput").ap()
        return dram[name]

    if 0 in layers:
        x0 = din("x0", (NB, T, D))
        din("l0_wqkv", (D, 3 * D)); din("l0_wf", (DH, DH))
        din("l0_bf_row", (1, D)); din("l0_bf_col", (128, 1))
        din("l0_wout", (D, D)); din("l0_bout_row", (1, D))
        din("l0_rwT", (D, E0)); din("l0_rb_row", (1, E0))
        din("l0_w1", (E0, D, H0)); din("l0_b1", (128, 2 * E0))
        din("l0_w2", (E0, H0, D)); din("l0_b2", (E0, D))
        din("c_pen", (E0, E0 * 128), BF16)
        hb = dint("hbounce", (NG, 128, 2 * E0, 128), BF16)
    if 1 in layers:
        for nm in "qkvo":
            din(f"l1_w{nm}T", (D, D))
        din("l1_bqT", (128, NI)); din("l1_bkT", (128, NI))
        din("l1_bv_row", (1, D)); din("l1_bo_row", (1, D))
        din("l1_w1", (E1, D, DFF)); din("l1_w3", (E1, D, DFF))
        din("l1_w2", (E1, DFF, D))
        din("c_abias", (128, H * 16)); din("c_abias2", (128, 96))
        din("c_abiasw", (128, 440))
        din("c_maskm", (128, 128), BF16); din("c_maskw", (128, 896), BF16)
    din("c_ident", (128, 128))

    if layers == (0,):
        x_mid = dout("x_mid", (NB, T, D))
    elif 0 in layers:
        x_mid = dint("x_mid", (NB, T, D))
    else:
        x_mid = din("x_mid", (NB, T, D))
    if 1 in layers:
        y_out = dout("y", (NB, T, D))
        x_att = dint("x_att", (NB, T, D))
    xn_hbm = dint("xn_hbm", (NB, T, D), BF16)

    st = ExitStack()
    with st:
        E = st.enter_context
        ct = Ctr(st, nc, "ct")
        cv = Ctr(st, nc, "cv")
        cs = Ctr(st, nc, "cs")
        cg = Ctr(st, nc, "cg")
        ch = Ctr(st, nc, "ch")
        cq = Ctr(st, nc, "cq")
        ALLC = [ct, cv, cs, cg, ch, cq]
        dctrs = {}

        def dctr(name):
            if name not in dctrs:
                dctrs[name] = Ctr(st, nc, "d_" + name)
                ALLC.append(dctrs[name])
            return dctrs[name]

        def dmadone(name):
            c = dctr(name)
            return [(c, c.n)]

        def allw():
            return [(c, c.n) for c in ALLC]

        pool = [E(nc.psum_tensor(f"ps{i}", [128, 1024], F32)) for i in range(4)]

        def slot(i, rows=128, cols=512, r0=0):
            return pool[i // 2][r0:r0 + rows, (i % 2) * 512:(i % 2) * 512 + cols]

        ident_sb = E(nc.sbuf_tensor("ident_sb", [128, 128], F32))
        ones_sb = E(nc.sbuf_tensor("ones_sb", [1, 128], BF16))
        P.op("gpsimd", lambda g: g.dma_start(ident_sb[:, :], dram["c_ident"][:, :]),
             inc=cg, k=16)
        P.op("vector", lambda v: v.memset(ones_sb[:, :], 1.0), inc=cv)
        cst = E(nc.sbuf_tensor("cst", [128, 4], F32))
        P.op("vector", lambda v: v.memset(cst[:, 0:1], LN_EPS))
        P.op("vector", lambda v: v.memset(cst[:, 1:2], -0.5))
        P.op("vector", lambda v: v.memset(cst[:, 2:3], 1e-6))
        P.op("vector", lambda v: v.memset(cst[:, 3:4], -1.0), inc=cv)
        wst = [E(nc.sbuf_tensor(f"wst{i}", [128, 1024], F32)) for i in range(2)]
        wsti = {}

        def emit_wload(pairs, waits=(), dst_free=(), ring=None, rkey="wst", queue="scalar"):
            bufs = wst if ring is None else ring
            nb = len(bufs)
            groups, cur, acc = [], [], 0
            for pr in pairs:
                if acc + pr[2] > 1024:
                    groups.append(cur); cur, acc = [], 0
                cur.append(pr); acc += pr[2]
            if cur:
                groups.append(cur)
            firstall = True
            for grp in groups:
                i = wsti.get(rkey, 0) % nb
                wsti[rkey] = wsti.get(rkey, 0) + 1
                wc = dctr(f"{rkey}{i}")
                off = 0
                first = True
                for dst, srcap, cols in grp:
                    w = list(free.get((rkey, i), []))
                    if firstall and first:
                        w = list(waits) + w
                    P.op(queue, lambda s, i=i, off=off, srcap=srcap, cols=cols: s.dma_start(
                        bufs[i][:, off:off + cols], srcap), inc=wc, k=16,
                        waits=(w if first else ()))
                    first = False
                    off += cols
                ms = [(wc, wc.n)]
                off = 0
                first = True
                for dst, srcap, cols in grp:
                    w = (ms + (list(dst_free) if firstall else [])) if first else ()
                    P.op("vector", lambda v, i=i, off=off, dst=dst, cols=cols: v.tensor_copy(
                        dst, bufs[i][:, off:off + cols]), inc=cv, waits=w)
                    first = False
                    off += cols
                free[(rkey, i)] = [(cv, cv.n)]
                firstall = False
            return [(cv, cv.n)]

        xres = [E(nc.sbuf_tensor(f"xres{i}", [128, D], F32)) for i in range(2)]
        res_sb = [E(nc.sbuf_tensor(f"res{i}", [128, D], F32)) for i in range(2)]
        sln = ExitStack()
        L = sln.enter_context
        xin = [L(nc.sbuf_tensor(f"xin{i}", [128, D], F32)) for i in range(2)]
        xn_sb = [L(nc.sbuf_tensor(f"xnsb{i}", [128, D], BF16)) for i in range(2)]
        bn6 = [L(nc.sbuf_tensor(f"bn6_{i}", [128, 12], F32)) for i in range(2)]
        mvb = [L(nc.sbuf_tensor(f"mv{i}", [128, 2], F32)) for i in range(2)]
        rsd = [L(nc.sbuf_tensor(f"rstd{i}", [128, 2], F32)) for i in range(2)]

        free = {}

        def emit_layernorm(src_dram, xnT):
            base = allw()
            for i in range(2):
                for key in (("xin", i), ("bn", i), ("mv", i), ("rstd", i), ("xn", i)):
                    free.pop(key, None)
            for g in range(NG):
                b, t = divmod(g, NT)
                i = g % 2
                xi, xo, b6, m2, rs = xin[i], xn_sb[i], bn6[i], mvb[i], rsd[i]
                P.op("sync", lambda s, xi=xi, b=b, t=t: s.dma_start(
                    xi[:, :], src_dram[b, t * 128:(t + 1) * 128, :]),
                    inc=dctr(f"xin{i}"), k=16, waits=free.get(("xin", i), base))
                ld = dmadone(f"xin{i}")
                P.op("vector", lambda v, b6=b6, xi=xi: v.bn_stats(b6[:, 0:6], xi[:, 0:512]),
                     waits=ld + free.get(("bn", i), base))
                P.op("vector", lambda v, b6=b6, xi=xi: v.bn_stats(b6[:, 6:12], xi[:, 512:1024]),
                     inc=cv)
                P.op("vector", lambda v, b6=b6, m2=m2: v.bn_aggr(m2[:, :], b6[:, :]),
                     inc=cv, waits=[(cv, cv.n)] + free.get(("mv", i), base))
                free[("bn", i)] = [(cv, cv.n)]
                P.op("scalar", lambda s, m2=m2, rs=rs: s.activation(
                    rs[:, 0:1], m2[:, 1:2], AF.Ln, bias=cst[:, 0:1]), inc=cs,
                    waits=[(cv, cv.n)] + free.get(("rstd", i), base))
                P.op("scalar", lambda s, rs=rs: s.activation(
                    rs[:, 1:2], rs[:, 0:1], AF.Exp, scale=cst[:, 1:2]), inc=cs,
                    waits=[(cs, cs.n)])
                P.op("vector", lambda v, xo=xo, xi=xi, m2=m2, rs=rs: v.tensor_scalar(
                    xo[:, :], xi[:, :], m2[:, 0:1], rs[:, 1:2], ALU.subtract, ALU.mult),
                    inc=cv, waits=[(cs, cs.n)] + free.get(("xn", i), base))
                free[("xin", i)] = [(cv, cv.n)]
                free[("mv", i)] = [(cv, cv.n), (cs, cs.n)]
                free[("rstd", i)] = [(cv, cv.n), (cs, cs.n)]
                P.op("sync", lambda s, xo=xo, b=b, t=t: s.dma_start(
                    xn_hbm[b, t * 128:(t + 1) * 128, :], xo[:, :]),
                    inc=dctr(f"xnst{i}"), k=16, waits=[(cv, cv.n)])
                free[("xn", i)] = dmadone(f"xnst{i}")
            stored = dmadone("xnst0") + dmadone("xnst1") + base
            for it in range(NI):
                for b in range(NB):
                    off = it * XNT_B + b * T
                    P.op("sync", lambda s, it=it, b=b, off=off: s.dma_start_transpose(
                        xnT[:, off:off + T], xn_hbm[b, :, it * 128:(it + 1) * 128]),
                        inc=dctr("tp"), k=16, waits=stored)
            return dmadone("tp")

        def xnt_col(it, g):
            return it * XNT_B + g * 128

        if 0 in layers:
            sx = ExitStack()
            xnT = sx.enter_context(nc.sbuf_tensor("xnT", [128, NI * XNT_B], BF16, side="right"))
            sa0 = ExitStack()
            A = sa0.enter_context
            wf_sb = A(nc.sbuf_tensor("wf_sb", [128, DH], BF16, side="right"))
            bfrow_sb = A(nc.sbuf_tensor("bfrow_sb", [1, D], BF16, side="right"))
            bfcol_sb = A(nc.sbuf_tensor("bfcol_sb", [128, 1], F32, side="right"))
            boutrow_sb = A(nc.sbuf_tensor("boutrow_sb", [1, D], BF16, side="right"))
            kt_sb = A(nc.sbuf_tensor("kt_sb", [128, 8 * 512], BF16, side="right"))
            fk_sb = [A(nc.sbuf_tensor(f"fk{i}", [128, D], BF16, side="right")) for i in range(2)]
            vext = [A(nc.sbuf_tensor(f"vex0_{i}", [128, H * 66], BF16, side="right")) for i in range(2)]
            kv_sb = [A(nc.sbuf_tensor(f"kv{i}", [128, 8 * 65], BF16, side="right")) for i in range(2)]
            kvacc = A(nc.sbuf_tensor("kvacc", [128, 2 * 260], F32, side="right"))
            qt_sb = [A(nc.sbuf_tensor(f"qt{i}", [128, 512], BF16, side="right")) for i in range(2)]
            fq_sb = [A(nc.sbuf_tensor(f"fq{i}", [128, 512], BF16, side="right")) for i in range(2)]
            oe_sb = [A(nc.sbuf_tensor(f"oe{i}", [128, 512], BF16, side="right")) for i in range(2)]
            tln = [A(nc.sbuf_tensor(f"tln{i}", [1, 512], F32, side="right")) for i in range(2)]
            rrow = [A(nc.sbuf_tensor(f"rrow{i}", [1, 512], BF16, side="right")) for i in range(2)]
            apT = A(nc.sbuf_tensor("apT", [128, NI * T], BF16, side="right"))

            P.op("gpsimd", lambda g: g.dma_start(wf_sb[0:64, :], dram["l0_wf"][:, :]),
                 inc=cg, k=16)
            P.op("gpsimd", lambda g: g.dma_start(wf_sb[64:128, :], dram["l0_wf"][:, :]),
                 inc=cg, k=16)
            P.op("gpsimd", lambda g: g.dma_start(bfrow_sb[:, :], dram["l0_bf_row"][:, :]),
                 inc=cg, k=16)
            P.op("gpsimd", lambda g: g.dma_start(bfcol_sb[:, :], dram["l0_bf_col"][:, :]),
                 inc=cg, k=16)
            P.op("gpsimd", lambda g: g.dma_start(boutrow_sb[:, :], dram["l0_bout_row"][:, :]),
                 inc=cg, k=16)
            for i in range(2):
                P.op("vector", lambda v, i=i: v.memset(
                    vext[i][:, :].rearrange("p (h c) -> p h c", c=66)[:, :, 64:66], 1.0),
                    inc=cv)

            sa1 = ExitStack()
            wk_sb = sa1.enter_context(nc.sbuf_tensor("wk_sb", [128, NI * D], BF16, side="right"))
            wv_sb = sa1.enter_context(nc.sbuf_tensor("wv_sb", [128, NI * D], BF16, side="right"))
            for it in range(NI):
                wload = emit_wload([
                    (wk_sb[:, it * D:(it + 1) * D],
                     dram["l0_wqkv"][it * 128:(it + 1) * 128, D:2 * D], D),
                    (wv_sb[:, it * D:(it + 1) * D],
                     dram["l0_wqkv"][it * 128:(it + 1) * 128, 2 * D:3 * D], D)])

            wload = wload + [(cg, cg.n)]
            xnT_ready = emit_layernorm(x0, xnT)

            for b in range(NB):
                P.op("vector", lambda v: v.memset(kvacc[:, :], 0.0), inc=cv,
                     waits=free.get("kvacc", []))
                for c in range(4):
                    for kt in range(8):
                        ps = slot(kt % 2)
                        pw = free.get(("ps01", kt % 2), [])
                        for it in range(NI):
                            P.op("tensor", lambda t, ps=ps, it=it, kt=kt, c=c, b=b: t.matmul(
                                ps, wk_sb[:, it * D + kt * 128: it * D + (kt + 1) * 128],
                                xnT[:, xnt_col(it, b * NT) + c * 512: xnt_col(it, b * NT) + (c + 1) * 512],
                                start=(it == 0), stop=(it == NI - 1)),
                                inc=(ct if it == NI - 1 else None),
                                waits=(xnT_ready + wload + pw) if it == 0 else ())
                        P.op("scalar", lambda s, ps=ps, kt=kt: s.copy(
                            kt_sb[:, kt * 512:(kt + 1) * 512], ps), inc=cs,
                            waits=[(ct, ct.n)] + free.get(("ktsb", kt), []))
                        free[("ps01", kt % 2)] = [(cs, cs.n)]
                    ktev = [(cs, cs.n)]
                    for ti in range(4):
                        g = b * NT + c * 4 + ti
                        i = ti % 2
                        fkps0 = slot(2, cols=512)
                        fkps1 = slot(3, cols=512)
                        pw = free.get("fkps", [])
                        for h in range(H):
                            ps = fkps0 if h < 8 else fkps1
                            co = (h % 8) * 64
                            hb0 = (h % 2) * 64
                            P.op("tensor", lambda t, ps=ps, co=co, h=h, ti=ti, hb0=hb0: t.matmul(
                                ps[:, co:co + 64],
                                kt_sb[hb0:hb0 + 64,
                                      (h // 2) * 512 + ti * 128:(h // 2) * 512 + (ti + 1) * 128],
                                wf_sb[hb0:hb0 + 64, :], start=True, stop=False),
                                waits=(ktev + pw + wload) if h == 0 else ())
                            P.op("tensor", lambda t, ps=ps, co=co, h=h: t.matmul(
                                ps[:, co:co + 64], ones_sb[:, :],
                                bfrow_sb[:, h * 64:(h + 1) * 64], start=False, stop=True),
                                inc=(ct if h == H - 1 else None))
                        P.op("scalar", lambda s, i=i: s.activation(
                            fk_sb[i][:, :], pool[1][:, :], AF.Relu), inc=cs,
                            waits=[(ct, ct.n)] + free.get(("fksb", i), []))
                        free["fkps"] = [(cs, cs.n)]
                        fkdone = [(cs, cs.n)]
                        vps = pool[2]
                        pw = free.get("vps", [])
                        for half in range(2):
                            for it in range(NI):
                                P.op("tensor", lambda t, vps=vps, half=half, it=it, g=g: t.matmul(
                                    vps[:, half * 512:(half + 1) * 512],
                                    xnT[:, xnt_col(it, g): xnt_col(it, g) + 128],
                                    wv_sb[:, it * D + half * 512: it * D + (half + 1) * 512],
                                    start=(it == 0), stop=(it == NI - 1)),
                                    inc=(ct if it == NI - 1 and half == 1 else None),
                                    waits=pw if (half == 0 and it == 0) else ())
                        P.op("vector", lambda v, i=i: v.tensor_copy(
                            vext[i][:, 0:528].rearrange("p (h c) -> p h c", c=66)[:, :, 0:64],
                            pool[2][:, 0:512].rearrange("p (h c) -> p h c", c=64)),
                            waits=[(ct, ct.n)] + free.get(("vext", i), []))
                        P.op("vector", lambda v, i=i: v.tensor_copy(
                            vext[i][:, 528:1056].rearrange("p (h c) -> p h c", c=66)[:, :, 0:64],
                            pool[2][:, 512:1024].rearrange("p (h c) -> p h c", c=64)),
                            inc=cv)
                        free["vps"] = [(cv, cv.n)]
                        vdone = [(cv, cv.n)]
                        for h in range(H):
                            kvp = pool[3][(h % 2) * 64:(h % 2) * 64 + 64,
                                          (h // 2 % 4) * 65 + (h // 8) * 512:
                                          (h // 2 % 4) * 65 + (h // 8) * 512 + 65]
                            P.op("tensor", lambda t, kvp=kvp, h=h, i=i: t.matmul(
                                kvp, fk_sb[i][:, h * 64:(h + 1) * 64],
                                vext[i][:, h * 66:h * 66 + 65],
                                start=True, stop=True),
                                inc=(ct if h == H - 1 else None),
                                waits=(fkdone + vdone + free.get("kvps", [])) if h == 0 else ())
                        free[("fksb", i)] = [(ct, ct.n)]
                        free[("vext", i)] = [(ct, ct.n)]
                        P.op("vector", lambda v: v.tensor_tensor(
                            kvacc[:, 0:260], kvacc[:, 0:260], pool[3][:, 0:260], ALU.add),
                            waits=[(ct, ct.n), (cv, cv.n)])
                        P.op("vector", lambda v: v.tensor_tensor(
                            kvacc[:, 260:520], kvacc[:, 260:520], pool[3][:, 512:772], ALU.add),
                            inc=cv)
                        free["kvps"] = [(cv, cv.n)]
                    for kt in range(8):
                        free[("ktsb", kt)] = [(ct, ct.n)]
                P.op("vector", lambda v, b=b: v.tensor_copy(
                    kv_sb[b][:, :], kvacc[:, :]),
                    inc=cv, waits=[(cv, cv.n)] + free.get(("kvsb", b), []))
                free["kvacc"] = [(cv, cv.n)]
            kvdone = [(cv, cv.n)]

            sa1.close()
            sa2 = ExitStack()
            wq_sb = sa2.enter_context(nc.sbuf_tensor("wq_sb", [128, NI * D], BF16, side="right"))
            wout_sb = sa2.enter_context(nc.sbuf_tensor("wout_sb", [128, NI * D], BF16, side="right"))
            pa_done = [(ct, ct.n)]
            for it in range(NI):
                wload2 = emit_wload([
                    (wq_sb[:, it * D:(it + 1) * D],
                     dram["l0_wqkv"][it * 128:(it + 1) * 128, 0:D], D),
                    (wout_sb[:, it * D:(it + 1) * D],
                     dram["l0_wout"][it * 128:(it + 1) * 128, :], D)],
                    dst_free=pa_done)

            free[("ps01", 0)] = kvdone
            free[("ps01", 1)] = kvdone
            free["wops"] = kvdone
            pend_op = []
            for b in range(NB):
                for c in range(4):
                    for p in range(8):
                        i = p % 2
                        ps = slot(i)
                        for it in range(NI):
                            P.op("tensor", lambda t, ps=ps, it=it, p=p, c=c, b=b: t.matmul(
                                ps, wq_sb[:, it * D + p * 128: it * D + (p + 1) * 128],
                                xnT[:, xnt_col(it, b * NT) + c * 512: xnt_col(it, b * NT) + (c + 1) * 512],
                                start=(it == 0), stop=(it == NI - 1)),
                                inc=(ct if it == NI - 1 else None),
                                waits=(kvdone + wload2 + free.get(("ps01", i), [])) if it == 0 else ())
                        P.op("scalar", lambda s, ps=ps, i=i: s.copy(qt_sb[i][:, :], ps),
                             inc=cs, waits=[(ct, ct.n)] + free.get(("qtsb", i), []))
                        qtdone = [(cs, cs.n)]
                        fqp = slot(i)
                        P.op("tensor", lambda t, fqp=fqp, i=i: t.matmul(
                            fqp[0:64, :], wf_sb[0:64, :], qt_sb[i][0:64, :],
                            start=True, stop=True),
                            waits=qtdone)
                        P.op("tensor", lambda t, fqp=fqp, i=i: t.matmul(
                            fqp[64:128, :], wf_sb[64:128, :], qt_sb[i][64:128, :],
                            start=True, stop=True), inc=ct)
                        free[("qtsb", i)] = [(ct, ct.n)]
                        P.op("scalar", lambda s, fqp=fqp, i=i: s.activation(
                            fq_sb[i][:, :], fqp, AF.Relu, bias=bfcol_sb[:, 0:1]),
                            inc=cs, waits=[(ct, ct.n)] + free.get(("fqsb", i), []))
                        free[("ps01", i)] = [(cs, cs.n)]
                        fqdone = [(cs, cs.n)]
                        oep = slot(4)
                        n0 = slot(5, rows=1)
                        n1 = slot(6, rows=1)
                        pw = free.get("oeps", [])
                        for hh in range(2):
                            h = 2 * p + hh
                            kcol = (h // 2 % 4) * 65 + (h // 8) * 260
                            kbase = (h % 2) * 64
                            P.op("tensor", lambda t, oep=oep, hh=hh, kcol=kcol, kbase=kbase, i=i, b=b: t.matmul(
                                oep[hh * 64:(hh + 1) * 64, :],
                                kv_sb[b][kbase:kbase + 64, kcol:kcol + 64],
                                fq_sb[i][hh * 64:(hh + 1) * 64, :],
                                start=True, stop=True),
                                waits=(fqdone + pw) if hh == 0 else ())
                            nps = n0 if hh == 0 else n1
                            P.op("tensor", lambda t, nps=nps, hh=hh, kcol=kcol, kbase=kbase, i=i, b=b: t.matmul(
                                nps, kv_sb[b][kbase:kbase + 64, kcol + 64:kcol + 65],
                                fq_sb[i][hh * 64:(hh + 1) * 64, :],
                                start=True, stop=True),
                                inc=(ct if hh == 1 else None))
                        oedone = [(ct, ct.n)]
                        free[("fqsb", i)] = oedone
                        rbp = slot(7)
                        for hh in range(2):
                            nps = n0 if hh == 0 else n1
                            P.op("scalar", lambda s, nps=nps, hh=hh: s.activation(
                                tln[hh][:, :], nps, AF.Ln, bias=cst[0:1, 2:3]), inc=cs,
                                waits=oedone + free.get(("tln", hh), []))
                            P.op("scalar", lambda s, hh=hh: s.activation(
                                rrow[hh][:, :], tln[hh][:, :], AF.Exp, scale=cst[0:1, 3:4]),
                                inc=cs, waits=[(cs, cs.n)] + free.get(("rrow", hh), []))
                            free[("tln", hh)] = [(cs, cs.n)]
                        rdone = [(cs, cs.n)]
                        if pend_op:
                            pend_op.pop(0)()
                        pwrb = free.get("rbps", [])
                        for hh in range(2):
                            P.op("tensor", lambda t, rbp=rbp, hh=hh: t.matmul(
                                rbp[hh * 64:(hh + 1) * 64, :], ones_sb[0:1, 0:64],
                                rrow[hh][:, :], start=True, stop=True),
                                inc=(ct if hh == 1 else None),
                                waits=(rdone + pwrb) if hh == 0 else ())
                        free[("rrow", 0)] = [(ct, ct.n)]
                        free[("rrow", 1)] = [(ct, ct.n)]
                        P.op("scalar", lambda s, oep=oep, i=i: s.copy(oe_sb[i][:, :], oep),
                             inc=cs, waits=[(ct, ct.n)] + free.get(("oesb", i), []))
                        free["oeps"] = [(cs, cs.n)]
                        P.op("vector", lambda v, i=i, p=p, c=c, rbp=rbp: v.tensor_tensor(
                            apT[:, p * T + c * 512:p * T + (c + 1) * 512],
                            oe_sb[i][:, :], rbp[:, :], ALU.mult),
                            inc=cv, waits=[(cs, cs.n)] + free.get(("apT", p, c), []))
                        free[("oesb", i)] = [(cv, cv.n)]
                        free["rbps"] = [(cv, cv.n)]
                    apdone = [(cv, cv.n)]

                    def mk_op0(b, c, ti, apdone):
                        def run():
                            i = ti % 2
                            yp = pool[1]
                            P.op("sync", lambda s: s.dma_start(
                                xres[i][:, :], x0[b, (c * 4 + ti) * 128:(c * 4 + ti + 1) * 128, :]),
                                inc=dctr(f"xres{i}"), k=16, waits=free.get(("xres", i), []))
                            pw = free.get("wops", [])
                            for it in range(NI):
                                P.op("tensor", lambda t, it=it: t.matmul(
                                    yp[:, 0:512],
                                    apT[:, it * T + (c * 4 + ti) * 128: it * T + (c * 4 + ti + 1) * 128],
                                    wout_sb[:, it * D: it * D + 512],
                                    start=(it == 0), stop=False),
                                    waits=(apdone + pw + wload2) if it == 0 else ())
                            P.op("tensor", lambda t: t.matmul(
                                yp[:, 0:512], ones_sb[:, :], boutrow_sb[:, 0:512],
                                start=False, stop=True))
                        return run

                    def mk_op1(b, c, ti, apdone):
                        def run():
                            i = ti % 2
                            yp = pool[1]
                            xrdy = dmadone(f"xres{i}")
                            for it in range(NI):
                                P.op("tensor", lambda t, it=it: t.matmul(
                                    yp[:, 512:1024],
                                    apT[:, it * T + (c * 4 + ti) * 128: it * T + (c * 4 + ti + 1) * 128],
                                    wout_sb[:, it * D + 512: it * D + 1024],
                                    start=(it == 0), stop=False))
                            P.op("tensor", lambda t: t.matmul(
                                yp[:, 512:1024], ones_sb[:, :], boutrow_sb[:, 512:1024],
                                start=False, stop=True), inc=ct)
                            if ti == 3:
                                free.update({("apT", pp, c): [(ct, ct.n)] for pp in range(8)})
                            for half in range(2):
                                P.op("vector", lambda v, half=half: v.tensor_tensor(
                                    res_sb[i][:, half * 512:(half + 1) * 512],
                                    yp[:, half * 512:(half + 1) * 512],
                                    xres[i][:, half * 512:(half + 1) * 512], ALU.add),
                                    inc=(cv if half == 1 else None),
                                    waits=([(ct, ct.n)] + xrdy + free.get(("ressb", i), [])) if half == 0 else ())
                            free[("xres", i)] = [(cv, cv.n)]
                            free["wops"] = [(cv, cv.n)]
                            P.op("sync", lambda s: s.dma_start(
                                x_mid[b, (c * 4 + ti) * 128:(c * 4 + ti + 1) * 128, :],
                                res_sb[i][:, :]), inc=dctr(f"res{i}"), k=16, waits=[(cv, cv.n)])
                            free[("ressb", i)] = dmadone(f"res{i}")
                        return run

                    for ti in range(4):
                        pend_op.append(mk_op0(b, c, ti, apdone))
                        pend_op.append(mk_op1(b, c, ti, apdone))
            while pend_op:
                pend_op.pop(0)()
            attn_done = [(ct, ct.n), (cv, cv.n), (cs, cs.n)]
            sa2.close()
            sa0.close()

            mT_sb = L(nc.sbuf_tensor("mT_sb", [E0, NG * 128], BF16))
            sm = ExitStack()
            M = sm.enter_context
            rw_sb = M(nc.sbuf_tensor("rw_sb", [128, NI * E0], BF16, side="right"))
            rb_sb = M(nc.sbuf_tensor("rb_sb", [1, E0], BF16, side="right"))
            pen_sb = M(nc.sbuf_tensor("pen_sb", [E0, E0 * 128], BF16, side="right"))
            b1_sb = M(nc.sbuf_tensor("b1_sb", [128, 2 * E0], F32, side="right"))
            lmax = [M(nc.sbuf_tensor(f"lmax{i}", [128, 1], F32, side="right")) for i in range(2)]
            mf_sb = [M(nc.sbuf_tensor(f"mf{i}", [128, E0], F32, side="right")) for i in range(2)]
            w1_sb = [M(nc.sbuf_tensor(f"w1sb{i}", [128, NI * H0], BF16, side="right")) for i in range(2)]
            hT_sb = [M(nc.sbuf_tensor(f"hT0_{i}", [128, 512], BF16, side="right")) for i in range(4)]
            w1st = [M(nc.sbuf_tensor(f"w1st{i}", [128, 1024], F32, side="right")) for i in range(2)]
            for i in range(2):
                free[("w1st", i)] = attn_done
                free[("w1", i)] = attn_done
            rw_rdy = emit_wload([(rw_sb[:, :].rearrange("p (i e) -> p i e", e=E0),
                                  dram["l0_rwT"].rearrange("(i p) e -> p i e", p=128), NI * E0)],
                                dst_free=attn_done)
            P.op("gpsimd", lambda g: g.dma_start(rb_sb[:, :], dram["l0_rb_row"][:, :]),
                 inc=dctr("rcst"), k=16, waits=attn_done)
            P.op("gpsimd", lambda g: g.dma_start(pen_sb[:, :], dram["c_pen"][:, :]),
                 inc=dctr("rcst"), k=16)
            P.op("gpsimd", lambda g: g.dma_start(b1_sb[:, :], dram["l0_b1"][:, :]),
                 inc=dctr("rcst"), k=16)
            rld = dmadone("rcst") + rw_rdy
            for g in range(NG):
                i = g % 2
                lp = slot(i, cols=E0)
                for it in range(NI):
                    P.op("tensor", lambda t, lp=lp, it=it, g=g: t.matmul(
                        lp, xnT[:, xnt_col(it, g): xnt_col(it, g) + 128],
                        rw_sb[:, it * E0:(it + 1) * E0],
                        start=(it == 0), stop=False),
                        waits=(rld + free.get(("lps", i), [])) if it == 0 else ())
                P.op("tensor", lambda t, lp=lp: t.matmul(
                    lp, ones_sb[:, :], rb_sb[:, :], start=False, stop=True), inc=ct)
                P.op("vector", lambda v, lp=lp, i=i: v.reduce_max(
                    lmax[i][:, :], lp, axis=AX.X), inc=cv,
                    waits=[(ct, ct.n)] + free.get(("lmax", i), []))
                P.op("vector", lambda v, lp=lp, i=i: v.tensor_single_scalar(
                    mf_sb[i][:, :], lp, lmax[i][:, :], ALU.is_equal), inc=cv,
                    waits=[(cv, cv.n)] + free.get(("mf", i), []))
                free[("lps", i)] = [(cv, cv.n)]
                free[("lmax", i)] = [(cv, cv.n)]
                mtp = slot(2, rows=E0, cols=128)
                P.op("tensor", lambda t, mtp=mtp, i=i: t.transpose(
                    mtp, mf_sb[i][:, :], ident_sb[:, :]), inc=ct,
                    waits=[(cv, cv.n)] + free.get("mtp", []))
                free[("mf", i)] = [(ct, ct.n)]
                P.op("vector", lambda v, mtp=mtp, g=g: v.tensor_copy(
                    mT_sb[:, g * 128:(g + 1) * 128], mtp), inc=cv,
                    waits=[(ct, ct.n)])
                free["mtp"] = [(cv, cv.n)]
            mtdone = [(cv, cv.n)]

            def load_w1(e):
                i = e % 2
                return emit_wload(
                    [(w1_sb[i][:, a * 1024:(a + 1) * 1024].rearrange("p (i f) -> p i f", f=H0),
                      dram["l0_w1"][e][a * 512:(a + 1) * 512, :].rearrange("(i p) f -> p i f", p=128), 1024)
                     for a in range(2)],
                    dst_free=free.get(("w1", i), []), ring=w1st, rkey="w1st", queue="sync")

            w1rdys = {0: load_w1(0)}
            for e in range(E0):
                i = e % 2
                if e + 1 < E0:
                    w1rdys[e + 1] = load_w1(e + 1)
                w1rdy = w1rdys[e]
                for fh in range(2):
                    for g2 in range(NC2):
                        j = (fh * NC2 + g2) % 2
                        hp = slot(j)
                        for it in range(NI):
                            P.op("tensor", lambda t, hp=hp, it=it, i=i, fh=fh, g2=g2: t.matmul(
                                hp, w1_sb[i][:, it * H0 + fh * 128: it * H0 + fh * 128 + 128],
                                xnT[:, it * XNT_B + g2 * 512: it * XNT_B + (g2 + 1) * 512],
                                start=(it == 0), stop=False),
                                waits=(w1rdy + mtdone + free.get(("hps", j), [])) if it == 0 else ())
                        P.op("tensor", lambda t, hp=hp, e=e, g2=g2: t.matmul(
                            hp, pen_sb[:, e * 128:(e + 1) * 128],
                            mT_sb[:, g2 * 512:(g2 + 1) * 512],
                            start=False, stop=True), inc=ct)
                        k = (fh * NC2 + g2) % 4
                        P.op("scalar", lambda s, hp=hp, k=k, e=e, fh=fh: s.activation(
                            hT_sb[k][:, :], hp, AF.Relu, bias=b1_sb[:, 2 * e + fh:2 * e + fh + 1]),
                            inc=cs, waits=[(ct, ct.n)] + free.get(("hT", k), []))
                        free[("hps", j)] = [(cs, cs.n)]
                        P.op("scalar", lambda s, k=k, e=e, fh=fh, g2=g2: s.dma_start(
                            hb[g2 * 4:(g2 + 1) * 4, :, 2 * e + fh, :].rearrange("a p n -> p a n"),
                            hT_sb[k][:, :].rearrange("p (ti n) -> p ti n", n=128)),
                            inc=dctr(f"hbw{k}"), k=16, waits=[(cs, cs.n)])
                        free[("hT", k)] = dmadone(f"hbw{k}")
                free[("w1", i)] = [(ct, ct.n)]
            hbdone = (dmadone("hbw0") + dmadone("hbw1") + dmadone("hbw2")
                      + dmadone("hbw3") + [(ct, ct.n)])
            sm.close()
            sx.close()

            se = ExitStack()
            Me = se.enter_context
            w2_sb = Me(nc.sbuf_tensor("w2_sb", [128, 2 * E0 * D], BF16, side="right"))
            b2_sb = Me(nc.sbuf_tensor("b2_sb", [E0, D], BF16, side="right"))
            hall = [Me(nc.sbuf_tensor(f"hall{j}", [128, 2 * E0 * 128], BF16, side="right"))
                    for j in range(2)]
            w2lds = []
            for e in range(E0):
                w2lds.append(emit_wload(
                    [(w2_sb[:, (e * 2 + fh) * D:(e * 2 + fh + 1) * D],
                      dram["l0_w2"][e][fh * 128:(fh + 1) * 128, :], D)
                     for fh in range(2)],
                    waits=(hbdone if e == 0 else ()),
                    dst_free=(hbdone if e == 0 else ())))
            P.op("gpsimd", lambda g: g.dma_start(b2_sb[:, :], dram["l0_b2"][:, :]),
                 inc=dctr("b2"), k=16, waits=hbdone)
            w2rdy = dmadone("b2")
            def load_hall(g):
                j = g % 2
                P.op("sync", lambda s, g=g, j=j: s.dma_start(
                    hall[j][:, :], hb[g].rearrange("p k n -> p (k n)")),
                    inc=dctr(f"hall{j}"), k=16,
                    waits=(hbdone if g < 2 else []) + free.get(("hall", j), []))
                return dmadone(f"hall{j}")

            hrdys = {0: load_hall(0)}
            for g in range(NG):
                i = g % 2
                if g + 1 < NG:
                    hrdys[g + 1] = load_hall(g + 1)
                hrdy = hrdys[g]
                yp = pool[0] if i == 0 else pool[1]
                pw = free.get(("yps", i), [])
                for half in range(2):
                    for k in range(2 * E0):
                        kw = w2lds[k // 2] if (k % 2 == 0 and half == 0) else []
                        P.op("tensor", lambda t, yp=yp, half=half, k=k, i=i: t.matmul(
                            yp[:, half * 512:(half + 1) * 512],
                            hall[i][:, k * 128:(k + 1) * 128],
                            w2_sb[:, k * D + half * 512: k * D + (half + 1) * 512],
                            start=(k == 0), stop=False),
                            waits=(hrdy + w2rdy + pw + kw) if (half == 0 and k == 0) else kw)
                    P.op("tensor", lambda t, yp=yp, half=half, g=g: t.matmul(
                        yp[:, half * 512:(half + 1) * 512],
                        mT_sb[:, g * 128:(g + 1) * 128],
                        b2_sb[:, half * 512:(half + 1) * 512],
                        start=False, stop=True),
                        inc=(ct if half == 1 else None))
                free[("hall", i)] = [(ct, ct.n)]
                mmdone = [(ct, ct.n)]
                b, t0 = divmod(g, NT)
                P.op("sync", lambda s, i=i, b=b, t0=t0: s.dma_start(
                    xres[i][:, :], x_mid[b, t0 * 128:(t0 + 1) * 128, :]),
                    inc=dctr(f"xres{i}"), k=16, waits=free.get(("xres", i), []))
                xrdy2 = dmadone(f"xres{i}")
                for half in range(2):
                    P.op("vector", lambda v, i=i, yp=yp, half=half: v.scalar_tensor_tensor(
                        res_sb[i][:, half * 512:(half + 1) * 512],
                        yp[:, half * 512:(half + 1) * 512], 0.0,
                        xres[i][:, half * 512:(half + 1) * 512],
                        ALU.max, ALU.add),
                        inc=(cv if half == 1 else None),
                        waits=(mmdone + xrdy2 + free.get(("ressb", i), [])) if half == 0 else ())
                free[("yps", i)] = [(cv, cv.n)]
                free[("xres", i)] = [(cv, cv.n)]
                P.op("sync", lambda s, i=i, b=b, t0=t0: s.dma_start(
                    x_mid[b, t0 * 128:(t0 + 1) * 128, :], res_sb[i][:, :]),
                    inc=dctr(f"res{i}"), k=16, waits=[(cv, cv.n)])
                free[("ressb", i)] = dmadone(f"res{i}")
                if 1 in layers:
                    b6, m2, rs, xo = bn6[i], mvb[i], rsd[i], xn_sb[i]
                    P.op("vector", lambda v, b6=b6, i=i: v.bn_stats(
                        b6[:, 0:6], res_sb[i][:, 0:512]),
                        waits=[(cv, cv.n)] + free.get(("bn", i), []))
                    P.op("vector", lambda v, b6=b6, i=i: v.bn_stats(
                        b6[:, 6:12], res_sb[i][:, 512:1024]), inc=cv)
                    P.op("vector", lambda v, b6=b6, m2=m2: v.bn_aggr(
                        m2[:, :], b6[:, :]), inc=cv,
                        waits=[(cv, cv.n)] + free.get(("mv", i), []))
                    free[("bn", i)] = [(cv, cv.n)]
                    P.op("scalar", lambda s, m2=m2, rs=rs: s.activation(
                        rs[:, 0:1], m2[:, 1:2], AF.Ln, bias=cst[:, 0:1]), inc=cs,
                        waits=[(cv, cv.n)] + free.get(("rstd", i), []))
                    P.op("scalar", lambda s, rs=rs: s.activation(
                        rs[:, 1:2], rs[:, 0:1], AF.Exp, scale=cst[:, 1:2]), inc=cs,
                        waits=[(cs, cs.n)])
                    P.op("vector", lambda v, xo=xo, i=i, m2=m2, rs=rs: v.tensor_scalar(
                        xo[:, :], res_sb[i][:, :], m2[:, 0:1], rs[:, 1:2],
                        ALU.subtract, ALU.mult),
                        inc=cv, waits=[(cs, cs.n)] + free.get(("xn", i), []))
                    free[("mv", i)] = [(cv, cv.n), (cs, cs.n)]
                    free[("rstd", i)] = [(cv, cv.n), (cs, cs.n)]
                    free[("ressb", i)] = dmadone(f"res{i}") + [(cv, cv.n)]
                    P.op("sync", lambda s, xo=xo, b=b, t0=t0: s.dma_start(
                        xn_hbm[b, t0 * 128:(t0 + 1) * 128, :], xo[:, :]),
                        inc=dctr(f"xnst{i}"), k=16, waits=[(cv, cv.n)])
                    free[("xn", i)] = dmadone(f"xnst{i}")
            se.close()

        if 1 in layers:
            if 0 in layers:
                sx1 = ExitStack()
                xnT1 = sx1.enter_context(
                    nc.sbuf_tensor("xnT1", [128, NI * XNT_B], BF16, side="right"))
                stored1 = [(ct, ct.n)] + dmadone("xnst0") + dmadone("xnst1")
                vb_ready = []
                for tb in range(NB):
                    tpc = dctr(f"tp{tb}f")
                    for it in range(NI):
                        off = it * XNT_B + tb * T
                        P.op("sync", lambda s, it=it, tb=tb, off=off: s.dma_start_transpose(
                            xnT1[:, off:off + T], xn_hbm[tb, :, it * 128:(it + 1) * 128]),
                            inc=tpc, k=16, waits=stored1)
                    vb_ready.append([(tpc, tpc.n)])
                xnT_ready1 = vb_ready[0] + vb_ready[1]
            else:
                sx1 = ExitStack()
                xnT1 = sx1.enter_context(
                    nc.sbuf_tensor("xnT1", [128, NI * XNT_B], BF16, side="right"))
                xnT_ready1 = emit_layernorm(x_mid, xnT1)
                vb_ready = [xnT_ready1, xnT_ready1]
            ln1_done = xnT_ready1 + [(ct, ct.n), (cv, cv.n), (cs, cs.n)]
            sln.close()

            sga = ExitStack()
            G = sga.enter_context
            vext1 = G(nc.sbuf_tensor("vext1", [128, NT * H * 66], BF16, side="right"))
            catT = G(nc.sbuf_tensor("catT", [128, NI * T], BF16, side="right"))
            wbuf = G(nc.sbuf_tensor("wbuf", [128, NI * D], BF16, side="right"))
            wqs = [wbuf[:, i * 1024:(i + 1) * 1024] for i in range(2)]
            wks = [wbuf[:, 2048 + i * 1024:2048 + (i + 1) * 1024] for i in range(2)]
            qtp = [G(nc.sbuf_tensor(f"qtp{i}", [128, T], BF16, side="right")) for i in range(2)]
            ktp = [G(nc.sbuf_tensor(f"ktp{i}", [128, T], BF16, side="right")) for i in range(2)]
            probs = [G(nc.sbuf_tensor(f"probs{i}", [128, 512], BF16, side="right")) for i in range(2)]
            ysb = [G(nc.sbuf_tensor("ysb0", [128, 512], BF16, side="right"))] * 2
            tl1 = [G(nc.sbuf_tensor("tl1_0", [1, 512], F32, side="right"))] * 2
            rr1 = [G(nc.sbuf_tensor("rr1_0", [1, 512], BF16, side="right"))] * 2
            abias_sb = G(nc.sbuf_tensor("abias_sb", [128, H * 16], F32, side="right"))
            abias2_sb = G(nc.sbuf_tensor("abias2_sb", [128, 96], F32, side="right"))
            abiasw_sb = G(nc.sbuf_tensor("abiasw_sb", [128, 440], F32, side="right"))
            maskw_sb = G(nc.sbuf_tensor("maskw_sb", [128, 896], BF16, side="right"))
            maskm_sb = G(nc.sbuf_tensor("maskm_sb", [128, 128], BF16, side="right"))
            bqT_sb = G(nc.sbuf_tensor("bqT_sb", [128, NI], F32, side="right"))
            bkT_sb = G(nc.sbuf_tensor("bkT_sb", [128, NI], F32, side="right"))
            bvrow_sb = G(nc.sbuf_tensor("bvrow_sb", [1, D], BF16, side="right"))
            borow_sb = G(nc.sbuf_tensor("borow_sb", [1, D], BF16, side="right"))
            P.op("gpsimd", lambda g: g.dma_start(abias_sb[:, :], dram["c_abias"][:, :]),
                 inc=dctr("l1cst"), k=16, waits=ln1_done)
            P.op("gpsimd", lambda g: g.dma_start(maskm_sb[:, :], dram["c_maskm"][:, :]),
                 inc=dctr("l1cst"), k=16)
            P.op("gpsimd", lambda g: g.dma_start(abias2_sb[:, :], dram["c_abias2"][:, :]),
                 inc=dctr("l1cst"), k=16)
            P.op("gpsimd", lambda g: g.dma_start(abiasw_sb[:, :], dram["c_abiasw"][:, :]),
                 inc=dctr("l1cst"), k=16)
            P.op("gpsimd", lambda g: g.dma_start(maskw_sb[:, :], dram["c_maskw"][:, :]),
                 inc=dctr("l1cst"), k=16)
            P.op("gpsimd", lambda g: g.dma_start(bqT_sb[:, :], dram["l1_bqT"][:, :]),
                 inc=dctr("l1cst"), k=16)
            P.op("gpsimd", lambda g: g.dma_start(bkT_sb[:, :], dram["l1_bkT"][:, :]),
                 inc=dctr("l1cst"), k=16)
            P.op("gpsimd", lambda g: g.dma_start(bvrow_sb[:, :], dram["l1_bv_row"][:, :]),
                 inc=dctr("l1cst"), k=16)
            P.op("gpsimd", lambda g: g.dma_start(borow_sb[:, :], dram["l1_bo_row"][:, :]),
                 inc=dctr("l1cst"), k=16)
            c1ld = dmadone("l1cst")
            P.op("vector", lambda v: v.memset(
                vext1[:, :].rearrange("p (t h c) -> p t h c", h=H, c=66)[:, :, :, 64:66], 1.0),
                inc=cv, waits=ln1_done)
            WOFF = [0, 4, 12, 24]

            for b in range(NB):
                for it in range(NI):
                    wvrdy = emit_wload(
                        [(wbuf[:, it * D:(it + 1) * D],
                          dram["l1_wvT"][it * 128:(it + 1) * 128, :], D)],
                        dst_free=(free.get("wbuf", c1ld) if it == 0 else ()))
                for ti in range(NT):
                    g0 = b * NT + ti
                    vps = pool[2]
                    pw = free.get("vps1", [])
                    for half in range(2):
                        for it in range(NI):
                            P.op("tensor", lambda t, half=half, it=it, g0=g0: t.matmul(
                                vps[:, half * 512:(half + 1) * 512],
                                xnT1[:, xnt_col(it, g0): xnt_col(it, g0) + 128],
                                wbuf[:, it * D + half * 512: it * D + (half + 1) * 512],
                                start=(it == 0), stop=False),
                                waits=(wvrdy + pw + vb_ready[b]) if (half == 0 and it == 0) else ())
                        P.op("tensor", lambda t, half=half: t.matmul(
                            vps[:, half * 512:(half + 1) * 512], ones_sb[:, :],
                            bvrow_sb[:, half * 512:(half + 1) * 512],
                            start=False, stop=True),
                            inc=(ct if half == 1 else None))
                    P.op("vector", lambda v, ti=ti: v.tensor_copy(
                        vext1[:, ti * 1056:ti * 1056 + 528].rearrange("p (h c) -> p h c", c=66)[:, :, 0:64],
                        pool[2][:, 0:512].rearrange("p (h c) -> p h c", c=64)),
                        waits=[(ct, ct.n)] + free.get("vext1", []))
                    P.op("vector", lambda v, ti=ti: v.tensor_copy(
                        vext1[:, ti * 1056 + 528:ti * 1056 + 1056].rearrange("p (h c) -> p h c", c=66)[:, :, 0:64],
                        pool[2][:, 512:1024].rearrange("p (h c) -> p h c", c=64)),
                        inc=cv)
                    free["vps1"] = [(cv, cv.n)]
                vdone1 = [(cv, cv.n)]

                for p in range(8):
                    i = p % 2
                    wqw = free.get(("wqs", i), []) + ([(ct, ct.n)] if p < 2 else [])
                    wrdy = emit_wload(
                        [(wqs[i].rearrange("p (i o) -> p i o", o=128),
                          dram["l1_wqT"][:, p * 128:(p + 1) * 128].rearrange("(i p) o -> p i o", p=128), 1024),
                         (wks[i].rearrange("p (i o) -> p i o", o=128),
                          dram["l1_wkT"][:, p * 128:(p + 1) * 128].rearrange("(i p) o -> p i o", p=128), 1024)],
                        dst_free=wqw)
                    for c in range(4):
                        ps = slot(c % 2)
                        for it in range(NI):
                            P.op("tensor", lambda t, ps=ps, it=it, i=i, c=c, b=b: t.matmul(
                                ps, wqs[i].rearrange("p (i o) -> p i o", o=128)[:, it, :],
                                xnT1[:, it * XNT_B + b * T + c * 512: it * XNT_B + b * T + (c + 1) * 512],
                                start=(it == 0), stop=(it == NI - 1)),
                                inc=(ct if it == NI - 1 else None),
                                waits=(wrdy + vdone1 + free.get(("ps01a", c % 2), [])) if it == 0 else ())
                        P.op("vector", lambda v, ps=ps, i=i, c=c, p=p: v.tensor_scalar(
                            qtp[i][:, c * 512:(c + 1) * 512], ps,
                            bqT_sb[:, p:p + 1], 0.125, ALU.add, ALU.mult),
                            inc=cv, waits=[(ct, ct.n)] + free.get(("qtp", i), []))
                        free[("ps01a", c % 2)] = [(cv, cv.n)]
                    for c in range(4):
                        ps = slot(c % 2)
                        for it in range(NI):
                            P.op("tensor", lambda t, ps=ps, it=it, i=i, c=c, b=b: t.matmul(
                                ps, wks[i].rearrange("p (i o) -> p i o", o=128)[:, it, :],
                                xnT1[:, it * XNT_B + b * T + c * 512: it * XNT_B + b * T + (c + 1) * 512],
                                start=(it == 0), stop=(it == NI - 1)),
                                inc=(ct if it == NI - 1 else None),
                                waits=free.get(("ps01a", c % 2), []) if it == 0 else ())
                        P.op("scalar", lambda s, ps=ps, i=i, c=c, p=p: s.add(
                            ktp[i][:, c * 512:(c + 1) * 512], ps,
                            bkT_sb[:, p:p + 1]),
                            inc=cs, waits=[(ct, ct.n)] + free.get(("ktp", i), []))
                        free[("ps01a", c % 2)] = [(cs, cs.n)]
                    prdone = [(cv, cv.n), (cs, cs.n)]
                    free[("wqs", i)] = [(ct, ct.n)]
                    for hh in range(2):
                        h = 2 * p + hh
                        hb0 = hh * 64
                        pend_div_s = []
                        pend_div_t = []
                        divstate = {}

                        def emit_div_s(w, hh=hh):
                            ysl = pool[2] if w % 2 == 0 else pool[0]
                            nsrc = (ysl[64:65, 0:512] if hh == 0
                                    else ysl[0:1, 512:1024])
                            P.op("scalar", lambda s, nsrc=nsrc: s.activation(
                                tl1[0][:, :], nsrc, AF.Ln), inc=cs,
                                waits=free.get("ydone", []) + free.get("tl1f", []))
                            P.op("scalar", lambda s: s.activation(
                                rr1[0][:, :], tl1[0][:, :], AF.Exp, scale=cst[0:1, 3:4]),
                                inc=cs, waits=[(cs, cs.n)] + free.get("rr1f", []))
                            free["tl1f"] = [(cs, cs.n)]
                            divstate[w] = [(cs, cs.n)]

                        def emit_div(w, hh=hh, hb0=hb0, p=p, i=i):
                            ysl = pool[2] if w % 2 == 0 else pool[0]
                            rdone1 = divstate.pop(w)
                            rbp = pool[3][hb0:hb0 + 64, 0:512]
                            P.op("tensor", lambda t, rbp=rbp: t.matmul(
                                rbp, ones_sb[0:1, 0:64], rr1[0][:, :],
                                start=True, stop=True), inc=ct,
                                waits=rdone1 + free.get("rbps1", []))
                            free["rr1f"] = [(ct, ct.n)]
                            ysrc = ysl[hb0:hb0 + 64, 0:512]
                            P.op("vector", lambda v, ysrc=ysrc, hb0=hb0: v.tensor_copy(
                                ysb[0][hb0:hb0 + 64, :], ysrc), inc=cv,
                                waits=[(ct, ct.n)] + free.get("ysbf", []))
                            free[("yps", w % 2)] = [(cv, cv.n)]
                            P.op("vector", lambda v, hb0=hb0, p=p, w=w, rbp=rbp: v.tensor_tensor(
                                catT[hb0:hb0 + 64, p * T + w * 512:p * T + (w + 1) * 512],
                                ysb[0][hb0:hb0 + 64, :], rbp, ALU.mult),
                                inc=cv, waits=[(cv, cv.n)])
                            free["ysbf"] = [(cv, cv.n)]
                            free["rbps1"] = [(cv, cv.n)]

                        for w in range(4):
                            nkt = 4 * w + 4
                            ysl = pool[2] if w % 2 == 0 else pool[0]
                            tierC = h >= 5
                            pdone = {}

                            def emit_se(kt, w=w, hb0=hb0, h=h, i=i, tierC=tierC, pdone=pdone):
                                sc = slot(2 + kt % 2)
                                P.op("tensor", lambda t, sc=sc, kt=kt, w=w, hb0=hb0, i=i: t.matmul(
                                    sc, ktp[i][hb0:hb0 + 64, kt * 128:(kt + 1) * 128],
                                    qtp[i][hb0:hb0 + 64, w * 512:(w + 1) * 512],
                                    start=True, stop=True),
                                    inc=ct,
                                    waits=prdone + free.get(("scps", kt % 2), []))
                                scdep = [(ct, ct.n)]
                                pfree = free.get(("probs", kt % 2), [])
                                pb = probs[kt % 2]
                                cvd = False
                                if tierC:
                                    if kt >= 4 * w:
                                        j = kt - 4 * w
                                        P.op("vector", lambda v, sc=sc, j=j: v.tensor_tensor(
                                            sc, sc, maskw_sb[:, (3 - j) * 128:(3 - j) * 128 + 512],
                                            ALU.add), inc=cv, waits=scdep)
                                        scdep = [(cv, cv.n)]
                                        cvd = True
                                    col = (h - 5) * 40 + WOFF[w] + kt
                                    P.op("scalar", lambda s, sc=sc, pb=pb, col=col: s.activation(
                                        pb[:, :], sc, AF.Exp,
                                        bias=abiasw_sb[:, col:col + 1]),
                                        inc=cs, waits=scdep + pfree)
                                else:
                                    steep = h < 3
                                    for qt in range(4):
                                        qtg = 4 * w + qt
                                        if kt > qtg:
                                            P.op("vector", lambda v, pb=pb, qt=qt: v.memset(
                                                pb[:, qt * 128:(qt + 1) * 128], 0.0),
                                                inc=cv, waits=scdep + pfree)
                                            cvd = True
                                            continue
                                        if kt == qtg:
                                            P.op("vector", lambda v, sc=sc, qt=qt: v.tensor_tensor(
                                                sc[:, qt * 128:(qt + 1) * 128],
                                                sc[:, qt * 128:(qt + 1) * 128],
                                                maskm_sb[:, :], ALU.add),
                                                inc=cv, waits=scdep)
                                            scdep2 = [(cv, cv.n)]
                                            cvd = True
                                        else:
                                            scdep2 = scdep
                                        if steep:
                                            for q6 in range(2):
                                                jj = (8 * w + 2 * qt + q6) - 2 * kt
                                                c0 = qt * 128 + q6 * 64
                                                if jj < 0:
                                                    P.op("vector", lambda v, pb=pb, c0=c0: v.memset(
                                                        pb[:, c0:c0 + 64], 0.0),
                                                        inc=cv, waits=scdep2 + pfree)
                                                    cvd = True
                                                    continue
                                                col = h * 32 + jj
                                                P.op("scalar", lambda s, sc=sc, pb=pb, c0=c0, col=col: s.activation(
                                                    pb[:, c0:c0 + 64],
                                                    sc[:, c0:c0 + 64], AF.Exp,
                                                    bias=abias2_sb[:, col:col + 1]),
                                                    inc=cs, waits=scdep2 + pfree)
                                        else:
                                            col = h * 16 + (qtg - kt)
                                            P.op("scalar", lambda s, sc=sc, pb=pb, qt=qt, col=col: s.activation(
                                                pb[:, qt * 128:(qt + 1) * 128],
                                                sc[:, qt * 128:(qt + 1) * 128], AF.Exp,
                                                bias=abias_sb[:, col:col + 1]),
                                                inc=cs, waits=scdep2 + pfree)
                                free[("scps", kt % 2)] = [(cs, cs.n)]
                                pdone[kt] = [(cs, cs.n)] + ([(cv, cv.n)] if cvd else [])

                            emit_se(0)
                            for kt in range(nkt):
                                if kt + 1 < nkt:
                                    emit_se(kt + 1)
                                elif pend_div_t:
                                    pend_div_t.pop(0)()
                                if kt == 0 and pend_div_s:
                                    pend_div_s.pop(0)()
                                w0 = (pdone[kt] + prdone + free.get(("yps", w % 2), [])
                                      ) if kt == 0 else pdone[kt]
                                if hh == 0:
                                    P.op("tensor", lambda t, kt=kt, h=h, w=w, nkt=nkt, ysl=ysl: t.matmul(
                                        ysl[0:65, 0:512],
                                        vext1[:, kt * 1056 + h * 66: kt * 1056 + h * 66 + 65],
                                        probs[kt % 2][:, :],
                                        start=(kt == 0), stop=(kt == nkt - 1)),
                                        inc=ct, waits=w0)
                                else:
                                    P.op("tensor", lambda t, kt=kt, h=h, w=w, nkt=nkt, ysl=ysl: t.matmul(
                                        ysl[64:128, 0:512],
                                        vext1[:, kt * 1056 + h * 66: kt * 1056 + h * 66 + 64],
                                        probs[kt % 2][:, :],
                                        start=(kt == 0), stop=(kt == nkt - 1)),
                                        waits=w0)
                                    P.op("tensor", lambda t, kt=kt, h=h, w=w, nkt=nkt, ysl=ysl: t.matmul(
                                        ysl[0:1, 512:1024],
                                        vext1[:, kt * 1056 + h * 66 + 64: kt * 1056 + h * 66 + 65],
                                        probs[kt % 2][:, :],
                                        start=(kt == 0), stop=(kt == nkt - 1)),
                                        inc=ct)
                                free[("probs", kt % 2)] = [(ct, ct.n)]
                            free["ydone"] = [(ct, ct.n)]
                            pend_div_s.append(lambda w=w: emit_div_s(w))
                            pend_div_t.append(lambda w=w: emit_div(w))
                        while pend_div_s:
                            pend_div_s.pop(0)()
                        while pend_div_t:
                            pend_div_t.pop(0)()
                    free[("ps01a", 0)] = [(cv, cv.n), (cs, cs.n)]
                    free[("ps01a", 1)] = [(cv, cv.n), (cs, cs.n)]
                    free[("qtp", i)] = [(ct, ct.n)]
                    free[("ktp", i)] = [(ct, ct.n)]
                catdone = [(cv, cv.n)]
                free["vext1"] = [(ct, ct.n)]

                hd_done = [(ct, ct.n)]
                for it in range(NI):
                    wordy = emit_wload(
                        [(wbuf[:, it * D:(it + 1) * D],
                          dram["l1_woT"][it * 128:(it + 1) * 128, :], D)],
                        dst_free=(hd_done if it == 0 else ()))
                for ti in range(NT):
                    i = ti % 2
                    yp = pool[1]
                    P.op("sync", lambda s, i=i, b=b, ti=ti: s.dma_start(
                        xres[i][:, :], x_mid[b, ti * 128:(ti + 1) * 128, :]),
                        inc=dctr(f"xres{i}"), k=16, waits=free.get(("xres", i), []))
                    xrdy = dmadone(f"xres{i}")
                    pw = free.get("wops1", [])
                    for half in range(2):
                        for it in range(NI):
                            P.op("tensor", lambda t, yp=yp, half=half, it=it, ti=ti: t.matmul(
                                yp[:, half * 512:(half + 1) * 512],
                                catT[:, it * T + ti * 128: it * T + (ti + 1) * 128],
                                wbuf[:, it * D + half * 512: it * D + (half + 1) * 512],
                                start=(it == 0), stop=False),
                                waits=(catdone + pw + wordy) if (half == 0 and it == 0) else ())
                        P.op("tensor", lambda t, yp=yp, half=half: t.matmul(
                            yp[:, half * 512:(half + 1) * 512], ones_sb[:, :],
                            borow_sb[:, half * 512:(half + 1) * 512],
                            start=False, stop=True),
                            inc=(ct if half == 1 else None))
                    for half in range(2):
                        P.op("vector", lambda v, i=i, half=half, yp=yp: v.tensor_tensor(
                            res_sb[i][:, half * 512:(half + 1) * 512],
                            yp[:, half * 512:(half + 1) * 512],
                            xres[i][:, half * 512:(half + 1) * 512], ALU.add),
                            inc=(cv if half == 1 else None),
                            waits=([(ct, ct.n)] + xrdy + free.get(("ressb", i), [])) if half == 0 else ())
                    free[("xres", i)] = [(cv, cv.n)]
                    free["wops1"] = [(cv, cv.n)]
                    P.op("sync", lambda s, i=i, b=b, ti=ti: s.dma_start(
                        x_att[b, ti * 128:(ti + 1) * 128, :], res_sb[i][:, :]),
                        inc=dctr(f"res{i}"), k=16, waits=[(cv, cv.n)])
                    free[("ressb", i)] = dmadone(f"res{i}")
                free["wbuf"] = [(ct, ct.n)]
            attn1_done = [(ct, ct.n), (cv, cv.n), (cs, cs.n)]
            sga.close()

            sf = ExitStack()
            FT = sf.enter_context
            NFT = DFF // 128
            hT1 = FT(nc.sbuf_tensor("hT1", [128, NFT * 512], BF16, side="right"))
            w1c = [FT(nc.sbuf_tensor(f"w1c{i}", [128, NI * 128], BF16, side="right")) for i in range(4)]
            w3c = [FT(nc.sbuf_tensor(f"w3c{i}", [128, NI * 128], BF16, side="right")) for i in range(4)]
            w2c = [FT(nc.sbuf_tensor(f"w2c{i}", [128, D], BF16, side="right")) for i in range(4)]
            gsb = [FT(nc.sbuf_tensor(f"gsb{i}", [128, 512], BF16, side="right")) for i in range(4)]
            wstf = [FT(nc.sbuf_tensor(f"wstf{i}", [128, 1024], F32, side="right")) for i in range(4)]
            wsts = [FT(nc.sbuf_tensor(f"wsts{i}", [128, 1024], F32, side="right")) for i in range(4)]
            for i in range(4):
                free[("wstf", i)] = attn1_done
                free[("wsts", i)] = attn1_done
            xnv = xnT1[:, :].rearrange("p (i b t) -> p i b t", b=NB, t=T)
            def load_ft(e, ft):
                i = (e * NFT + ft) % 4
                df = (attn1_done if (e == 0 and ft < 4) else [])
                r1 = emit_wload(
                    [(w1c[i][:, :].rearrange("p (i f) -> p i f", f=128),
                      dram["l1_w1"][e][:, ft * 128:(ft + 1) * 128].rearrange("(i p) f -> p i f", p=128), 1024)],
                    dst_free=df + free.get(("w1c", i), []),
                    ring=wstf, rkey="wstf", queue="sync")
                r3 = emit_wload(
                    [(w3c[i][:, :].rearrange("p (i f) -> p i f", f=128),
                      dram["l1_w3"][e][:, ft * 128:(ft + 1) * 128].rearrange("(i p) f -> p i f", p=128), 1024)],
                    dst_free=df + free.get(("w3c", i), []),
                    ring=wsts, rkey="wsts", queue="scalar")
                return (r1, r3)

            wrdys = {}
            for e in range(E1):
                if e == 0:
                    wrdys[0] = load_ft(0, 0)
                    wrdys[1] = load_ft(0, 1)
                for ft in range(NFT):
                    i = (e * NFT + ft) % 4
                    if ft + 2 < NFT:
                        wrdys[ft + 2] = load_ft(e, ft + 2)
                    elif e + 1 < E1:
                        wrdys[ft + 2 - NFT] = load_ft(e + 1, ft + 2 - NFT)
                    wrdy1, wrdy3 = wrdys[ft]
                    par = ft % 2
                    gp = slot(0 + 2 * par)
                    up = slot(1 + 2 * par)
                    for it in range(NI):
                        P.op("tensor", lambda t, gp=gp, it=it, i=i, e=e: t.matmul(
                            gp, w1c[i][:, it * 128:(it + 1) * 128],
                            xnv[:, it, :, e * 256:(e + 1) * 256],
                            start=(it == 0), stop=(it == NI - 1)),
                            inc=(ct if it == NI - 1 else None),
                            waits=(wrdy1 + free.get(("gup", par), [])) if it == 0 else ())
                    for it in range(NI):
                        P.op("tensor", lambda t, up=up, it=it, i=i, e=e: t.matmul(
                            up, w3c[i][:, it * 128:(it + 1) * 128],
                            xnv[:, it, :, e * 256:(e + 1) * 256],
                            start=(it == 0), stop=(it == NI - 1)),
                            inc=(ct if it == NI - 1 else None),
                            waits=wrdy3 if it == 0 else ())
                    free[("w1c", i)] = [(ct, ct.n)]
                    free[("w3c", i)] = [(ct, ct.n)]
                    P.op("scalar", lambda s, gp=gp, i=i: s.activation(
                        gsb[i][:, :], gp, AF.Silu), inc=cs,
                        waits=[(ct, ct.n)] + free.get(("gsb", i), []))
                    P.op("vector", lambda v, up=up, i=i, ft=ft: v.tensor_tensor(
                        hT1[:, ft * 512:(ft + 1) * 512], gsb[i][:, :], up, ALU.mult),
                        inc=cv, waits=[(cs, cs.n)] + free.get(("hT1", ft), []))
                    free[("gsb", i)] = [(cv, cv.n)]
                    free[("gup", par)] = [(cv, cv.n)]
                hdone1 = [(cv, cv.n)]
                ypools = [pool[0], pool[1], pool[2], pool[3]]
                pwy = free.get("ypsf", [])
                w2rdys = {}

                def load_w2ft(ft2):
                    ring, rk, qu = ((wstf, "wstf", "sync") if ft2 % 2 == 0
                                    else (wsts, "wsts", "scalar"))
                    return emit_wload(
                        [(w2c[ft2 % 4][:, :],
                          dram["l1_w2"][e][ft2 * 128:(ft2 + 1) * 128, :], D)],
                        dst_free=free.get(("w2c", ft2 % 4), []),
                        ring=ring, rkey=rk, queue=qu)

                w2rdys[0] = load_w2ft(0)
                w2rdys[1] = load_w2ft(1)
                for ft in range(NFT):
                    i = ft % 4
                    if ft + 2 < NFT:
                        w2rdys[ft + 2] = load_w2ft(ft + 2)
                    w2rdy1 = w2rdys[ft]
                    for sl in range(4):
                        for half in range(2):
                            P.op("tensor", lambda t, sl=sl, half=half, ft=ft, i=i, e=e: t.matmul(
                                ypools[sl][:, half * 512:(half + 1) * 512],
                                hT1[:, ft * 512 + sl * 128: ft * 512 + (sl + 1) * 128],
                                w2c[i][:, half * 512:(half + 1) * 512],
                                start=(ft == 0), stop=(ft == NFT - 1)),
                                inc=(ct if (sl == 3 and half == 1) else None),
                                waits=(w2rdy1 + ((hdone1 + pwy) if ft == 0 else [])) if (sl == 0 and half == 0) else ())
                    free[("w2c", i)] = [(ct, ct.n)]
                free.update({("hT1", ft): [(ct, ct.n)] for ft in range(NFT)})
                ydone1 = [(ct, ct.n)]
                for sl in range(4):
                    i = sl % 2
                    b, st2 = sl // 2, sl % 2
                    ti = e * 2 + st2
                    P.op("sync", lambda s, i=i, b=b, ti=ti: s.dma_start(
                        xres[i][:, :], x_att[b, ti * 128:(ti + 1) * 128, :]),
                        inc=dctr(f"xres{i}"), k=16, waits=free.get(("xres", i), []))
                    xrdy3 = dmadone(f"xres{i}")
                    for half in range(2):
                        P.op("vector", lambda v, i=i, half=half, sl=sl: v.tensor_tensor(
                            res_sb[i][:, half * 512:(half + 1) * 512],
                            ypools[sl][:, half * 512:(half + 1) * 512],
                            xres[i][:, half * 512:(half + 1) * 512], ALU.add),
                            inc=(cv if half == 1 else None),
                            waits=(ydone1 + xrdy3 + free.get(("ressb", i), [])) if half == 0 else ())
                    free[("xres", i)] = [(cv, cv.n)]
                    P.op("sync", lambda s, i=i, b=b, ti=ti: s.dma_start(
                        y_out[b, ti * 128:(ti + 1) * 128, :], res_sb[i][:, :]),
                        inc=dctr(f"res{i}"), k=16, waits=[(cv, cv.n)])
                    free[("ressb", i)] = dmadone(f"res{i}")
                free["ypsf"] = [(cv, cv.n)]
                free[("gup", 0)] = [(cv, cv.n)]
                free[("gup", 1)] = [(cv, cv.n)]
            sf.close()
            sx1.close()

        if 1 not in layers:
            sln.close()
        finals = [(c.h, c.n) for c in ALLC]
        P.raw("sync", lambda e, fl=finals: [e.wait_ge(h, n) for h, n in fl if n > 0])
        P.realize(nc)
    return nc


def _emit_layer1_body(*a, **k):
    raise NotImplementedError("layer 1 not yet implemented")


FUSED = True


def kernel(**inputs):
    W = _host_prep(inputs)
    x = np.ascontiguousarray(np.asarray(inputs["L0_attn_x"], np.float32))
    shards = [x[2 * c:2 * c + 2] for c in range(NCORES)]

    l0_keys = ["l0_wqkv", "l0_wf", "l0_bf_row", "l0_bf_col", "l0_wout",
               "l0_bout_row", "l0_rwT", "l0_rb_row", "l0_w1", "l0_b1",
               "l0_w2", "l0_b2", "c_pen"]
    l1_keys = ["l1_wqT", "l1_wkT", "l1_wvT", "l1_woT", "l1_bqT", "l1_bkT",
               "l1_bv_row", "l1_bo_row", "l1_w1", "l1_w3", "l1_w2",
               "c_abias", "c_abias2", "c_abiasw", "c_maskm", "c_maskw"]

    if FUSED:
        nc = build_program((0, 1))
        maps = [dict({k: W[k] for k in l0_keys + l1_keys},
                     c_ident=W["c_ident"], x0=shards[c]) for c in range(NCORES)]
        res = run_bass_kernel_spmd(nc, maps, core_ids=list(range(NCORES)))
        return np.concatenate([np.asarray(r["y"]).reshape(NB, T, D) for r in res.results], axis=0)
    nc0 = build_program((0,))
    maps = [dict({k: W[k] for k in l0_keys}, c_ident=W["c_ident"], x0=shards[c])
            for c in range(NCORES)]
    res0 = run_bass_kernel_spmd(nc0, maps, core_ids=list(range(NCORES)))
    xmid = [np.ascontiguousarray(np.asarray(r["x_mid"]).reshape(NB, T, D)) for r in res0.results]
    nc1 = build_program((1,))
    maps = [dict({k: W[k] for k in l1_keys}, c_ident=W["c_ident"], x_mid=xmid[c])
            for c in range(NCORES)]
    res1 = run_bass_kernel_spmd(nc1, maps, core_ids=list(range(NCORES)))
    return np.concatenate([np.asarray(r["y"]).reshape(NB, T, D) for r in res1.results], axis=0)


# version 13
# speedup vs baseline: 1.0220x; 1.0012x over previous
import math
from contextlib import ExitStack

import numpy as np
import ml_dtypes

import concourse.bass as bass
import concourse.mybir as mybir
from concourse.bass_utils import run_bass_kernel_spmd

F32 = mybir.dt.float32
BF16 = mybir.dt.bfloat16
AF = mybir.ActivationFunctionType
ALU = mybir.AluOpType
AX = mybir.AxisListType
BFD = ml_dtypes.bfloat16

NCORES = 8
NB = 2
T = 2048
D = 1024
H = 16
DH = 64
NT = T // 128
NI = D // 128
NG = NB * NT
NC2 = NB * NT // 4
E0, H0 = 32, 256
E1, DFF = 8, 2816
BIG = 1.0e30
SM_SHIFT = 16.0
LN_EPS = 1e-5
XNT_B = NB * T


def _alibi_slopes(n_heads):
    start = 2 ** (-(2 ** (-(math.log2(n_heads) - 3))))
    return [start * start**i for i in range(n_heads)]


def _host_prep(inputs):
    f32 = np.float32
    c = np.ascontiguousarray
    W = {}
    W["l0_wqkv"] = c(np.asarray(inputs["L0_attn_W_qkv"], f32))
    W["l0_wf"] = c(np.asarray(inputs["L0_attn_W_f"], f32))
    bf = np.asarray(inputs["L0_attn_b_f"], f32)
    W["l0_bf_row"] = c(np.tile(bf, H)[None, :])
    W["l0_bf_col"] = c(np.concatenate([bf, bf])[:, None])
    W["l0_wout"] = c(np.asarray(inputs["L0_attn_W_out"], f32))
    W["l0_bout_row"] = c(np.asarray(inputs["L0_attn_b_out"], f32)[None, :])
    W["l0_rwT"] = c(np.asarray(inputs["L0_ffn_router_w"], f32).T)
    W["l0_rb_row"] = c(np.asarray(inputs["L0_ffn_router_b"], f32)[None, :])
    W["l0_w1"] = c(np.asarray(inputs["L0_ffn_weights1"], f32))
    W["l0_b1"] = c(np.asarray(inputs["L0_ffn_bias1"], f32)
                   .reshape(E0, 2, 128).transpose(2, 0, 1).reshape(128, 2 * E0))
    W["l0_w2"] = c(np.asarray(inputs["L0_ffn_weights2"], f32))
    W["l0_b2"] = c(np.asarray(inputs["L0_ffn_bias2"], f32))
    for nm, k in (("q", "Wq"), ("k", "Wk"), ("v", "Wv"), ("o", "Wo")):
        W[f"l1_w{nm}T"] = c(np.asarray(inputs[f"L1_attn_{k}"], f32).T)
    W["l1_bqT"] = c(np.asarray(inputs["L1_attn_bq"], f32).reshape(NI, 128).T)
    W["l1_bkT"] = c(np.asarray(inputs["L1_attn_bk"], f32).reshape(NI, 128).T)
    W["l1_bv_row"] = c(np.asarray(inputs["L1_attn_bv"], f32)[None, :])
    W["l1_bo_row"] = c(np.asarray(inputs["L1_attn_bo"], f32)[None, :])
    W["l1_w1"] = c(np.asarray(inputs["L1_ffn_w1"], f32))
    W["l1_w3"] = c(np.asarray(inputs["L1_ffn_w3"], f32))
    W["l1_w2"] = c(np.asarray(inputs["L1_ffn_w2"], f32))

    W["c_ident"] = np.eye(128, dtype=f32)
    slopes = _alibi_slopes(H)
    abias = np.zeros((128, H * 16), np.float64)
    p = np.arange(128)
    for h in range(H):
        for m2 in range(16):
            abias[:, h * 16 + m2] = slopes[h] * (p - m2 * 128 - 64)
    W["c_abias"] = abias.astype(f32)
    abias2 = np.zeros((128, 3 * 32), np.float64)
    for h in range(3):
        for j in range(32):
            abias2[:, h * 32 + j] = slopes[h] * (p - 64 * j - 32)
    W["c_abias2"] = abias2.astype(f32)
    woff = [0, 4, 12, 24]
    abw = np.zeros((128, 11 * 40), np.float64)
    for h in range(5, H):
        for w in range(4):
            for kt in range(4 * w + 4):
                abw[:, (h - 5) * 40 + woff[w] + kt] = slopes[h] * (kt * 128 + p - (w * 512 + 384))
    W["c_abiasw"] = abw.astype(f32)
    pen = np.full((E0, E0, 128), -BIG, f32)
    for e in range(E0):
        pen[e, e, :] = 0.0
    W["c_pen"] = np.ascontiguousarray(pen.reshape(E0, E0 * 128)).astype(BFD)
    kk = np.arange(128)[:, None]
    qr = np.arange(128)[None, :]
    tri = np.where(kk > qr, -BIG, 0.0).astype(f32)
    W["c_maskm"] = tri.astype(BFD)
    mw = np.zeros((128, 896), f32)
    mw[:, 0:384] = -BIG
    mw[:, 384:512] = tri
    W["c_maskw"] = mw.astype(BFD)
    return W


class Ctr:
    __slots__ = ("h", "n", "name")

    def __init__(self, stack, nc, name):
        self.h = stack.enter_context(nc.semaphore(name))
        self.n = 0
        self.name = name


class Prog:
    ENGS = ("tensor", "vector", "scalar", "gpsimd", "sync")

    def __init__(self):
        self.q = {e: [] for e in self.ENGS}
        self.last_wait = {e: {} for e in self.ENGS}

    def op(self, eng, fn, inc=None, k=1, waits=()):
        lw = self.last_wait[eng]
        wl = []
        for ctr, v in waits:
            if v <= 0 or lw.get(ctr.name, -1) >= v:
                continue
            lw[ctr.name] = v
            wl.append((ctr.h, v))
        tgt = None
        if inc is not None:
            inc.n += k
            tgt = (inc.h, k)

        def emit(e, wl=wl, fn=fn, tgt=tgt):
            for h, v in wl:
                e.wait_ge(h, v)
            ins = fn(e)
            if tgt is not None:
                ins.then_inc(tgt[0], tgt[1])
        self.q[eng].append(emit)

    def raw(self, eng, fn):
        self.q[eng].append(fn)



    def realize(self, nc):
        with nc.Block() as blk:
            for eng in self.ENGS:
                ops = self.q[eng]
                if not ops:
                    continue
                def make(ops):
                    def run(e):
                        for f in ops:
                            f(e)
                    return run
                getattr(blk, eng)(make(ops))


def build_program(layers=(0, 1)):
    nc = bass.Bass("TRN2", target_bir_lowering=False)
    P = Prog()
    dram = {}

    def din(name, shape, dt=F32):
        dram[name] = nc.dram_tensor(name, list(shape), dt, kind="ExternalInput").ap()
        return dram[name]

    def dint(name, shape, dt=F32):
        dram[name] = nc.dram_tensor(name, list(shape), dt, kind="Internal").ap()
        return dram[name]

    def dout(name, shape, dt=F32):
        dram[name] = nc.dram_tensor(name, list(shape), dt, kind="ExternalOutput").ap()
        return dram[name]

    if 0 in layers:
        x0 = din("x0", (NB, T, D))
        din("l0_wqkv", (D, 3 * D)); din("l0_wf", (DH, DH))
        din("l0_bf_row", (1, D)); din("l0_bf_col", (128, 1))
        din("l0_wout", (D, D)); din("l0_bout_row", (1, D))
        din("l0_rwT", (D, E0)); din("l0_rb_row", (1, E0))
        din("l0_w1", (E0, D, H0)); din("l0_b1", (128, 2 * E0))
        din("l0_w2", (E0, H0, D)); din("l0_b2", (E0, D))
        din("c_pen", (E0, E0 * 128), BF16)
        hb = dint("hbounce", (NG, 128, 2 * E0, 128), BF16)
    if 1 in layers:
        for nm in "qkvo":
            din(f"l1_w{nm}T", (D, D))
        din("l1_bqT", (128, NI)); din("l1_bkT", (128, NI))
        din("l1_bv_row", (1, D)); din("l1_bo_row", (1, D))
        din("l1_w1", (E1, D, DFF)); din("l1_w3", (E1, D, DFF))
        din("l1_w2", (E1, DFF, D))
        din("c_abias", (128, H * 16)); din("c_abias2", (128, 96))
        din("c_abiasw", (128, 440))
        din("c_maskm", (128, 128), BF16); din("c_maskw", (128, 896), BF16)
    din("c_ident", (128, 128))

    if layers == (0,):
        x_mid = dout("x_mid", (NB, T, D))
    elif 0 in layers:
        x_mid = dint("x_mid", (NB, T, D))
    else:
        x_mid = din("x_mid", (NB, T, D))
    if 1 in layers:
        y_out = dout("y", (NB, T, D))
        x_att = dint("x_att", (NB, T, D))
    xn_hbm = dint("xn_hbm", (NB, T, D), BF16)

    st = ExitStack()
    with st:
        E = st.enter_context
        ct = Ctr(st, nc, "ct")
        cv = Ctr(st, nc, "cv")
        cs = Ctr(st, nc, "cs")
        cg = Ctr(st, nc, "cg")
        ch = Ctr(st, nc, "ch")
        cq = Ctr(st, nc, "cq")
        ALLC = [ct, cv, cs, cg, ch, cq]
        dctrs = {}

        def dctr(name):
            if name not in dctrs:
                dctrs[name] = Ctr(st, nc, "d_" + name)
                ALLC.append(dctrs[name])
            return dctrs[name]

        def dmadone(name):
            c = dctr(name)
            return [(c, c.n)]

        def allw():
            return [(c, c.n) for c in ALLC]

        pool = [E(nc.psum_tensor(f"ps{i}", [128, 1024], F32)) for i in range(4)]

        def slot(i, rows=128, cols=512, r0=0):
            return pool[i // 2][r0:r0 + rows, (i % 2) * 512:(i % 2) * 512 + cols]

        ident_sb = E(nc.sbuf_tensor("ident_sb", [128, 128], F32))
        ones_sb = E(nc.sbuf_tensor("ones_sb", [1, 128], BF16))
        P.op("gpsimd", lambda g: g.dma_start(ident_sb[:, :], dram["c_ident"][:, :]),
             inc=cg, k=16)
        P.op("vector", lambda v: v.memset(ones_sb[:, :], 1.0), inc=cv)
        cst = E(nc.sbuf_tensor("cst", [128, 4], F32))
        P.op("vector", lambda v: v.memset(cst[:, 0:1], LN_EPS))
        P.op("vector", lambda v: v.memset(cst[:, 1:2], -0.5))
        P.op("vector", lambda v: v.memset(cst[:, 2:3], 1e-6))
        P.op("vector", lambda v: v.memset(cst[:, 3:4], -1.0), inc=cv)
        wst = [E(nc.sbuf_tensor(f"wst{i}", [128, 1024], F32)) for i in range(2)]
        wsti = {}

        def emit_wload(pairs, waits=(), dst_free=(), ring=None, rkey="wst", queue="scalar",
                       cast_eng="vector"):
            cc = cv if cast_eng == "vector" else cs
            bufs = wst if ring is None else ring
            nb = len(bufs)
            groups, cur, acc = [], [], 0
            for pr in pairs:
                if acc + pr[2] > 1024:
                    groups.append(cur); cur, acc = [], 0
                cur.append(pr); acc += pr[2]
            if cur:
                groups.append(cur)
            firstall = True
            for grp in groups:
                i = wsti.get(rkey, 0) % nb
                wsti[rkey] = wsti.get(rkey, 0) + 1
                wc = dctr(f"{rkey}{i}")
                off = 0
                first = True
                for dst, srcap, cols in grp:
                    w = list(free.get((rkey, i), []))
                    if firstall and first:
                        w = list(waits) + w
                    P.op(queue, lambda s, i=i, off=off, srcap=srcap, cols=cols: s.dma_start(
                        bufs[i][:, off:off + cols], srcap), inc=wc, k=16,
                        waits=(w if first else ()))
                    first = False
                    off += cols
                ms = [(wc, wc.n)]
                off = 0
                first = True
                for dst, srcap, cols in grp:
                    w = (ms + (list(dst_free) if firstall else [])) if first else ()
                    if cast_eng == "vector":
                        P.op("vector", lambda v, i=i, off=off, dst=dst, cols=cols: v.tensor_copy(
                            dst, bufs[i][:, off:off + cols]), inc=cc, waits=w)
                    else:
                        P.op("scalar", lambda s, i=i, off=off, dst=dst, cols=cols: s.copy(
                            dst, bufs[i][:, off:off + cols]), inc=cc, waits=w)
                    first = False
                    off += cols
                free[(rkey, i)] = [(cc, cc.n)]
                firstall = False
            return [(cc, cc.n)]

        xres = [E(nc.sbuf_tensor(f"xres{i}", [128, D], F32)) for i in range(2)]
        res_sb = [E(nc.sbuf_tensor(f"res{i}", [128, D], F32)) for i in range(2)]
        sln = ExitStack()
        L = sln.enter_context
        xin = [L(nc.sbuf_tensor(f"xin{i}", [128, D], F32)) for i in range(2)]
        xn_sb = [L(nc.sbuf_tensor(f"xnsb{i}", [128, D], BF16)) for i in range(2)]
        bn6 = [L(nc.sbuf_tensor(f"bn6_{i}", [128, 12], F32)) for i in range(2)]
        mvb = [L(nc.sbuf_tensor(f"mv{i}", [128, 2], F32)) for i in range(2)]
        rsd = [L(nc.sbuf_tensor(f"rstd{i}", [128, 2], F32)) for i in range(2)]

        free = {}

        def emit_layernorm(src_dram, xnT):
            base = allw()
            for i in range(2):
                for key in (("xin", i), ("bn", i), ("mv", i), ("rstd", i), ("xn", i)):
                    free.pop(key, None)
            for g in range(NG):
                b, t = divmod(g, NT)
                i = g % 2
                xi, xo, b6, m2, rs = xin[i], xn_sb[i], bn6[i], mvb[i], rsd[i]
                P.op("sync", lambda s, xi=xi, b=b, t=t: s.dma_start(
                    xi[:, :], src_dram[b, t * 128:(t + 1) * 128, :]),
                    inc=dctr(f"xin{i}"), k=16, waits=free.get(("xin", i), base))
                ld = dmadone(f"xin{i}")
                P.op("vector", lambda v, b6=b6, xi=xi: v.bn_stats(b6[:, 0:6], xi[:, 0:512]),
                     waits=ld + free.get(("bn", i), base))
                P.op("vector", lambda v, b6=b6, xi=xi: v.bn_stats(b6[:, 6:12], xi[:, 512:1024]),
                     inc=cv)
                P.op("vector", lambda v, b6=b6, m2=m2: v.bn_aggr(m2[:, :], b6[:, :]),
                     inc=cv, waits=[(cv, cv.n)] + free.get(("mv", i), base))
                free[("bn", i)] = [(cv, cv.n)]
                P.op("scalar", lambda s, m2=m2, rs=rs: s.activation(
                    rs[:, 0:1], m2[:, 1:2], AF.Ln, bias=cst[:, 0:1]), inc=cs,
                    waits=[(cv, cv.n)] + free.get(("rstd", i), base))
                P.op("scalar", lambda s, rs=rs: s.activation(
                    rs[:, 1:2], rs[:, 0:1], AF.Exp, scale=cst[:, 1:2]), inc=cs,
                    waits=[(cs, cs.n)])
                P.op("vector", lambda v, xo=xo, xi=xi, m2=m2, rs=rs: v.tensor_scalar(
                    xo[:, :], xi[:, :], m2[:, 0:1], rs[:, 1:2], ALU.subtract, ALU.mult),
                    inc=cv, waits=[(cs, cs.n)] + free.get(("xn", i), base))
                free[("xin", i)] = [(cv, cv.n)]
                free[("mv", i)] = [(cv, cv.n), (cs, cs.n)]
                free[("rstd", i)] = [(cv, cv.n), (cs, cs.n)]
                P.op("sync", lambda s, xo=xo, b=b, t=t: s.dma_start(
                    xn_hbm[b, t * 128:(t + 1) * 128, :], xo[:, :]),
                    inc=dctr(f"xnst{i}"), k=16, waits=[(cv, cv.n)])
                free[("xn", i)] = dmadone(f"xnst{i}")
            stored = dmadone("xnst0") + dmadone("xnst1") + base
            for it in range(NI):
                for b in range(NB):
                    off = it * XNT_B + b * T
                    P.op("sync", lambda s, it=it, b=b, off=off: s.dma_start_transpose(
                        xnT[:, off:off + T], xn_hbm[b, :, it * 128:(it + 1) * 128]),
                        inc=dctr("tp"), k=16, waits=stored)
            return dmadone("tp")

        def xnt_col(it, g):
            return it * XNT_B + g * 128

        if 0 in layers:
            sx = ExitStack()
            xnT = sx.enter_context(nc.sbuf_tensor("xnT", [128, NI * XNT_B], BF16, side="right"))
            sa0 = ExitStack()
            A = sa0.enter_context
            wf_sb = A(nc.sbuf_tensor("wf_sb", [128, DH], BF16, side="right"))
            bfrow_sb = A(nc.sbuf_tensor("bfrow_sb", [1, D], BF16, side="right"))
            bfcol_sb = A(nc.sbuf_tensor("bfcol_sb", [128, 1], F32, side="right"))
            boutrow_sb = A(nc.sbuf_tensor("boutrow_sb", [1, D], BF16, side="right"))
            kt_sb = A(nc.sbuf_tensor("kt_sb", [128, 8 * 512], BF16, side="right"))
            fk_sb = [A(nc.sbuf_tensor(f"fk{i}", [128, D], BF16, side="right")) for i in range(2)]
            vext = [A(nc.sbuf_tensor(f"vex0_{i}", [128, H * 66], BF16, side="right")) for i in range(2)]
            kv_sb = [A(nc.sbuf_tensor(f"kv{i}", [128, 8 * 65], BF16, side="right")) for i in range(2)]
            kvacc = A(nc.sbuf_tensor("kvacc", [128, 2 * 260], F32, side="right"))
            qt_sb = [A(nc.sbuf_tensor(f"qt{i}", [128, 512], BF16, side="right")) for i in range(2)]
            fq_sb = [A(nc.sbuf_tensor(f"fq{i}", [128, 512], BF16, side="right")) for i in range(2)]
            oe_sb = [A(nc.sbuf_tensor(f"oe{i}", [128, 512], BF16, side="right")) for i in range(2)]
            tln = [A(nc.sbuf_tensor(f"tln{i}", [1, 512], F32, side="right")) for i in range(2)]
            rrow = [A(nc.sbuf_tensor(f"rrow{i}", [1, 512], BF16, side="right")) for i in range(2)]
            apT = A(nc.sbuf_tensor("apT", [128, NI * T], BF16, side="right"))

            P.op("gpsimd", lambda g: g.dma_start(wf_sb[0:64, :], dram["l0_wf"][:, :]),
                 inc=cg, k=16)
            P.op("gpsimd", lambda g: g.dma_start(wf_sb[64:128, :], dram["l0_wf"][:, :]),
                 inc=cg, k=16)
            P.op("gpsimd", lambda g: g.dma_start(bfrow_sb[:, :], dram["l0_bf_row"][:, :]),
                 inc=cg, k=16)
            P.op("gpsimd", lambda g: g.dma_start(bfcol_sb[:, :], dram["l0_bf_col"][:, :]),
                 inc=cg, k=16)
            P.op("gpsimd", lambda g: g.dma_start(boutrow_sb[:, :], dram["l0_bout_row"][:, :]),
                 inc=cg, k=16)
            for i in range(2):
                P.op("vector", lambda v, i=i: v.memset(
                    vext[i][:, :].rearrange("p (h c) -> p h c", c=66)[:, :, 64:66], 1.0),
                    inc=cv)

            sa1 = ExitStack()
            wk_sb = sa1.enter_context(nc.sbuf_tensor("wk_sb", [128, NI * D], BF16, side="right"))
            wv_sb = sa1.enter_context(nc.sbuf_tensor("wv_sb", [128, NI * D], BF16, side="right"))
            for it in range(NI):
                wload = emit_wload([
                    (wk_sb[:, it * D:(it + 1) * D],
                     dram["l0_wqkv"][it * 128:(it + 1) * 128, D:2 * D], D),
                    (wv_sb[:, it * D:(it + 1) * D],
                     dram["l0_wqkv"][it * 128:(it + 1) * 128, 2 * D:3 * D], D)])

            wload = wload + [(cg, cg.n)]
            xnT_ready = emit_layernorm(x0, xnT)

            for b in range(NB):
                P.op("vector", lambda v: v.memset(kvacc[:, :], 0.0), inc=cv,
                     waits=free.get("kvacc", []))
                for c in range(4):
                    for kt in range(8):
                        ps = slot(kt % 2)
                        pw = free.get(("ps01", kt % 2), [])
                        for it in range(NI):
                            P.op("tensor", lambda t, ps=ps, it=it, kt=kt, c=c, b=b: t.matmul(
                                ps, wk_sb[:, it * D + kt * 128: it * D + (kt + 1) * 128],
                                xnT[:, xnt_col(it, b * NT) + c * 512: xnt_col(it, b * NT) + (c + 1) * 512],
                                start=(it == 0), stop=(it == NI - 1)),
                                inc=(ct if it == NI - 1 else None),
                                waits=(xnT_ready + wload + pw) if it == 0 else ())
                        P.op("scalar", lambda s, ps=ps, kt=kt: s.copy(
                            kt_sb[:, kt * 512:(kt + 1) * 512], ps), inc=cs,
                            waits=[(ct, ct.n)] + free.get(("ktsb", kt), []))
                        free[("ps01", kt % 2)] = [(cs, cs.n)]
                    ktev = [(cs, cs.n)]
                    for ti in range(4):
                        g = b * NT + c * 4 + ti
                        i = ti % 2
                        fkps0 = slot(2, cols=512)
                        fkps1 = slot(3, cols=512)
                        pw = free.get("fkps", [])
                        for h in range(H):
                            ps = fkps0 if h < 8 else fkps1
                            co = (h % 8) * 64
                            hb0 = (h % 2) * 64
                            P.op("tensor", lambda t, ps=ps, co=co, h=h, ti=ti, hb0=hb0: t.matmul(
                                ps[:, co:co + 64],
                                kt_sb[hb0:hb0 + 64,
                                      (h // 2) * 512 + ti * 128:(h // 2) * 512 + (ti + 1) * 128],
                                wf_sb[hb0:hb0 + 64, :], start=True, stop=False),
                                waits=(ktev + pw + wload) if h == 0 else ())
                            P.op("tensor", lambda t, ps=ps, co=co, h=h: t.matmul(
                                ps[:, co:co + 64], ones_sb[:, :],
                                bfrow_sb[:, h * 64:(h + 1) * 64], start=False, stop=True),
                                inc=(ct if h == H - 1 else None))
                        P.op("scalar", lambda s, i=i: s.activation(
                            fk_sb[i][:, :], pool[1][:, :], AF.Relu), inc=cs,
                            waits=[(ct, ct.n)] + free.get(("fksb", i), []))
                        free["fkps"] = [(cs, cs.n)]
                        fkdone = [(cs, cs.n)]
                        vps = pool[2]
                        pw = free.get("vps", [])
                        for half in range(2):
                            for it in range(NI):
                                P.op("tensor", lambda t, vps=vps, half=half, it=it, g=g: t.matmul(
                                    vps[:, half * 512:(half + 1) * 512],
                                    xnT[:, xnt_col(it, g): xnt_col(it, g) + 128],
                                    wv_sb[:, it * D + half * 512: it * D + (half + 1) * 512],
                                    start=(it == 0), stop=(it == NI - 1)),
                                    inc=(ct if it == NI - 1 and half == 1 else None),
                                    waits=pw if (half == 0 and it == 0) else ())
                        P.op("vector", lambda v, i=i: v.tensor_copy(
                            vext[i][:, 0:528].rearrange("p (h c) -> p h c", c=66)[:, :, 0:64],
                            pool[2][:, 0:512].rearrange("p (h c) -> p h c", c=64)),
                            waits=[(ct, ct.n)] + free.get(("vext", i), []))
                        P.op("vector", lambda v, i=i: v.tensor_copy(
                            vext[i][:, 528:1056].rearrange("p (h c) -> p h c", c=66)[:, :, 0:64],
                            pool[2][:, 512:1024].rearrange("p (h c) -> p h c", c=64)),
                            inc=cv)
                        free["vps"] = [(cv, cv.n)]
                        vdone = [(cv, cv.n)]
                        for h in range(H):
                            kvp = pool[3][(h % 2) * 64:(h % 2) * 64 + 64,
                                          (h // 2 % 4) * 65 + (h // 8) * 512:
                                          (h // 2 % 4) * 65 + (h // 8) * 512 + 65]
                            P.op("tensor", lambda t, kvp=kvp, h=h, i=i: t.matmul(
                                kvp, fk_sb[i][:, h * 64:(h + 1) * 64],
                                vext[i][:, h * 66:h * 66 + 65],
                                start=True, stop=True),
                                inc=(ct if h == H - 1 else None),
                                waits=(fkdone + vdone + free.get("kvps", [])) if h == 0 else ())
                        free[("fksb", i)] = [(ct, ct.n)]
                        free[("vext", i)] = [(ct, ct.n)]
                        P.op("vector", lambda v: v.tensor_tensor(
                            kvacc[:, 0:260], kvacc[:, 0:260], pool[3][:, 0:260], ALU.add),
                            waits=[(ct, ct.n), (cv, cv.n)])
                        P.op("vector", lambda v: v.tensor_tensor(
                            kvacc[:, 260:520], kvacc[:, 260:520], pool[3][:, 512:772], ALU.add),
                            inc=cv)
                        free["kvps"] = [(cv, cv.n)]
                    for kt in range(8):
                        free[("ktsb", kt)] = [(ct, ct.n)]
                P.op("vector", lambda v, b=b: v.tensor_copy(
                    kv_sb[b][:, :], kvacc[:, :]),
                    inc=cv, waits=[(cv, cv.n)] + free.get(("kvsb", b), []))
                free["kvacc"] = [(cv, cv.n)]
            kvdone = [(cv, cv.n)]

            sa1.close()
            sa2 = ExitStack()
            wq_sb = sa2.enter_context(nc.sbuf_tensor("wq_sb", [128, NI * D], BF16, side="right"))
            wout_sb = sa2.enter_context(nc.sbuf_tensor("wout_sb", [128, NI * D], BF16, side="right"))
            pa_done = [(ct, ct.n)]
            for it in range(NI):
                wload2 = emit_wload([
                    (wq_sb[:, it * D:(it + 1) * D],
                     dram["l0_wqkv"][it * 128:(it + 1) * 128, 0:D], D),
                    (wout_sb[:, it * D:(it + 1) * D],
                     dram["l0_wout"][it * 128:(it + 1) * 128, :], D)],
                    dst_free=pa_done)

            free[("ps01", 0)] = kvdone
            free[("ps01", 1)] = kvdone
            free["wops"] = kvdone
            pend_op = []
            for b in range(NB):
                for c in range(4):
                    for p in range(8):
                        i = p % 2
                        ps = slot(i)
                        for it in range(NI):
                            P.op("tensor", lambda t, ps=ps, it=it, p=p, c=c, b=b: t.matmul(
                                ps, wq_sb[:, it * D + p * 128: it * D + (p + 1) * 128],
                                xnT[:, xnt_col(it, b * NT) + c * 512: xnt_col(it, b * NT) + (c + 1) * 512],
                                start=(it == 0), stop=(it == NI - 1)),
                                inc=(ct if it == NI - 1 else None),
                                waits=(kvdone + wload2 + free.get(("ps01", i), [])) if it == 0 else ())
                        P.op("scalar", lambda s, ps=ps, i=i: s.copy(qt_sb[i][:, :], ps),
                             inc=cs, waits=[(ct, ct.n)] + free.get(("qtsb", i), []))
                        qtdone = [(cs, cs.n)]
                        fqp = slot(i)
                        P.op("tensor", lambda t, fqp=fqp, i=i: t.matmul(
                            fqp[0:64, :], wf_sb[0:64, :], qt_sb[i][0:64, :],
                            start=True, stop=True),
                            waits=qtdone)
                        P.op("tensor", lambda t, fqp=fqp, i=i: t.matmul(
                            fqp[64:128, :], wf_sb[64:128, :], qt_sb[i][64:128, :],
                            start=True, stop=True), inc=ct)
                        free[("qtsb", i)] = [(ct, ct.n)]
                        P.op("scalar", lambda s, fqp=fqp, i=i: s.activation(
                            fq_sb[i][:, :], fqp, AF.Relu, bias=bfcol_sb[:, 0:1]),
                            inc=cs, waits=[(ct, ct.n)] + free.get(("fqsb", i), []))
                        free[("ps01", i)] = [(cs, cs.n)]
                        fqdone = [(cs, cs.n)]
                        oep = slot(4)
                        n0 = slot(5, rows=1)
                        n1 = slot(6, rows=1)
                        pw = free.get("oeps", [])
                        for hh in range(2):
                            h = 2 * p + hh
                            kcol = (h // 2 % 4) * 65 + (h // 8) * 260
                            kbase = (h % 2) * 64
                            P.op("tensor", lambda t, oep=oep, hh=hh, kcol=kcol, kbase=kbase, i=i, b=b: t.matmul(
                                oep[hh * 64:(hh + 1) * 64, :],
                                kv_sb[b][kbase:kbase + 64, kcol:kcol + 64],
                                fq_sb[i][hh * 64:(hh + 1) * 64, :],
                                start=True, stop=True),
                                waits=(fqdone + pw) if hh == 0 else ())
                            nps = n0 if hh == 0 else n1
                            P.op("tensor", lambda t, nps=nps, hh=hh, kcol=kcol, kbase=kbase, i=i, b=b: t.matmul(
                                nps, kv_sb[b][kbase:kbase + 64, kcol + 64:kcol + 65],
                                fq_sb[i][hh * 64:(hh + 1) * 64, :],
                                start=True, stop=True),
                                inc=(ct if hh == 1 else None))
                        oedone = [(ct, ct.n)]
                        free[("fqsb", i)] = oedone
                        rbp = slot(7)
                        for hh in range(2):
                            nps = n0 if hh == 0 else n1
                            P.op("scalar", lambda s, nps=nps, hh=hh: s.activation(
                                tln[hh][:, :], nps, AF.Ln, bias=cst[0:1, 2:3]), inc=cs,
                                waits=oedone + free.get(("tln", hh), []))
                            P.op("scalar", lambda s, hh=hh: s.activation(
                                rrow[hh][:, :], tln[hh][:, :], AF.Exp, scale=cst[0:1, 3:4]),
                                inc=cs, waits=[(cs, cs.n)] + free.get(("rrow", hh), []))
                            free[("tln", hh)] = [(cs, cs.n)]
                        rdone = [(cs, cs.n)]
                        if pend_op:
                            pend_op.pop(0)()
                        pwrb = free.get("rbps", [])
                        for hh in range(2):
                            P.op("tensor", lambda t, rbp=rbp, hh=hh: t.matmul(
                                rbp[hh * 64:(hh + 1) * 64, :], ones_sb[0:1, 0:64],
                                rrow[hh][:, :], start=True, stop=True),
                                inc=(ct if hh == 1 else None),
                                waits=(rdone + pwrb) if hh == 0 else ())
                        free[("rrow", 0)] = [(ct, ct.n)]
                        free[("rrow", 1)] = [(ct, ct.n)]
                        P.op("scalar", lambda s, oep=oep, i=i: s.copy(oe_sb[i][:, :], oep),
                             inc=cs, waits=[(ct, ct.n)] + free.get(("oesb", i), []))
                        free["oeps"] = [(cs, cs.n)]
                        P.op("vector", lambda v, i=i, p=p, c=c, rbp=rbp: v.tensor_tensor(
                            apT[:, p * T + c * 512:p * T + (c + 1) * 512],
                            oe_sb[i][:, :], rbp[:, :], ALU.mult),
                            inc=cv, waits=[(cs, cs.n)] + free.get(("apT", p, c), []))
                        free[("oesb", i)] = [(cv, cv.n)]
                        free["rbps"] = [(cv, cv.n)]
                    apdone = [(cv, cv.n)]

                    def mk_op0(b, c, ti, apdone):
                        def run():
                            i = ti % 2
                            yp = pool[1]
                            P.op("sync", lambda s: s.dma_start(
                                xres[i][:, :], x0[b, (c * 4 + ti) * 128:(c * 4 + ti + 1) * 128, :]),
                                inc=dctr(f"xres{i}"), k=16, waits=free.get(("xres", i), []))
                            pw = free.get("wops", [])
                            for it in range(NI):
                                P.op("tensor", lambda t, it=it: t.matmul(
                                    yp[:, 0:512],
                                    apT[:, it * T + (c * 4 + ti) * 128: it * T + (c * 4 + ti + 1) * 128],
                                    wout_sb[:, it * D: it * D + 512],
                                    start=(it == 0), stop=False),
                                    waits=(apdone + pw + wload2) if it == 0 else ())
                            P.op("tensor", lambda t: t.matmul(
                                yp[:, 0:512], ones_sb[:, :], boutrow_sb[:, 0:512],
                                start=False, stop=True))
                        return run

                    def mk_op1(b, c, ti, apdone):
                        def run():
                            i = ti % 2
                            yp = pool[1]
                            xrdy = dmadone(f"xres{i}")
                            for it in range(NI):
                                P.op("tensor", lambda t, it=it: t.matmul(
                                    yp[:, 512:1024],
                                    apT[:, it * T + (c * 4 + ti) * 128: it * T + (c * 4 + ti + 1) * 128],
                                    wout_sb[:, it * D + 512: it * D + 1024],
                                    start=(it == 0), stop=False))
                            P.op("tensor", lambda t: t.matmul(
                                yp[:, 512:1024], ones_sb[:, :], boutrow_sb[:, 512:1024],
                                start=False, stop=True), inc=ct)
                            if ti == 3:
                                free.update({("apT", pp, c): [(ct, ct.n)] for pp in range(8)})
                            for half in range(2):
                                P.op("vector", lambda v, half=half: v.tensor_tensor(
                                    res_sb[i][:, half * 512:(half + 1) * 512],
                                    yp[:, half * 512:(half + 1) * 512],
                                    xres[i][:, half * 512:(half + 1) * 512], ALU.add),
                                    inc=(cv if half == 1 else None),
                                    waits=([(ct, ct.n)] + xrdy + free.get(("ressb", i), [])) if half == 0 else ())
                            free[("xres", i)] = [(cv, cv.n)]
                            free["wops"] = [(cv, cv.n)]
                            P.op("sync", lambda s: s.dma_start(
                                x_mid[b, (c * 4 + ti) * 128:(c * 4 + ti + 1) * 128, :],
                                res_sb[i][:, :]), inc=dctr(f"res{i}"), k=16, waits=[(cv, cv.n)])
                            free[("ressb", i)] = dmadone(f"res{i}")
                        return run

                    for ti in range(4):
                        pend_op.append(mk_op0(b, c, ti, apdone))
                        pend_op.append(mk_op1(b, c, ti, apdone))
            while pend_op:
                pend_op.pop(0)()
            attn_done = [(ct, ct.n), (cv, cv.n), (cs, cs.n)]
            sa2.close()
            sa0.close()

            mT_sb = L(nc.sbuf_tensor("mT_sb", [E0, NG * 128], BF16))
            sm = ExitStack()
            M = sm.enter_context
            rw_sb = M(nc.sbuf_tensor("rw_sb", [128, NI * E0], BF16, side="right"))
            rb_sb = M(nc.sbuf_tensor("rb_sb", [1, E0], BF16, side="right"))
            pen_sb = M(nc.sbuf_tensor("pen_sb", [E0, E0 * 128], BF16, side="right"))
            b1_sb = M(nc.sbuf_tensor("b1_sb", [128, 2 * E0], F32, side="right"))
            lmax = [M(nc.sbuf_tensor(f"lmax{i}", [128, 1], F32, side="right")) for i in range(2)]
            mf_sb = [M(nc.sbuf_tensor(f"mf{i}", [128, E0], F32, side="right")) for i in range(2)]
            w1_sb = [M(nc.sbuf_tensor(f"w1sb{i}", [128, NI * H0], BF16, side="right")) for i in range(2)]
            hT_sb = [M(nc.sbuf_tensor(f"hT0_{i}", [128, 512], BF16, side="right")) for i in range(4)]
            w1st = [M(nc.sbuf_tensor(f"w1st{i}", [128, 1024], F32, side="right")) for i in range(2)]
            for i in range(2):
                free[("w1st", i)] = attn_done
                free[("w1", i)] = attn_done
            rw_rdy = emit_wload([(rw_sb[:, :].rearrange("p (i e) -> p i e", e=E0),
                                  dram["l0_rwT"].rearrange("(i p) e -> p i e", p=128), NI * E0)],
                                dst_free=attn_done)
            P.op("gpsimd", lambda g: g.dma_start(rb_sb[:, :], dram["l0_rb_row"][:, :]),
                 inc=dctr("rcst"), k=16, waits=attn_done)
            P.op("gpsimd", lambda g: g.dma_start(pen_sb[:, :], dram["c_pen"][:, :]),
                 inc=dctr("rcst"), k=16)
            P.op("gpsimd", lambda g: g.dma_start(b1_sb[:, :], dram["l0_b1"][:, :]),
                 inc=dctr("rcst"), k=16)
            rld = dmadone("rcst") + rw_rdy
            for g in range(NG):
                i = g % 2
                lp = slot(i, cols=E0)
                for it in range(NI):
                    P.op("tensor", lambda t, lp=lp, it=it, g=g: t.matmul(
                        lp, xnT[:, xnt_col(it, g): xnt_col(it, g) + 128],
                        rw_sb[:, it * E0:(it + 1) * E0],
                        start=(it == 0), stop=False),
                        waits=(rld + free.get(("lps", i), [])) if it == 0 else ())
                P.op("tensor", lambda t, lp=lp: t.matmul(
                    lp, ones_sb[:, :], rb_sb[:, :], start=False, stop=True), inc=ct)
                P.op("vector", lambda v, lp=lp, i=i: v.reduce_max(
                    lmax[i][:, :], lp, axis=AX.X), inc=cv,
                    waits=[(ct, ct.n)] + free.get(("lmax", i), []))
                P.op("vector", lambda v, lp=lp, i=i: v.tensor_single_scalar(
                    mf_sb[i][:, :], lp, lmax[i][:, :], ALU.is_equal), inc=cv,
                    waits=[(cv, cv.n)] + free.get(("mf", i), []))
                free[("lps", i)] = [(cv, cv.n)]
                free[("lmax", i)] = [(cv, cv.n)]
                mtp = slot(2, rows=E0, cols=128)
                P.op("tensor", lambda t, mtp=mtp, i=i: t.transpose(
                    mtp, mf_sb[i][:, :], ident_sb[:, :]), inc=ct,
                    waits=[(cv, cv.n)] + free.get("mtp", []))
                free[("mf", i)] = [(ct, ct.n)]
                P.op("vector", lambda v, mtp=mtp, g=g: v.tensor_copy(
                    mT_sb[:, g * 128:(g + 1) * 128], mtp), inc=cv,
                    waits=[(ct, ct.n)])
                free["mtp"] = [(cv, cv.n)]
            mtdone = [(cv, cv.n)]

            def load_w1(e):
                i = e % 2
                return emit_wload(
                    [(w1_sb[i][:, a * 1024:(a + 1) * 1024].rearrange("p (i f) -> p i f", f=H0),
                      dram["l0_w1"][e][a * 512:(a + 1) * 512, :].rearrange("(i p) f -> p i f", p=128), 1024)
                     for a in range(2)],
                    dst_free=free.get(("w1", i), []), ring=w1st, rkey="w1st", queue="sync")

            w1rdys = {0: load_w1(0)}
            for e in range(E0):
                i = e % 2
                if e + 1 < E0:
                    w1rdys[e + 1] = load_w1(e + 1)
                w1rdy = w1rdys[e]
                for fh in range(2):
                    for g2 in range(NC2):
                        j = (fh * NC2 + g2) % 2
                        hp = slot(j)
                        for it in range(NI):
                            P.op("tensor", lambda t, hp=hp, it=it, i=i, fh=fh, g2=g2: t.matmul(
                                hp, w1_sb[i][:, it * H0 + fh * 128: it * H0 + fh * 128 + 128],
                                xnT[:, it * XNT_B + g2 * 512: it * XNT_B + (g2 + 1) * 512],
                                start=(it == 0), stop=False),
                                waits=(w1rdy + mtdone + free.get(("hps", j), [])) if it == 0 else ())
                        P.op("tensor", lambda t, hp=hp, e=e, g2=g2: t.matmul(
                            hp, pen_sb[:, e * 128:(e + 1) * 128],
                            mT_sb[:, g2 * 512:(g2 + 1) * 512],
                            start=False, stop=True), inc=ct)
                        k = (fh * NC2 + g2) % 4
                        P.op("scalar", lambda s, hp=hp, k=k, e=e, fh=fh: s.activation(
                            hT_sb[k][:, :], hp, AF.Relu, bias=b1_sb[:, 2 * e + fh:2 * e + fh + 1]),
                            inc=cs, waits=[(ct, ct.n)] + free.get(("hT", k), []))
                        free[("hps", j)] = [(cs, cs.n)]
                        P.op("scalar", lambda s, k=k, e=e, fh=fh, g2=g2: s.dma_start(
                            hb[g2 * 4:(g2 + 1) * 4, :, 2 * e + fh, :].rearrange("a p n -> p a n"),
                            hT_sb[k][:, :].rearrange("p (ti n) -> p ti n", n=128)),
                            inc=dctr(f"hbw{k}"), k=16, waits=[(cs, cs.n)])
                        free[("hT", k)] = dmadone(f"hbw{k}")
                free[("w1", i)] = [(ct, ct.n)]
            hbdone = (dmadone("hbw0") + dmadone("hbw1") + dmadone("hbw2")
                      + dmadone("hbw3") + [(ct, ct.n)])
            sm.close()
            sx.close()

            se = ExitStack()
            Me = se.enter_context
            w2_sb = Me(nc.sbuf_tensor("w2_sb", [128, 2 * E0 * D], BF16, side="right"))
            b2_sb = Me(nc.sbuf_tensor("b2_sb", [E0, D], BF16, side="right"))
            hall = [Me(nc.sbuf_tensor(f"hall{j}", [128, 2 * E0 * 128], BF16, side="right"))
                    for j in range(2)]
            w2lds = []
            for e in range(E0):
                w2lds.append(emit_wload(
                    [(w2_sb[:, (e * 2 + fh) * D:(e * 2 + fh + 1) * D],
                      dram["l0_w2"][e][fh * 128:(fh + 1) * 128, :], D)
                     for fh in range(2)],
                    waits=(hbdone if e == 0 else ()),
                    dst_free=(hbdone if e == 0 else ()),
                    cast_eng="scalar"))
            P.op("gpsimd", lambda g: g.dma_start(b2_sb[:, :], dram["l0_b2"][:, :]),
                 inc=dctr("b2"), k=16, waits=hbdone)
            w2rdy = dmadone("b2")
            def load_hall(g):
                j = g % 2
                P.op("sync", lambda s, g=g, j=j: s.dma_start(
                    hall[j][:, :], hb[g].rearrange("p k n -> p (k n)")),
                    inc=dctr(f"hall{j}"), k=16,
                    waits=(hbdone if g < 2 else []) + free.get(("hall", j), []))
                return dmadone(f"hall{j}")

            hrdys = {0: load_hall(0)}
            for g in range(NG):
                i = g % 2
                if g + 1 < NG:
                    hrdys[g + 1] = load_hall(g + 1)
                hrdy = hrdys[g]
                yp = pool[0] if i == 0 else pool[1]
                pw = free.get(("yps", i), [])
                for half in range(2):
                    for k in range(2 * E0):
                        kw = w2lds[k // 2] if (k % 2 == 0 and half == 0) else []
                        P.op("tensor", lambda t, yp=yp, half=half, k=k, i=i: t.matmul(
                            yp[:, half * 512:(half + 1) * 512],
                            hall[i][:, k * 128:(k + 1) * 128],
                            w2_sb[:, k * D + half * 512: k * D + (half + 1) * 512],
                            start=(k == 0), stop=False),
                            waits=(hrdy + w2rdy + pw + kw) if (half == 0 and k == 0) else kw)
                    P.op("tensor", lambda t, yp=yp, half=half, g=g: t.matmul(
                        yp[:, half * 512:(half + 1) * 512],
                        mT_sb[:, g * 128:(g + 1) * 128],
                        b2_sb[:, half * 512:(half + 1) * 512],
                        start=False, stop=True),
                        inc=(ct if half == 1 else None))
                free[("hall", i)] = [(ct, ct.n)]
                mmdone = [(ct, ct.n)]
                b, t0 = divmod(g, NT)
                P.op("sync", lambda s, i=i, b=b, t0=t0: s.dma_start(
                    xres[i][:, :], x_mid[b, t0 * 128:(t0 + 1) * 128, :]),
                    inc=dctr(f"xres{i}"), k=16, waits=free.get(("xres", i), []))
                xrdy2 = dmadone(f"xres{i}")
                for half in range(2):
                    P.op("vector", lambda v, i=i, yp=yp, half=half: v.scalar_tensor_tensor(
                        res_sb[i][:, half * 512:(half + 1) * 512],
                        yp[:, half * 512:(half + 1) * 512], 0.0,
                        xres[i][:, half * 512:(half + 1) * 512],
                        ALU.max, ALU.add),
                        inc=(cv if half == 1 else None),
                        waits=(mmdone + xrdy2 + free.get(("ressb", i), [])) if half == 0 else ())
                free[("yps", i)] = [(cv, cv.n)]
                free[("xres", i)] = [(cv, cv.n)]
                P.op("sync", lambda s, i=i, b=b, t0=t0: s.dma_start(
                    x_mid[b, t0 * 128:(t0 + 1) * 128, :], res_sb[i][:, :]),
                    inc=dctr(f"res{i}"), k=16, waits=[(cv, cv.n)])
                free[("ressb", i)] = dmadone(f"res{i}")
                if 1 in layers:
                    b6, m2, rs, xo = bn6[i], mvb[i], rsd[i], xn_sb[i]
                    P.op("vector", lambda v, b6=b6, i=i: v.bn_stats(
                        b6[:, 0:6], res_sb[i][:, 0:512]),
                        waits=[(cv, cv.n)] + free.get(("bn", i), []))
                    P.op("vector", lambda v, b6=b6, i=i: v.bn_stats(
                        b6[:, 6:12], res_sb[i][:, 512:1024]), inc=cv)
                    P.op("vector", lambda v, b6=b6, m2=m2: v.bn_aggr(
                        m2[:, :], b6[:, :]), inc=cv,
                        waits=[(cv, cv.n)] + free.get(("mv", i), []))
                    free[("bn", i)] = [(cv, cv.n)]
                    P.op("scalar", lambda s, m2=m2, rs=rs: s.activation(
                        rs[:, 0:1], m2[:, 1:2], AF.Ln, bias=cst[:, 0:1]), inc=cs,
                        waits=[(cv, cv.n)] + free.get(("rstd", i), []))
                    P.op("scalar", lambda s, rs=rs: s.activation(
                        rs[:, 1:2], rs[:, 0:1], AF.Exp, scale=cst[:, 1:2]), inc=cs,
                        waits=[(cs, cs.n)])
                    P.op("vector", lambda v, xo=xo, i=i, m2=m2, rs=rs: v.tensor_scalar(
                        xo[:, :], res_sb[i][:, :], m2[:, 0:1], rs[:, 1:2],
                        ALU.subtract, ALU.mult),
                        inc=cv, waits=[(cs, cs.n)] + free.get(("xn", i), []))
                    free[("mv", i)] = [(cv, cv.n), (cs, cs.n)]
                    free[("rstd", i)] = [(cv, cv.n), (cs, cs.n)]
                    free[("ressb", i)] = dmadone(f"res{i}") + [(cv, cv.n)]
                    P.op("sync", lambda s, xo=xo, b=b, t0=t0: s.dma_start(
                        xn_hbm[b, t0 * 128:(t0 + 1) * 128, :], xo[:, :]),
                        inc=dctr(f"xnst{i}"), k=16, waits=[(cv, cv.n)])
                    free[("xn", i)] = dmadone(f"xnst{i}")
            se.close()

        if 1 in layers:
            if 0 in layers:
                sx1 = ExitStack()
                xnT1 = sx1.enter_context(
                    nc.sbuf_tensor("xnT1", [128, NI * XNT_B], BF16, side="right"))
                stored1 = [(ct, ct.n)] + dmadone("xnst0") + dmadone("xnst1")
                vb_ready = []
                for tb in range(NB):
                    tpc = dctr(f"tp{tb}f")
                    for it in range(NI):
                        off = it * XNT_B + tb * T
                        P.op("sync", lambda s, it=it, tb=tb, off=off: s.dma_start_transpose(
                            xnT1[:, off:off + T], xn_hbm[tb, :, it * 128:(it + 1) * 128]),
                            inc=tpc, k=16, waits=stored1)
                    vb_ready.append([(tpc, tpc.n)])
                xnT_ready1 = vb_ready[0] + vb_ready[1]
            else:
                sx1 = ExitStack()
                xnT1 = sx1.enter_context(
                    nc.sbuf_tensor("xnT1", [128, NI * XNT_B], BF16, side="right"))
                xnT_ready1 = emit_layernorm(x_mid, xnT1)
                vb_ready = [xnT_ready1, xnT_ready1]
            ln1_done = xnT_ready1 + [(ct, ct.n), (cv, cv.n), (cs, cs.n)]
            sln.close()

            sga = ExitStack()
            G = sga.enter_context
            vext1 = G(nc.sbuf_tensor("vext1", [128, NT * H * 66], BF16, side="right"))
            catT = G(nc.sbuf_tensor("catT", [128, NI * T], BF16, side="right"))
            wbuf = G(nc.sbuf_tensor("wbuf", [128, NI * D], BF16, side="right"))
            wqs = [wbuf[:, i * 1024:(i + 1) * 1024] for i in range(2)]
            wks = [wbuf[:, 2048 + i * 1024:2048 + (i + 1) * 1024] for i in range(2)]
            qtp = [G(nc.sbuf_tensor(f"qtp{i}", [128, T], BF16, side="right")) for i in range(2)]
            ktp = [G(nc.sbuf_tensor(f"ktp{i}", [128, T], BF16, side="right")) for i in range(2)]
            probs = [G(nc.sbuf_tensor(f"probs{i}", [128, 512], BF16, side="right")) for i in range(2)]
            ysb = [G(nc.sbuf_tensor("ysb0", [128, 512], BF16, side="right"))] * 2
            tl1 = [G(nc.sbuf_tensor("tl1_0", [1, 512], F32, side="right"))] * 2
            rr1 = [G(nc.sbuf_tensor("rr1_0", [1, 512], BF16, side="right"))] * 2
            abias_sb = G(nc.sbuf_tensor("abias_sb", [128, H * 16], F32, side="right"))
            abias2_sb = G(nc.sbuf_tensor("abias2_sb", [128, 96], F32, side="right"))
            abiasw_sb = G(nc.sbuf_tensor("abiasw_sb", [128, 440], F32, side="right"))
            maskw_sb = G(nc.sbuf_tensor("maskw_sb", [128, 896], BF16, side="right"))
            maskm_sb = G(nc.sbuf_tensor("maskm_sb", [128, 128], BF16, side="right"))
            bqT_sb = G(nc.sbuf_tensor("bqT_sb", [128, NI], F32, side="right"))
            bkT_sb = G(nc.sbuf_tensor("bkT_sb", [128, NI], F32, side="right"))
            bvrow_sb = G(nc.sbuf_tensor("bvrow_sb", [1, D], BF16, side="right"))
            borow_sb = G(nc.sbuf_tensor("borow_sb", [1, D], BF16, side="right"))
            P.op("gpsimd", lambda g: g.dma_start(abias_sb[:, :], dram["c_abias"][:, :]),
                 inc=dctr("l1cst"), k=16, waits=ln1_done)
            P.op("gpsimd", lambda g: g.dma_start(maskm_sb[:, :], dram["c_maskm"][:, :]),
                 inc=dctr("l1cst"), k=16)
            P.op("gpsimd", lambda g: g.dma_start(abias2_sb[:, :], dram["c_abias2"][:, :]),
                 inc=dctr("l1cst"), k=16)
            P.op("gpsimd", lambda g: g.dma_start(abiasw_sb[:, :], dram["c_abiasw"][:, :]),
                 inc=dctr("l1cst"), k=16)
            P.op("gpsimd", lambda g: g.dma_start(maskw_sb[:, :], dram["c_maskw"][:, :]),
                 inc=dctr("l1cst"), k=16)
            P.op("gpsimd", lambda g: g.dma_start(bqT_sb[:, :], dram["l1_bqT"][:, :]),
                 inc=dctr("l1cst"), k=16)
            P.op("gpsimd", lambda g: g.dma_start(bkT_sb[:, :], dram["l1_bkT"][:, :]),
                 inc=dctr("l1cst"), k=16)
            P.op("gpsimd", lambda g: g.dma_start(bvrow_sb[:, :], dram["l1_bv_row"][:, :]),
                 inc=dctr("l1cst"), k=16)
            P.op("gpsimd", lambda g: g.dma_start(borow_sb[:, :], dram["l1_bo_row"][:, :]),
                 inc=dctr("l1cst"), k=16)
            c1ld = dmadone("l1cst")
            P.op("vector", lambda v: v.memset(
                vext1[:, :].rearrange("p (t h c) -> p t h c", h=H, c=66)[:, :, :, 64:66], 1.0),
                inc=cv, waits=ln1_done)
            WOFF = [0, 4, 12, 24]

            for b in range(NB):
                for it in range(NI):
                    wvrdy = emit_wload(
                        [(wbuf[:, it * D:(it + 1) * D],
                          dram["l1_wvT"][it * 128:(it + 1) * 128, :], D)],
                        dst_free=(free.get("wbuf", c1ld) if it == 0 else ()))
                for ti in range(NT):
                    g0 = b * NT + ti
                    vps = pool[2]
                    pw = free.get("vps1", [])
                    for half in range(2):
                        for it in range(NI):
                            P.op("tensor", lambda t, half=half, it=it, g0=g0: t.matmul(
                                vps[:, half * 512:(half + 1) * 512],
                                xnT1[:, xnt_col(it, g0): xnt_col(it, g0) + 128],
                                wbuf[:, it * D + half * 512: it * D + (half + 1) * 512],
                                start=(it == 0), stop=False),
                                waits=(wvrdy + pw + vb_ready[b]) if (half == 0 and it == 0) else ())
                        P.op("tensor", lambda t, half=half: t.matmul(
                            vps[:, half * 512:(half + 1) * 512], ones_sb[:, :],
                            bvrow_sb[:, half * 512:(half + 1) * 512],
                            start=False, stop=True),
                            inc=(ct if half == 1 else None))
                    P.op("vector", lambda v, ti=ti: v.tensor_copy(
                        vext1[:, ti * 1056:ti * 1056 + 528].rearrange("p (h c) -> p h c", c=66)[:, :, 0:64],
                        pool[2][:, 0:512].rearrange("p (h c) -> p h c", c=64)),
                        waits=[(ct, ct.n)] + free.get("vext1", []))
                    P.op("vector", lambda v, ti=ti: v.tensor_copy(
                        vext1[:, ti * 1056 + 528:ti * 1056 + 1056].rearrange("p (h c) -> p h c", c=66)[:, :, 0:64],
                        pool[2][:, 512:1024].rearrange("p (h c) -> p h c", c=64)),
                        inc=cv)
                    free["vps1"] = [(cv, cv.n)]
                vdone1 = [(cv, cv.n)]

                for p in range(8):
                    i = p % 2
                    wqw = free.get(("wqs", i), []) + ([(ct, ct.n)] if p < 2 else [])
                    wrdy = emit_wload(
                        [(wqs[i].rearrange("p (i o) -> p i o", o=128),
                          dram["l1_wqT"][:, p * 128:(p + 1) * 128].rearrange("(i p) o -> p i o", p=128), 1024),
                         (wks[i].rearrange("p (i o) -> p i o", o=128),
                          dram["l1_wkT"][:, p * 128:(p + 1) * 128].rearrange("(i p) o -> p i o", p=128), 1024)],
                        dst_free=wqw)
                    for c in range(4):
                        ps = slot(c % 2)
                        for it in range(NI):
                            P.op("tensor", lambda t, ps=ps, it=it, i=i, c=c, b=b: t.matmul(
                                ps, wqs[i].rearrange("p (i o) -> p i o", o=128)[:, it, :],
                                xnT1[:, it * XNT_B + b * T + c * 512: it * XNT_B + b * T + (c + 1) * 512],
                                start=(it == 0), stop=(it == NI - 1)),
                                inc=(ct if it == NI - 1 else None),
                                waits=(wrdy + vdone1 + free.get(("ps01a", c % 2), [])) if it == 0 else ())
                        P.op("vector", lambda v, ps=ps, i=i, c=c, p=p: v.tensor_scalar(
                            qtp[i][:, c * 512:(c + 1) * 512], ps,
                            bqT_sb[:, p:p + 1], 0.125, ALU.add, ALU.mult),
                            inc=cv, waits=[(ct, ct.n)] + free.get(("qtp", i), []))
                        free[("ps01a", c % 2)] = [(cv, cv.n)]
                    for c in range(4):
                        ps = slot(c % 2)
                        for it in range(NI):
                            P.op("tensor", lambda t, ps=ps, it=it, i=i, c=c, b=b: t.matmul(
                                ps, wks[i].rearrange("p (i o) -> p i o", o=128)[:, it, :],
                                xnT1[:, it * XNT_B + b * T + c * 512: it * XNT_B + b * T + (c + 1) * 512],
                                start=(it == 0), stop=(it == NI - 1)),
                                inc=(ct if it == NI - 1 else None),
                                waits=free.get(("ps01a", c % 2), []) if it == 0 else ())
                        P.op("scalar", lambda s, ps=ps, i=i, c=c, p=p: s.add(
                            ktp[i][:, c * 512:(c + 1) * 512], ps,
                            bkT_sb[:, p:p + 1]),
                            inc=cs, waits=[(ct, ct.n)] + free.get(("ktp", i), []))
                        free[("ps01a", c % 2)] = [(cs, cs.n)]
                    prdone = [(cv, cv.n), (cs, cs.n)]
                    free[("wqs", i)] = [(ct, ct.n)]
                    for hh in range(2):
                        h = 2 * p + hh
                        hb0 = hh * 64
                        pend_div_s = []
                        pend_div_t = []
                        divstate = {}

                        def emit_div_s(w, hh=hh):
                            ysl = pool[2] if w % 2 == 0 else pool[0]
                            nsrc = (ysl[64:65, 0:512] if hh == 0
                                    else ysl[0:1, 512:1024])
                            P.op("scalar", lambda s, nsrc=nsrc: s.activation(
                                tl1[0][:, :], nsrc, AF.Ln), inc=cs,
                                waits=free.get("ydone", []) + free.get("tl1f", []))
                            P.op("scalar", lambda s: s.activation(
                                rr1[0][:, :], tl1[0][:, :], AF.Exp, scale=cst[0:1, 3:4]),
                                inc=cs, waits=[(cs, cs.n)] + free.get("rr1f", []))
                            free["tl1f"] = [(cs, cs.n)]
                            divstate[w] = [(cs, cs.n)]

                        def emit_div(w, hh=hh, hb0=hb0, p=p, i=i):
                            ysl = pool[2] if w % 2 == 0 else pool[0]
                            rdone1 = divstate.pop(w)
                            rbp = pool[3][hb0:hb0 + 64, 0:512]
                            P.op("tensor", lambda t, rbp=rbp: t.matmul(
                                rbp, ones_sb[0:1, 0:64], rr1[0][:, :],
                                start=True, stop=True), inc=ct,
                                waits=rdone1 + free.get("rbps1", []))
                            free["rr1f"] = [(ct, ct.n)]
                            ysrc = ysl[hb0:hb0 + 64, 0:512]
                            P.op("vector", lambda v, ysrc=ysrc, hb0=hb0: v.tensor_copy(
                                ysb[0][hb0:hb0 + 64, :], ysrc), inc=cv,
                                waits=[(ct, ct.n)] + free.get("ysbf", []))
                            free[("yps", w % 2)] = [(cv, cv.n)]
                            P.op("vector", lambda v, hb0=hb0, p=p, w=w, rbp=rbp: v.tensor_tensor(
                                catT[hb0:hb0 + 64, p * T + w * 512:p * T + (w + 1) * 512],
                                ysb[0][hb0:hb0 + 64, :], rbp, ALU.mult),
                                inc=cv, waits=[(cv, cv.n)])
                            free["ysbf"] = [(cv, cv.n)]
                            free["rbps1"] = [(cv, cv.n)]

                        for w in range(4):
                            nkt = 4 * w + 4
                            ysl = pool[2] if w % 2 == 0 else pool[0]
                            tierC = h >= 5
                            pdone = {}

                            def emit_se(kt, w=w, hb0=hb0, h=h, i=i, tierC=tierC, pdone=pdone):
                                sc = slot(2 + kt % 2)
                                P.op("tensor", lambda t, sc=sc, kt=kt, w=w, hb0=hb0, i=i: t.matmul(
                                    sc, ktp[i][hb0:hb0 + 64, kt * 128:(kt + 1) * 128],
                                    qtp[i][hb0:hb0 + 64, w * 512:(w + 1) * 512],
                                    start=True, stop=True),
                                    inc=ct,
                                    waits=prdone + free.get(("scps", kt % 2), []))
                                scdep = [(ct, ct.n)]
                                pfree = free.get(("probs", kt % 2), [])
                                pb = probs[kt % 2]
                                cvd = False
                                if tierC:
                                    if kt >= 4 * w:
                                        j = kt - 4 * w
                                        P.op("vector", lambda v, sc=sc, j=j: v.tensor_tensor(
                                            sc, sc, maskw_sb[:, (3 - j) * 128:(3 - j) * 128 + 512],
                                            ALU.add), inc=cv, waits=scdep)
                                        scdep = [(cv, cv.n)]
                                        cvd = True
                                    col = (h - 5) * 40 + WOFF[w] + kt
                                    P.op("scalar", lambda s, sc=sc, pb=pb, col=col: s.activation(
                                        pb[:, :], sc, AF.Exp,
                                        bias=abiasw_sb[:, col:col + 1]),
                                        inc=cs, waits=scdep + pfree)
                                else:
                                    steep = h < 3
                                    for qt in range(4):
                                        qtg = 4 * w + qt
                                        if kt > qtg:
                                            P.op("vector", lambda v, pb=pb, qt=qt: v.memset(
                                                pb[:, qt * 128:(qt + 1) * 128], 0.0),
                                                inc=cv, waits=scdep + pfree)
                                            cvd = True
                                            continue
                                        if kt == qtg:
                                            P.op("vector", lambda v, sc=sc, qt=qt: v.tensor_tensor(
                                                sc[:, qt * 128:(qt + 1) * 128],
                                                sc[:, qt * 128:(qt + 1) * 128],
                                                maskm_sb[:, :], ALU.add),
                                                inc=cv, waits=scdep)
                                            scdep2 = [(cv, cv.n)]
                                            cvd = True
                                        else:
                                            scdep2 = scdep
                                        if steep:
                                            for q6 in range(2):
                                                jj = (8 * w + 2 * qt + q6) - 2 * kt
                                                c0 = qt * 128 + q6 * 64
                                                if jj < 0:
                                                    P.op("vector", lambda v, pb=pb, c0=c0: v.memset(
                                                        pb[:, c0:c0 + 64], 0.0),
                                                        inc=cv, waits=scdep2 + pfree)
                                                    cvd = True
                                                    continue
                                                col = h * 32 + jj
                                                P.op("scalar", lambda s, sc=sc, pb=pb, c0=c0, col=col: s.activation(
                                                    pb[:, c0:c0 + 64],
                                                    sc[:, c0:c0 + 64], AF.Exp,
                                                    bias=abias2_sb[:, col:col + 1]),
                                                    inc=cs, waits=scdep2 + pfree)
                                        else:
                                            col = h * 16 + (qtg - kt)
                                            P.op("scalar", lambda s, sc=sc, pb=pb, qt=qt, col=col: s.activation(
                                                pb[:, qt * 128:(qt + 1) * 128],
                                                sc[:, qt * 128:(qt + 1) * 128], AF.Exp,
                                                bias=abias_sb[:, col:col + 1]),
                                                inc=cs, waits=scdep2 + pfree)
                                free[("scps", kt % 2)] = [(cs, cs.n)]
                                pdone[kt] = [(cs, cs.n)] + ([(cv, cv.n)] if cvd else [])

                            emit_se(0)
                            for kt in range(nkt):
                                if kt + 1 < nkt:
                                    emit_se(kt + 1)
                                elif pend_div_t:
                                    pend_div_t.pop(0)()
                                if kt == 0 and pend_div_s:
                                    pend_div_s.pop(0)()
                                w0 = (pdone[kt] + prdone + free.get(("yps", w % 2), [])
                                      ) if kt == 0 else pdone[kt]
                                if hh == 0:
                                    P.op("tensor", lambda t, kt=kt, h=h, w=w, nkt=nkt, ysl=ysl: t.matmul(
                                        ysl[0:65, 0:512],
                                        vext1[:, kt * 1056 + h * 66: kt * 1056 + h * 66 + 65],
                                        probs[kt % 2][:, :],
                                        start=(kt == 0), stop=(kt == nkt - 1)),
                                        inc=ct, waits=w0)
                                else:
                                    P.op("tensor", lambda t, kt=kt, h=h, w=w, nkt=nkt, ysl=ysl: t.matmul(
                                        ysl[64:128, 0:512],
                                        vext1[:, kt * 1056 + h * 66: kt * 1056 + h * 66 + 64],
                                        probs[kt % 2][:, :],
                                        start=(kt == 0), stop=(kt == nkt - 1)),
                                        waits=w0)
                                    P.op("tensor", lambda t, kt=kt, h=h, w=w, nkt=nkt, ysl=ysl: t.matmul(
                                        ysl[0:1, 512:1024],
                                        vext1[:, kt * 1056 + h * 66 + 64: kt * 1056 + h * 66 + 65],
                                        probs[kt % 2][:, :],
                                        start=(kt == 0), stop=(kt == nkt - 1)),
                                        inc=ct)
                                free[("probs", kt % 2)] = [(ct, ct.n)]
                            free["ydone"] = [(ct, ct.n)]
                            pend_div_s.append(lambda w=w: emit_div_s(w))
                            pend_div_t.append(lambda w=w: emit_div(w))
                        while pend_div_s:
                            pend_div_s.pop(0)()
                        while pend_div_t:
                            pend_div_t.pop(0)()
                    free[("ps01a", 0)] = [(cv, cv.n), (cs, cs.n)]
                    free[("ps01a", 1)] = [(cv, cv.n), (cs, cs.n)]
                    free[("qtp", i)] = [(ct, ct.n)]
                    free[("ktp", i)] = [(ct, ct.n)]
                catdone = [(cv, cv.n)]
                free["vext1"] = [(ct, ct.n)]

                hd_done = [(ct, ct.n)]
                for it in range(NI):
                    wordy = emit_wload(
                        [(wbuf[:, it * D:(it + 1) * D],
                          dram["l1_woT"][it * 128:(it + 1) * 128, :], D)],
                        dst_free=(hd_done if it == 0 else ()))
                for ti in range(NT):
                    i = ti % 2
                    yp = pool[1]
                    P.op("sync", lambda s, i=i, b=b, ti=ti: s.dma_start(
                        xres[i][:, :], x_mid[b, ti * 128:(ti + 1) * 128, :]),
                        inc=dctr(f"xres{i}"), k=16, waits=free.get(("xres", i), []))
                    xrdy = dmadone(f"xres{i}")
                    pw = free.get("wops1", [])
                    for half in range(2):
                        for it in range(NI):
                            P.op("tensor", lambda t, yp=yp, half=half, it=it, ti=ti: t.matmul(
                                yp[:, half * 512:(half + 1) * 512],
                                catT[:, it * T + ti * 128: it * T + (ti + 1) * 128],
                                wbuf[:, it * D + half * 512: it * D + (half + 1) * 512],
                                start=(it == 0), stop=False),
                                waits=(catdone + pw + wordy) if (half == 0 and it == 0) else ())
                        P.op("tensor", lambda t, yp=yp, half=half: t.matmul(
                            yp[:, half * 512:(half + 1) * 512], ones_sb[:, :],
                            borow_sb[:, half * 512:(half + 1) * 512],
                            start=False, stop=True),
                            inc=(ct if half == 1 else None))
                    for half in range(2):
                        P.op("vector", lambda v, i=i, half=half, yp=yp: v.tensor_tensor(
                            res_sb[i][:, half * 512:(half + 1) * 512],
                            yp[:, half * 512:(half + 1) * 512],
                            xres[i][:, half * 512:(half + 1) * 512], ALU.add),
                            inc=(cv if half == 1 else None),
                            waits=([(ct, ct.n)] + xrdy + free.get(("ressb", i), [])) if half == 0 else ())
                    free[("xres", i)] = [(cv, cv.n)]
                    free["wops1"] = [(cv, cv.n)]
                    P.op("sync", lambda s, i=i, b=b, ti=ti: s.dma_start(
                        x_att[b, ti * 128:(ti + 1) * 128, :], res_sb[i][:, :]),
                        inc=dctr(f"res{i}"), k=16, waits=[(cv, cv.n)])
                    free[("ressb", i)] = dmadone(f"res{i}")
                free["wbuf"] = [(ct, ct.n)]
            attn1_done = [(ct, ct.n), (cv, cv.n), (cs, cs.n)]
            sga.close()

            sf = ExitStack()
            FT = sf.enter_context
            NFT = DFF // 128
            hT1 = FT(nc.sbuf_tensor("hT1", [128, NFT * 512], BF16, side="right"))
            w1c = [FT(nc.sbuf_tensor(f"w1c{i}", [128, NI * 128], BF16, side="right")) for i in range(4)]
            w3c = [FT(nc.sbuf_tensor(f"w3c{i}", [128, NI * 128], BF16, side="right")) for i in range(4)]
            w2c = [FT(nc.sbuf_tensor(f"w2c{i}", [128, D], BF16, side="right")) for i in range(4)]
            gsb = [FT(nc.sbuf_tensor(f"gsb{i}", [128, 512], BF16, side="right")) for i in range(4)]
            wstf = [FT(nc.sbuf_tensor(f"wstf{i}", [128, 1024], F32, side="right")) for i in range(4)]
            wsts = [FT(nc.sbuf_tensor(f"wsts{i}", [128, 1024], F32, side="right")) for i in range(4)]
            for i in range(4):
                free[("wstf", i)] = attn1_done
                free[("wsts", i)] = attn1_done
            xnv = xnT1[:, :].rearrange("p (i b t) -> p i b t", b=NB, t=T)
            def load_ft(e, ft):
                i = (e * NFT + ft) % 4
                df = (attn1_done if (e == 0 and ft < 4) else [])
                r1 = emit_wload(
                    [(w1c[i][:, :].rearrange("p (i f) -> p i f", f=128),
                      dram["l1_w1"][e][:, ft * 128:(ft + 1) * 128].rearrange("(i p) f -> p i f", p=128), 1024)],
                    dst_free=df + free.get(("w1c", i), []),
                    ring=wstf, rkey="wstf", queue="sync")
                r3 = emit_wload(
                    [(w3c[i][:, :].rearrange("p (i f) -> p i f", f=128),
                      dram["l1_w3"][e][:, ft * 128:(ft + 1) * 128].rearrange("(i p) f -> p i f", p=128), 1024)],
                    dst_free=df + free.get(("w3c", i), []),
                    ring=wsts, rkey="wsts", queue="scalar")
                return (r1, r3)

            wrdys = {}
            for e in range(E1):
                if e == 0:
                    wrdys[0] = load_ft(0, 0)
                    wrdys[1] = load_ft(0, 1)
                for ft in range(NFT):
                    i = (e * NFT + ft) % 4
                    if ft + 2 < NFT:
                        wrdys[ft + 2] = load_ft(e, ft + 2)
                    elif e + 1 < E1:
                        wrdys[ft + 2 - NFT] = load_ft(e + 1, ft + 2 - NFT)
                    wrdy1, wrdy3 = wrdys[ft]
                    par = ft % 2
                    gp = slot(0 + 2 * par)
                    up = slot(1 + 2 * par)
                    for it in range(NI):
                        P.op("tensor", lambda t, gp=gp, it=it, i=i, e=e: t.matmul(
                            gp, w1c[i][:, it * 128:(it + 1) * 128],
                            xnv[:, it, :, e * 256:(e + 1) * 256],
                            start=(it == 0), stop=(it == NI - 1)),
                            inc=(ct if it == NI - 1 else None),
                            waits=(wrdy1 + free.get(("gup", par), [])) if it == 0 else ())
                    for it in range(NI):
                        P.op("tensor", lambda t, up=up, it=it, i=i, e=e: t.matmul(
                            up, w3c[i][:, it * 128:(it + 1) * 128],
                            xnv[:, it, :, e * 256:(e + 1) * 256],
                            start=(it == 0), stop=(it == NI - 1)),
                            inc=(ct if it == NI - 1 else None),
                            waits=wrdy3 if it == 0 else ())
                    free[("w1c", i)] = [(ct, ct.n)]
                    free[("w3c", i)] = [(ct, ct.n)]
                    P.op("scalar", lambda s, gp=gp, i=i: s.activation(
                        gsb[i][:, :], gp, AF.Silu), inc=cs,
                        waits=[(ct, ct.n)] + free.get(("gsb", i), []))
                    P.op("vector", lambda v, up=up, i=i, ft=ft: v.tensor_tensor(
                        hT1[:, ft * 512:(ft + 1) * 512], gsb[i][:, :], up, ALU.mult),
                        inc=cv, waits=[(cs, cs.n)] + free.get(("hT1", ft), []))
                    free[("gsb", i)] = [(cv, cv.n)]
                    free[("gup", par)] = [(cv, cv.n)]
                hdone1 = [(cv, cv.n)]
                ypools = [pool[0], pool[1], pool[2], pool[3]]
                pwy = free.get("ypsf", [])
                w2rdys = {}

                def load_w2ft(ft2):
                    ring, rk, qu = ((wstf, "wstf", "sync") if ft2 % 2 == 0
                                    else (wsts, "wsts", "scalar"))
                    return emit_wload(
                        [(w2c[ft2 % 4][:, :],
                          dram["l1_w2"][e][ft2 * 128:(ft2 + 1) * 128, :], D)],
                        dst_free=free.get(("w2c", ft2 % 4), []),
                        ring=ring, rkey=rk, queue=qu)

                w2rdys[0] = load_w2ft(0)
                w2rdys[1] = load_w2ft(1)
                xrdy3s = {}
                for sl in range(2):
                    i, b, st2 = sl % 2, sl // 2, sl % 2
                    ti = e * 2 + st2
                    P.op("sync", lambda s, i=i, b=b, ti=ti: s.dma_start(
                        xres[i][:, :], x_att[b, ti * 128:(ti + 1) * 128, :]),
                        inc=dctr(f"xres{i}"), k=16, waits=free.get(("xres", i), []))
                    xrdy3s[sl] = dmadone(f"xres{i}")
                for ft in range(NFT):
                    i = ft % 4
                    if ft + 2 < NFT:
                        w2rdys[ft + 2] = load_w2ft(ft + 2)
                    w2rdy1 = w2rdys[ft]
                    for sl in range(4):
                        for half in range(2):
                            P.op("tensor", lambda t, sl=sl, half=half, ft=ft, i=i, e=e: t.matmul(
                                ypools[sl][:, half * 512:(half + 1) * 512],
                                hT1[:, ft * 512 + sl * 128: ft * 512 + (sl + 1) * 128],
                                w2c[i][:, half * 512:(half + 1) * 512],
                                start=(ft == 0), stop=(ft == NFT - 1)),
                                inc=(ct if (sl == 3 and half == 1) else None),
                                waits=(w2rdy1 + ((hdone1 + pwy) if ft == 0 else [])) if (sl == 0 and half == 0) else ())
                    free[("w2c", i)] = [(ct, ct.n)]
                free.update({("hT1", ft): [(ct, ct.n)] for ft in range(NFT)})
                ydone1 = [(ct, ct.n)]
                for sl in range(4):
                    i = sl % 2
                    b, st2 = sl // 2, sl % 2
                    ti = e * 2 + st2
                    xrdy3 = xrdy3s[sl]
                    for half in range(2):
                        P.op("vector", lambda v, i=i, half=half, sl=sl: v.tensor_tensor(
                            res_sb[i][:, half * 512:(half + 1) * 512],
                            ypools[sl][:, half * 512:(half + 1) * 512],
                            xres[i][:, half * 512:(half + 1) * 512], ALU.add),
                            inc=(cv if half == 1 else None),
                            waits=(ydone1 + xrdy3 + free.get(("ressb", i), [])) if half == 0 else ())
                    free[("xres", i)] = [(cv, cv.n)]
                    if sl < 2:
                        free[("gup", sl)] = [(cv, cv.n)]
                    if sl + 2 < 4:
                        sl2 = sl + 2
                        i2, b2_, st22 = sl2 % 2, sl2 // 2, sl2 % 2
                        ti2 = e * 2 + st22
                        P.op("sync", lambda s, i2=i2, b2_=b2_, ti2=ti2: s.dma_start(
                            xres[i2][:, :], x_att[b2_, ti2 * 128:(ti2 + 1) * 128, :]),
                            inc=dctr(f"xres{i2}"), k=16, waits=free.get(("xres", i2), []))
                        xrdy3s[sl2] = dmadone(f"xres{i2}")
                    P.op("sync", lambda s, i=i, b=b, ti=ti: s.dma_start(
                        y_out[b, ti * 128:(ti + 1) * 128, :], res_sb[i][:, :]),
                        inc=dctr(f"res{i}"), k=16, waits=[(cv, cv.n)])
                    free[("ressb", i)] = dmadone(f"res{i}")
                free["ypsf"] = [(cv, cv.n)]
            sf.close()
            sx1.close()

        if 1 not in layers:
            sln.close()
        finals = [(c.h, c.n) for c in ALLC]
        P.raw("sync", lambda e, fl=finals: [e.wait_ge(h, n) for h, n in fl if n > 0])
        P.realize(nc)
    return nc


def _emit_layer1_body(*a, **k):
    raise NotImplementedError("layer 1 not yet implemented")


FUSED = True


def kernel(**inputs):
    W = _host_prep(inputs)
    x = np.ascontiguousarray(np.asarray(inputs["L0_attn_x"], np.float32))
    shards = [x[2 * c:2 * c + 2] for c in range(NCORES)]

    l0_keys = ["l0_wqkv", "l0_wf", "l0_bf_row", "l0_bf_col", "l0_wout",
               "l0_bout_row", "l0_rwT", "l0_rb_row", "l0_w1", "l0_b1",
               "l0_w2", "l0_b2", "c_pen"]
    l1_keys = ["l1_wqT", "l1_wkT", "l1_wvT", "l1_woT", "l1_bqT", "l1_bkT",
               "l1_bv_row", "l1_bo_row", "l1_w1", "l1_w3", "l1_w2",
               "c_abias", "c_abias2", "c_abiasw", "c_maskm", "c_maskw"]

    if FUSED:
        nc = build_program((0, 1))
        maps = [dict({k: W[k] for k in l0_keys + l1_keys},
                     c_ident=W["c_ident"], x0=shards[c]) for c in range(NCORES)]
        res = run_bass_kernel_spmd(nc, maps, core_ids=list(range(NCORES)))
        return np.concatenate([np.asarray(r["y"]).reshape(NB, T, D) for r in res.results], axis=0)
    nc0 = build_program((0,))
    maps = [dict({k: W[k] for k in l0_keys}, c_ident=W["c_ident"], x0=shards[c])
            for c in range(NCORES)]
    res0 = run_bass_kernel_spmd(nc0, maps, core_ids=list(range(NCORES)))
    xmid = [np.ascontiguousarray(np.asarray(r["x_mid"]).reshape(NB, T, D)) for r in res0.results]
    nc1 = build_program((1,))
    maps = [dict({k: W[k] for k in l1_keys}, c_ident=W["c_ident"], x_mid=xmid[c])
            for c in range(NCORES)]
    res1 = run_bass_kernel_spmd(nc1, maps, core_ids=list(range(NCORES)))
    return np.concatenate([np.asarray(r["y"]).reshape(NB, T, D) for r in res1.results], axis=0)


# version 14
# speedup vs baseline: 1.0401x; 1.0178x over previous
import math
from contextlib import ExitStack

import numpy as np
import ml_dtypes

import concourse.bass as bass
import concourse.mybir as mybir
from concourse.bass_utils import run_bass_kernel_spmd

F32 = mybir.dt.float32
BF16 = mybir.dt.bfloat16
AF = mybir.ActivationFunctionType
ALU = mybir.AluOpType
AX = mybir.AxisListType
BFD = ml_dtypes.bfloat16

NCORES = 8
NB = 2
T = 2048
D = 1024
H = 16
DH = 64
NT = T // 128
NI = D // 128
NG = NB * NT
NC2 = NB * NT // 4
E0, H0 = 32, 256
E1, DFF = 8, 2816
BIG = 1.0e30
SM_SHIFT = 16.0
LN_EPS = 1e-5
XNT_B = NB * T


def _alibi_slopes(n_heads):
    start = 2 ** (-(2 ** (-(math.log2(n_heads) - 3))))
    return [start * start**i for i in range(n_heads)]


def _host_prep(inputs):
    f32 = np.float32
    c = np.ascontiguousarray
    W = {}
    W["l0_wqkv"] = c(np.asarray(inputs["L0_attn_W_qkv"], f32))
    W["l0_wf"] = c(np.asarray(inputs["L0_attn_W_f"], f32))
    bf = np.asarray(inputs["L0_attn_b_f"], f32)
    W["l0_bf_row"] = c(np.tile(bf, H)[None, :])
    W["l0_bf_col"] = c(np.concatenate([bf, bf])[:, None])
    W["l0_wout"] = c(np.asarray(inputs["L0_attn_W_out"], f32))
    W["l0_bout_row"] = c(np.asarray(inputs["L0_attn_b_out"], f32)[None, :])
    W["l0_rwT"] = c(np.asarray(inputs["L0_ffn_router_w"], f32).T)
    W["l0_rb_row"] = c(np.asarray(inputs["L0_ffn_router_b"], f32)[None, :])
    W["l0_w1"] = c(np.asarray(inputs["L0_ffn_weights1"], f32))
    W["l0_b1"] = c(np.asarray(inputs["L0_ffn_bias1"], f32)
                   .reshape(E0, 2, 128).transpose(2, 0, 1).reshape(128, 2 * E0))
    W["l0_w2"] = c(np.asarray(inputs["L0_ffn_weights2"], f32))
    W["l0_b2"] = c(np.asarray(inputs["L0_ffn_bias2"], f32))
    for nm, k in (("q", "Wq"), ("k", "Wk"), ("v", "Wv"), ("o", "Wo")):
        W[f"l1_w{nm}T"] = c(np.asarray(inputs[f"L1_attn_{k}"], f32).T)
    W["l1_bqT"] = c(np.asarray(inputs["L1_attn_bq"], f32).reshape(NI, 128).T)
    W["l1_bkT"] = c(np.asarray(inputs["L1_attn_bk"], f32).reshape(NI, 128).T)
    W["l1_bv_row"] = c(np.asarray(inputs["L1_attn_bv"], f32)[None, :])
    W["l1_bo_row"] = c(np.asarray(inputs["L1_attn_bo"], f32)[None, :])
    W["l1_w1"] = c(np.asarray(inputs["L1_ffn_w1"], f32))
    W["l1_w3"] = c(np.asarray(inputs["L1_ffn_w3"], f32))
    W["l1_w2"] = c(np.asarray(inputs["L1_ffn_w2"], f32))

    W["c_ident"] = np.eye(128, dtype=f32)
    slopes = _alibi_slopes(H)
    abias = np.zeros((128, H * 16), np.float64)
    p = np.arange(128)
    for h in range(H):
        for m2 in range(16):
            abias[:, h * 16 + m2] = slopes[h] * (p - m2 * 128 - 64)
    W["c_abias"] = abias.astype(f32)
    abias2 = np.zeros((128, 3 * 32), np.float64)
    for h in range(3):
        for j in range(32):
            abias2[:, h * 32 + j] = slopes[h] * (p - 64 * j - 32)
    W["c_abias2"] = abias2.astype(f32)
    woff = [0, 4, 12, 24]
    abw = np.zeros((128, 11 * 40), np.float64)
    for h in range(5, H):
        for w in range(4):
            for kt in range(4 * w + 4):
                abw[:, (h - 5) * 40 + woff[w] + kt] = slopes[h] * (kt * 128 + p - (w * 512 + 384))
    W["c_abiasw"] = abw.astype(f32)
    pen = np.full((E0, E0, 128), -BIG, f32)
    for e in range(E0):
        pen[e, e, :] = 0.0
    W["c_pen"] = np.ascontiguousarray(pen.reshape(E0, E0 * 128)).astype(BFD)
    kk = np.arange(128)[:, None]
    qr = np.arange(128)[None, :]
    tri = np.where(kk > qr, -BIG, 0.0).astype(f32)
    W["c_maskm"] = tri.astype(BFD)
    mw = np.zeros((128, 896), f32)
    mw[:, 0:384] = -BIG
    mw[:, 384:512] = tri
    W["c_maskw"] = mw.astype(BFD)
    return W


class Ctr:
    __slots__ = ("h", "n", "name")

    def __init__(self, stack, nc, name):
        self.h = stack.enter_context(nc.semaphore(name))
        self.n = 0
        self.name = name


class Prog:
    ENGS = ("tensor", "vector", "scalar", "gpsimd", "sync")

    def __init__(self):
        self.q = {e: [] for e in self.ENGS}
        self.last_wait = {e: {} for e in self.ENGS}

    def op(self, eng, fn, inc=None, k=1, waits=()):
        lw = self.last_wait[eng]
        wl = []
        for ctr, v in waits:
            if v <= 0 or lw.get(ctr.name, -1) >= v:
                continue
            lw[ctr.name] = v
            wl.append((ctr.h, v))
        tgt = None
        if inc is not None:
            inc.n += k
            tgt = (inc.h, k)

        def emit(e, wl=wl, fn=fn, tgt=tgt):
            for h, v in wl:
                e.wait_ge(h, v)
            ins = fn(e)
            if tgt is not None:
                ins.then_inc(tgt[0], tgt[1])
        self.q[eng].append(emit)

    def raw(self, eng, fn):
        self.q[eng].append(fn)



    def realize(self, nc):
        with nc.Block() as blk:
            for eng in self.ENGS:
                ops = self.q[eng]
                if not ops:
                    continue
                def make(ops):
                    def run(e):
                        for f in ops:
                            f(e)
                    return run
                getattr(blk, eng)(make(ops))


def build_program(layers=(0, 1)):
    nc = bass.Bass("TRN2", target_bir_lowering=False)
    P = Prog()
    dram = {}

    def din(name, shape, dt=F32):
        dram[name] = nc.dram_tensor(name, list(shape), dt, kind="ExternalInput").ap()
        return dram[name]

    def dint(name, shape, dt=F32):
        dram[name] = nc.dram_tensor(name, list(shape), dt, kind="Internal").ap()
        return dram[name]

    def dout(name, shape, dt=F32):
        dram[name] = nc.dram_tensor(name, list(shape), dt, kind="ExternalOutput").ap()
        return dram[name]

    if 0 in layers:
        x0 = din("x0", (NB, T, D))
        din("l0_wqkv", (D, 3 * D)); din("l0_wf", (DH, DH))
        din("l0_bf_row", (1, D)); din("l0_bf_col", (128, 1))
        din("l0_wout", (D, D)); din("l0_bout_row", (1, D))
        din("l0_rwT", (D, E0)); din("l0_rb_row", (1, E0))
        din("l0_w1", (E0, D, H0)); din("l0_b1", (128, 2 * E0))
        din("l0_w2", (E0, H0, D)); din("l0_b2", (E0, D))
        din("c_pen", (E0, E0 * 128), BF16)
        hb = dint("hbounce", (NG, 128, 2 * E0, 128), BF16)
    if 1 in layers:
        for nm in "qkvo":
            din(f"l1_w{nm}T", (D, D))
        din("l1_bqT", (128, NI)); din("l1_bkT", (128, NI))
        din("l1_bv_row", (1, D)); din("l1_bo_row", (1, D))
        din("l1_w1", (E1, D, DFF)); din("l1_w3", (E1, D, DFF))
        din("l1_w2", (E1, DFF, D))
        din("c_abias", (128, H * 16)); din("c_abias2", (128, 96))
        din("c_abiasw", (128, 440))
        din("c_maskm", (128, 128), BF16); din("c_maskw", (128, 896), BF16)
    din("c_ident", (128, 128))

    if layers == (0,):
        x_mid = dout("x_mid", (NB, T, D))
    elif 0 in layers:
        x_mid = dint("x_mid", (NB, T, D))
    else:
        x_mid = din("x_mid", (NB, T, D))
    if 1 in layers:
        y_out = dout("y", (NB, T, D))
        x_att = dint("x_att", (NB, T, D))
    xn_hbm = dint("xn_hbm", (NB, T, D), BF16)

    st = ExitStack()
    with st:
        E = st.enter_context
        ct = Ctr(st, nc, "ct")
        cv = Ctr(st, nc, "cv")
        cs = Ctr(st, nc, "cs")
        cg = Ctr(st, nc, "cg")
        ch = Ctr(st, nc, "ch")
        cq = Ctr(st, nc, "cq")
        ALLC = [ct, cv, cs, cg, ch, cq]
        dctrs = {}

        def dctr(name):
            if name not in dctrs:
                dctrs[name] = Ctr(st, nc, "d_" + name)
                ALLC.append(dctrs[name])
            return dctrs[name]

        def dmadone(name):
            c = dctr(name)
            return [(c, c.n)]

        def allw():
            return [(c, c.n) for c in ALLC]

        pool = [E(nc.psum_tensor(f"ps{i}", [128, 1024], F32)) for i in range(4)]

        def slot(i, rows=128, cols=512, r0=0):
            return pool[i // 2][r0:r0 + rows, (i % 2) * 512:(i % 2) * 512 + cols]

        ident_sb = E(nc.sbuf_tensor("ident_sb", [128, 128], F32))
        ones_sb = E(nc.sbuf_tensor("ones_sb", [1, 128], BF16))
        P.op("gpsimd", lambda g: g.dma_start(ident_sb[:, :], dram["c_ident"][:, :]),
             inc=cg, k=16)
        P.op("vector", lambda v: v.memset(ones_sb[:, :], 1.0), inc=cv)
        cst = E(nc.sbuf_tensor("cst", [128, 4], F32))
        P.op("vector", lambda v: v.memset(cst[:, 0:1], LN_EPS))
        P.op("vector", lambda v: v.memset(cst[:, 1:2], -0.5))
        P.op("vector", lambda v: v.memset(cst[:, 2:3], 1e-6))
        P.op("vector", lambda v: v.memset(cst[:, 3:4], -1.0), inc=cv)
        wst = [E(nc.sbuf_tensor(f"wst{i}", [128, 1024], F32)) for i in range(2)]
        wsti = {}

        def emit_wload(pairs, waits=(), dst_free=(), ring=None, rkey="wst", queue="scalar",
                       cast_eng="vector"):
            cc = cv if cast_eng == "vector" else cs
            bufs = wst if ring is None else ring
            nb = len(bufs)
            groups, cur, acc = [], [], 0
            for pr in pairs:
                if acc + pr[2] > 1024:
                    groups.append(cur); cur, acc = [], 0
                cur.append(pr); acc += pr[2]
            if cur:
                groups.append(cur)
            firstall = True
            for grp in groups:
                i = wsti.get(rkey, 0) % nb
                wsti[rkey] = wsti.get(rkey, 0) + 1
                wc = dctr(f"{rkey}{i}")
                off = 0
                first = True
                for dst, srcap, cols in grp:
                    w = list(free.get((rkey, i), []))
                    if firstall and first:
                        w = list(waits) + w
                    P.op(queue, lambda s, i=i, off=off, srcap=srcap, cols=cols: s.dma_start(
                        bufs[i][:, off:off + cols], srcap), inc=wc, k=16,
                        waits=(w if first else ()))
                    first = False
                    off += cols
                ms = [(wc, wc.n)]
                off = 0
                first = True
                for dst, srcap, cols in grp:
                    w = (ms + (list(dst_free) if firstall else [])) if first else ()
                    if cast_eng == "vector":
                        P.op("vector", lambda v, i=i, off=off, dst=dst, cols=cols: v.tensor_copy(
                            dst, bufs[i][:, off:off + cols]), inc=cc, waits=w)
                    else:
                        P.op("scalar", lambda s, i=i, off=off, dst=dst, cols=cols: s.copy(
                            dst, bufs[i][:, off:off + cols]), inc=cc, waits=w)
                    first = False
                    off += cols
                free[(rkey, i)] = [(cc, cc.n)]
                firstall = False
            return [(cc, cc.n)]

        xres = [E(nc.sbuf_tensor(f"xres{i}", [128, D], F32)) for i in range(2)]
        res_sb = [E(nc.sbuf_tensor(f"res{i}", [128, D], F32)) for i in range(2)]
        sln = ExitStack()
        L = sln.enter_context
        xin = [L(nc.sbuf_tensor(f"xin{i}", [128, D], F32)) for i in range(2)]
        xn_sb = [L(nc.sbuf_tensor(f"xnsb{i}", [128, D], BF16)) for i in range(2)]
        bn6 = [L(nc.sbuf_tensor(f"bn6_{i}", [128, 12], F32)) for i in range(2)]
        mvb = [L(nc.sbuf_tensor(f"mv{i}", [128, 2], F32)) for i in range(2)]
        rsd = [L(nc.sbuf_tensor(f"rstd{i}", [128, 2], F32)) for i in range(2)]

        free = {}

        def emit_layernorm(src_dram, xnT):
            base = allw()
            for i in range(2):
                for key in (("xin", i), ("bn", i), ("mv", i), ("rstd", i), ("xn", i)):
                    free.pop(key, None)
            for g in range(NG):
                b, t = divmod(g, NT)
                i = g % 2
                xi, xo, b6, m2, rs = xin[i], xn_sb[i], bn6[i], mvb[i], rsd[i]
                P.op("sync", lambda s, xi=xi, b=b, t=t: s.dma_start(
                    xi[:, :], src_dram[b, t * 128:(t + 1) * 128, :]),
                    inc=dctr(f"xin{i}"), k=16, waits=free.get(("xin", i), base))
                ld = dmadone(f"xin{i}")
                P.op("vector", lambda v, b6=b6, xi=xi: v.bn_stats(b6[:, 0:6], xi[:, 0:512]),
                     waits=ld + free.get(("bn", i), base))
                P.op("vector", lambda v, b6=b6, xi=xi: v.bn_stats(b6[:, 6:12], xi[:, 512:1024]),
                     inc=cv)
                P.op("vector", lambda v, b6=b6, m2=m2: v.bn_aggr(m2[:, :], b6[:, :]),
                     inc=cv, waits=[(cv, cv.n)] + free.get(("mv", i), base))
                free[("bn", i)] = [(cv, cv.n)]
                P.op("scalar", lambda s, m2=m2, rs=rs: s.activation(
                    rs[:, 0:1], m2[:, 1:2], AF.Ln, bias=cst[:, 0:1]), inc=cs,
                    waits=[(cv, cv.n)] + free.get(("rstd", i), base))
                P.op("scalar", lambda s, rs=rs: s.activation(
                    rs[:, 1:2], rs[:, 0:1], AF.Exp, scale=cst[:, 1:2]), inc=cs,
                    waits=[(cs, cs.n)])
                P.op("vector", lambda v, xo=xo, xi=xi, m2=m2, rs=rs: v.tensor_scalar(
                    xo[:, :], xi[:, :], m2[:, 0:1], rs[:, 1:2], ALU.subtract, ALU.mult),
                    inc=cv, waits=[(cs, cs.n)] + free.get(("xn", i), base))
                free[("xin", i)] = [(cv, cv.n)]
                free[("mv", i)] = [(cv, cv.n), (cs, cs.n)]
                free[("rstd", i)] = [(cv, cv.n), (cs, cs.n)]
                P.op("sync", lambda s, xo=xo, b=b, t=t: s.dma_start(
                    xn_hbm[b, t * 128:(t + 1) * 128, :], xo[:, :]),
                    inc=dctr(f"xnst{i}"), k=16, waits=[(cv, cv.n)])
                free[("xn", i)] = dmadone(f"xnst{i}")
            stored = dmadone("xnst0") + dmadone("xnst1") + base
            for it in range(NI):
                for b in range(NB):
                    off = it * XNT_B + b * T
                    P.op("sync", lambda s, it=it, b=b, off=off: s.dma_start_transpose(
                        xnT[:, off:off + T], xn_hbm[b, :, it * 128:(it + 1) * 128]),
                        inc=dctr("tp"), k=16, waits=stored)
            return dmadone("tp")

        def xnt_col(it, g):
            return it * XNT_B + g * 128

        if 0 in layers:
            sx = ExitStack()
            xnT = sx.enter_context(nc.sbuf_tensor("xnT", [128, NI * XNT_B], BF16, side="right"))
            sa0 = ExitStack()
            A = sa0.enter_context
            wf_sb = A(nc.sbuf_tensor("wf_sb", [128, DH], BF16, side="right"))
            bfrow_sb = A(nc.sbuf_tensor("bfrow_sb", [1, D], BF16, side="right"))
            bfcol_sb = A(nc.sbuf_tensor("bfcol_sb", [128, 1], F32, side="right"))
            boutrow_sb = A(nc.sbuf_tensor("boutrow_sb", [1, D], BF16, side="right"))
            kt_sb = A(nc.sbuf_tensor("kt_sb", [128, 8 * 512], BF16, side="right"))
            fk_sb = [A(nc.sbuf_tensor(f"fk{i}", [128, D], BF16, side="right")) for i in range(2)]
            vext = [A(nc.sbuf_tensor(f"vex0_{i}", [128, H * 66], BF16, side="right")) for i in range(2)]
            kv_sb = [A(nc.sbuf_tensor(f"kv{i}", [128, 8 * 65], BF16, side="right")) for i in range(2)]
            kvacc = A(nc.sbuf_tensor("kvacc", [128, 2 * 260], F32, side="right"))
            qt_sb = [A(nc.sbuf_tensor(f"qt{i}", [128, 512], BF16, side="right")) for i in range(2)]
            fq_sb = [A(nc.sbuf_tensor(f"fq{i}", [128, 512], BF16, side="right")) for i in range(2)]
            oe_sb = [A(nc.sbuf_tensor(f"oe{i}", [128, 512], BF16, side="right")) for i in range(2)]
            tln = [A(nc.sbuf_tensor(f"tln{i}", [1, 512], F32, side="right")) for i in range(2)]
            rrow = [A(nc.sbuf_tensor(f"rrow{i}", [1, 512], BF16, side="right")) for i in range(2)]
            apT = A(nc.sbuf_tensor("apT", [128, NI * T], BF16, side="right"))

            P.op("gpsimd", lambda g: g.dma_start(wf_sb[0:64, :], dram["l0_wf"][:, :]),
                 inc=cg, k=16)
            P.op("gpsimd", lambda g: g.dma_start(wf_sb[64:128, :], dram["l0_wf"][:, :]),
                 inc=cg, k=16)
            P.op("gpsimd", lambda g: g.dma_start(bfrow_sb[:, :], dram["l0_bf_row"][:, :]),
                 inc=cg, k=16)
            P.op("gpsimd", lambda g: g.dma_start(bfcol_sb[:, :], dram["l0_bf_col"][:, :]),
                 inc=cg, k=16)
            P.op("gpsimd", lambda g: g.dma_start(boutrow_sb[:, :], dram["l0_bout_row"][:, :]),
                 inc=cg, k=16)
            for i in range(2):
                P.op("vector", lambda v, i=i: v.memset(
                    vext[i][:, :].rearrange("p (h c) -> p h c", c=66)[:, :, 64:66], 1.0),
                    inc=cv)

            sa1 = ExitStack()
            wk_sb = sa1.enter_context(nc.sbuf_tensor("wk_sb", [128, NI * D], BF16, side="right"))
            wv_sb = sa1.enter_context(nc.sbuf_tensor("wv_sb", [128, NI * D], BF16, side="right"))
            for it in range(NI):
                wload = emit_wload([
                    (wk_sb[:, it * D:(it + 1) * D],
                     dram["l0_wqkv"][it * 128:(it + 1) * 128, D:2 * D], D),
                    (wv_sb[:, it * D:(it + 1) * D],
                     dram["l0_wqkv"][it * 128:(it + 1) * 128, 2 * D:3 * D], D)])

            wload = wload + [(cg, cg.n)]
            xnT_ready = emit_layernorm(x0, xnT)

            for b in range(NB):
                P.op("vector", lambda v: v.memset(kvacc[:, :], 0.0), inc=cv,
                     waits=free.get("kvacc", []))
                for c in range(4):
                    for kt in range(8):
                        ps = slot(kt % 2)
                        pw = free.get(("ps01", kt % 2), [])
                        for it in range(NI):
                            P.op("tensor", lambda t, ps=ps, it=it, kt=kt, c=c, b=b: t.matmul(
                                ps, wk_sb[:, it * D + kt * 128: it * D + (kt + 1) * 128],
                                xnT[:, xnt_col(it, b * NT) + c * 512: xnt_col(it, b * NT) + (c + 1) * 512],
                                start=(it == 0), stop=(it == NI - 1)),
                                inc=(ct if it == NI - 1 else None),
                                waits=(xnT_ready + wload + pw) if it == 0 else ())
                        P.op("scalar", lambda s, ps=ps, kt=kt: s.copy(
                            kt_sb[:, kt * 512:(kt + 1) * 512], ps), inc=cs,
                            waits=[(ct, ct.n)] + free.get(("ktsb", kt), []))
                        free[("ps01", kt % 2)] = [(cs, cs.n)]
                    ktev = [(cs, cs.n)]
                    for ti in range(4):
                        g = b * NT + c * 4 + ti
                        i = ti % 2
                        fkps0 = slot(2, cols=512)
                        fkps1 = slot(3, cols=512)
                        pw = free.get("fkps", [])
                        for h in range(H):
                            ps = fkps0 if h < 8 else fkps1
                            co = (h % 8) * 64
                            hb0 = (h % 2) * 64
                            P.op("tensor", lambda t, ps=ps, co=co, h=h, ti=ti, hb0=hb0: t.matmul(
                                ps[:, co:co + 64],
                                kt_sb[hb0:hb0 + 64,
                                      (h // 2) * 512 + ti * 128:(h // 2) * 512 + (ti + 1) * 128],
                                wf_sb[hb0:hb0 + 64, :], start=True, stop=False),
                                waits=(ktev + pw + wload) if h == 0 else ())
                            P.op("tensor", lambda t, ps=ps, co=co, h=h: t.matmul(
                                ps[:, co:co + 64], ones_sb[:, :],
                                bfrow_sb[:, h * 64:(h + 1) * 64], start=False, stop=True),
                                inc=(ct if h == H - 1 else None))
                        P.op("scalar", lambda s, i=i: s.activation(
                            fk_sb[i][:, :], pool[1][:, :], AF.Relu), inc=cs,
                            waits=[(ct, ct.n)] + free.get(("fksb", i), []))
                        free["fkps"] = [(cs, cs.n)]
                        fkdone = [(cs, cs.n)]
                        vps = pool[2]
                        pw = free.get("vps", [])
                        for half in range(2):
                            for it in range(NI):
                                P.op("tensor", lambda t, vps=vps, half=half, it=it, g=g: t.matmul(
                                    vps[:, half * 512:(half + 1) * 512],
                                    xnT[:, xnt_col(it, g): xnt_col(it, g) + 128],
                                    wv_sb[:, it * D + half * 512: it * D + (half + 1) * 512],
                                    start=(it == 0), stop=(it == NI - 1)),
                                    inc=(ct if it == NI - 1 and half == 1 else None),
                                    waits=pw if (half == 0 and it == 0) else ())
                        P.op("vector", lambda v, i=i: v.tensor_copy(
                            vext[i][:, 0:528].rearrange("p (h c) -> p h c", c=66)[:, :, 0:64],
                            pool[2][:, 0:512].rearrange("p (h c) -> p h c", c=64)),
                            waits=[(ct, ct.n)] + free.get(("vext", i), []))
                        P.op("vector", lambda v, i=i: v.tensor_copy(
                            vext[i][:, 528:1056].rearrange("p (h c) -> p h c", c=66)[:, :, 0:64],
                            pool[2][:, 512:1024].rearrange("p (h c) -> p h c", c=64)),
                            inc=cv)
                        free["vps"] = [(cv, cv.n)]
                        vdone = [(cv, cv.n)]
                        for h in range(H):
                            kvp = pool[3][(h % 2) * 64:(h % 2) * 64 + 64,
                                          (h // 2 % 4) * 65 + (h // 8) * 512:
                                          (h // 2 % 4) * 65 + (h // 8) * 512 + 65]
                            P.op("tensor", lambda t, kvp=kvp, h=h, i=i: t.matmul(
                                kvp, fk_sb[i][:, h * 64:(h + 1) * 64],
                                vext[i][:, h * 66:h * 66 + 65],
                                start=True, stop=True),
                                inc=(ct if h == H - 1 else None),
                                waits=(fkdone + vdone + free.get("kvps", [])) if h == 0 else ())
                        free[("fksb", i)] = [(ct, ct.n)]
                        free[("vext", i)] = [(ct, ct.n)]
                        P.op("vector", lambda v: v.tensor_tensor(
                            kvacc[:, 0:260], kvacc[:, 0:260], pool[3][:, 0:260], ALU.add),
                            waits=[(ct, ct.n), (cv, cv.n)])
                        P.op("vector", lambda v: v.tensor_tensor(
                            kvacc[:, 260:520], kvacc[:, 260:520], pool[3][:, 512:772], ALU.add),
                            inc=cv)
                        free["kvps"] = [(cv, cv.n)]
                    for kt in range(8):
                        free[("ktsb", kt)] = [(ct, ct.n)]
                P.op("vector", lambda v, b=b: v.tensor_copy(
                    kv_sb[b][:, :], kvacc[:, :]),
                    inc=cv, waits=[(cv, cv.n)] + free.get(("kvsb", b), []))
                free["kvacc"] = [(cv, cv.n)]
            kvdone = [(cv, cv.n)]

            sa1.close()
            sa2 = ExitStack()
            wq_sb = sa2.enter_context(nc.sbuf_tensor("wq_sb", [128, NI * D], BF16, side="right"))
            wout_sb = sa2.enter_context(nc.sbuf_tensor("wout_sb", [128, NI * D], BF16, side="right"))
            pa_done = [(ct, ct.n)]
            for it in range(NI):
                wload2 = emit_wload([
                    (wq_sb[:, it * D:(it + 1) * D],
                     dram["l0_wqkv"][it * 128:(it + 1) * 128, 0:D], D),
                    (wout_sb[:, it * D:(it + 1) * D],
                     dram["l0_wout"][it * 128:(it + 1) * 128, :], D)],
                    dst_free=pa_done)

            free[("ps01", 0)] = kvdone
            free[("ps01", 1)] = kvdone
            free["wops"] = kvdone
            pend_op = []
            for b in range(NB):
                for c in range(4):
                    for p in range(8):
                        i = p % 2
                        ps = slot(i)
                        for it in range(NI):
                            P.op("tensor", lambda t, ps=ps, it=it, p=p, c=c, b=b: t.matmul(
                                ps, wq_sb[:, it * D + p * 128: it * D + (p + 1) * 128],
                                xnT[:, xnt_col(it, b * NT) + c * 512: xnt_col(it, b * NT) + (c + 1) * 512],
                                start=(it == 0), stop=(it == NI - 1)),
                                inc=(ct if it == NI - 1 else None),
                                waits=(kvdone + wload2 + free.get(("ps01", i), [])) if it == 0 else ())
                        P.op("scalar", lambda s, ps=ps, i=i: s.copy(qt_sb[i][:, :], ps),
                             inc=cs, waits=[(ct, ct.n)] + free.get(("qtsb", i), []))
                        qtdone = [(cs, cs.n)]
                        fqp = slot(i)
                        P.op("tensor", lambda t, fqp=fqp, i=i: t.matmul(
                            fqp[0:64, :], wf_sb[0:64, :], qt_sb[i][0:64, :],
                            start=True, stop=True),
                            waits=qtdone)
                        P.op("tensor", lambda t, fqp=fqp, i=i: t.matmul(
                            fqp[64:128, :], wf_sb[64:128, :], qt_sb[i][64:128, :],
                            start=True, stop=True), inc=ct)
                        free[("qtsb", i)] = [(ct, ct.n)]
                        P.op("scalar", lambda s, fqp=fqp, i=i: s.activation(
                            fq_sb[i][:, :], fqp, AF.Relu, bias=bfcol_sb[:, 0:1]),
                            inc=cs, waits=[(ct, ct.n)] + free.get(("fqsb", i), []))
                        free[("ps01", i)] = [(cs, cs.n)]
                        fqdone = [(cs, cs.n)]
                        oep = slot(4)
                        n0 = slot(5, rows=1)
                        n1 = slot(6, rows=1)
                        pw = free.get("oeps", [])
                        for hh in range(2):
                            h = 2 * p + hh
                            kcol = (h // 2 % 4) * 65 + (h // 8) * 260
                            kbase = (h % 2) * 64
                            P.op("tensor", lambda t, oep=oep, hh=hh, kcol=kcol, kbase=kbase, i=i, b=b: t.matmul(
                                oep[hh * 64:(hh + 1) * 64, :],
                                kv_sb[b][kbase:kbase + 64, kcol:kcol + 64],
                                fq_sb[i][hh * 64:(hh + 1) * 64, :],
                                start=True, stop=True),
                                waits=(fqdone + pw) if hh == 0 else ())
                            nps = n0 if hh == 0 else n1
                            P.op("tensor", lambda t, nps=nps, hh=hh, kcol=kcol, kbase=kbase, i=i, b=b: t.matmul(
                                nps, kv_sb[b][kbase:kbase + 64, kcol + 64:kcol + 65],
                                fq_sb[i][hh * 64:(hh + 1) * 64, :],
                                start=True, stop=True),
                                inc=(ct if hh == 1 else None))
                        oedone = [(ct, ct.n)]
                        free[("fqsb", i)] = oedone
                        rbp = slot(7)
                        for hh in range(2):
                            nps = n0 if hh == 0 else n1
                            P.op("scalar", lambda s, nps=nps, hh=hh: s.activation(
                                tln[hh][:, :], nps, AF.Ln, bias=cst[0:1, 2:3]), inc=cs,
                                waits=oedone + free.get(("tln", hh), []))
                            P.op("scalar", lambda s, hh=hh: s.activation(
                                rrow[hh][:, :], tln[hh][:, :], AF.Exp, scale=cst[0:1, 3:4]),
                                inc=cs, waits=[(cs, cs.n)] + free.get(("rrow", hh), []))
                            free[("tln", hh)] = [(cs, cs.n)]
                        rdone = [(cs, cs.n)]
                        if pend_op:
                            pend_op.pop(0)()
                        pwrb = free.get("rbps", [])
                        for hh in range(2):
                            P.op("tensor", lambda t, rbp=rbp, hh=hh: t.matmul(
                                rbp[hh * 64:(hh + 1) * 64, :], ones_sb[0:1, 0:64],
                                rrow[hh][:, :], start=True, stop=True),
                                inc=(ct if hh == 1 else None),
                                waits=(rdone + pwrb) if hh == 0 else ())
                        free[("rrow", 0)] = [(ct, ct.n)]
                        free[("rrow", 1)] = [(ct, ct.n)]
                        P.op("scalar", lambda s, oep=oep, i=i: s.copy(oe_sb[i][:, :], oep),
                             inc=cs, waits=[(ct, ct.n)] + free.get(("oesb", i), []))
                        free["oeps"] = [(cs, cs.n)]
                        P.op("vector", lambda v, i=i, p=p, c=c, rbp=rbp: v.tensor_tensor(
                            apT[:, p * T + c * 512:p * T + (c + 1) * 512],
                            oe_sb[i][:, :], rbp[:, :], ALU.mult),
                            inc=cv, waits=[(cs, cs.n)] + free.get(("apT", p, c), []))
                        free[("oesb", i)] = [(cv, cv.n)]
                        free["rbps"] = [(cv, cv.n)]
                    apdone = [(cv, cv.n)]

                    def mk_op0(b, c, ti, apdone):
                        def run():
                            i = ti % 2
                            yp = pool[1]
                            P.op("sync", lambda s: s.dma_start(
                                xres[i][:, :], x0[b, (c * 4 + ti) * 128:(c * 4 + ti + 1) * 128, :]),
                                inc=dctr(f"xres{i}"), k=16, waits=free.get(("xres", i), []))
                            pw = free.get("wops", [])
                            for it in range(NI):
                                P.op("tensor", lambda t, it=it: t.matmul(
                                    yp[:, 0:512],
                                    apT[:, it * T + (c * 4 + ti) * 128: it * T + (c * 4 + ti + 1) * 128],
                                    wout_sb[:, it * D: it * D + 512],
                                    start=(it == 0), stop=False),
                                    waits=(apdone + pw + wload2) if it == 0 else ())
                            P.op("tensor", lambda t: t.matmul(
                                yp[:, 0:512], ones_sb[:, :], boutrow_sb[:, 0:512],
                                start=False, stop=True))
                        return run

                    def mk_op1(b, c, ti, apdone):
                        def run():
                            i = ti % 2
                            yp = pool[1]
                            xrdy = dmadone(f"xres{i}")
                            for it in range(NI):
                                P.op("tensor", lambda t, it=it: t.matmul(
                                    yp[:, 512:1024],
                                    apT[:, it * T + (c * 4 + ti) * 128: it * T + (c * 4 + ti + 1) * 128],
                                    wout_sb[:, it * D + 512: it * D + 1024],
                                    start=(it == 0), stop=False))
                            P.op("tensor", lambda t: t.matmul(
                                yp[:, 512:1024], ones_sb[:, :], boutrow_sb[:, 512:1024],
                                start=False, stop=True), inc=ct)
                            if ti == 3:
                                free.update({("apT", pp, c): [(ct, ct.n)] for pp in range(8)})
                            for half in range(2):
                                P.op("vector", lambda v, half=half: v.tensor_tensor(
                                    res_sb[i][:, half * 512:(half + 1) * 512],
                                    yp[:, half * 512:(half + 1) * 512],
                                    xres[i][:, half * 512:(half + 1) * 512], ALU.add),
                                    inc=(cv if half == 1 else None),
                                    waits=([(ct, ct.n)] + xrdy + free.get(("ressb", i), [])) if half == 0 else ())
                            free[("xres", i)] = [(cv, cv.n)]
                            free["wops"] = [(cv, cv.n)]
                            P.op("sync", lambda s: s.dma_start(
                                x_mid[b, (c * 4 + ti) * 128:(c * 4 + ti + 1) * 128, :],
                                res_sb[i][:, :]), inc=dctr(f"res{i}"), k=16, waits=[(cv, cv.n)])
                            free[("ressb", i)] = dmadone(f"res{i}")
                        return run

                    for ti in range(4):
                        pend_op.append(mk_op0(b, c, ti, apdone))
                        pend_op.append(mk_op1(b, c, ti, apdone))
            while pend_op:
                pend_op.pop(0)()
            attn_done = [(ct, ct.n), (cv, cv.n), (cs, cs.n)]
            sa2.close()
            sa0.close()

            mT_sb = L(nc.sbuf_tensor("mT_sb", [E0, NG * 128], BF16))
            sm = ExitStack()
            M = sm.enter_context
            rw_sb = M(nc.sbuf_tensor("rw_sb", [128, NI * E0], BF16, side="right"))
            rb_sb = M(nc.sbuf_tensor("rb_sb", [1, E0], BF16, side="right"))
            pen_sb = M(nc.sbuf_tensor("pen_sb", [E0, E0 * 128], BF16, side="right"))
            b1_sb = M(nc.sbuf_tensor("b1_sb", [128, 2 * E0], F32, side="right"))
            lmax = [M(nc.sbuf_tensor(f"lmax{i}", [128, 1], F32, side="right")) for i in range(2)]
            mf_sb = [M(nc.sbuf_tensor(f"mf{i}", [128, E0], F32, side="right")) for i in range(2)]
            w1_sb = [M(nc.sbuf_tensor(f"w1sb{i}", [128, NI * H0], BF16, side="right")) for i in range(2)]
            hT_sb = [M(nc.sbuf_tensor(f"hT0_{i}", [128, 512], BF16, side="right")) for i in range(4)]
            w1st = [M(nc.sbuf_tensor(f"w1st{i}", [128, 1024], F32, side="right")) for i in range(2)]
            for i in range(2):
                free[("w1st", i)] = attn_done
                free[("w1", i)] = attn_done
            rw_rdy = emit_wload([(rw_sb[:, :].rearrange("p (i e) -> p i e", e=E0),
                                  dram["l0_rwT"].rearrange("(i p) e -> p i e", p=128), NI * E0)],
                                dst_free=attn_done)
            P.op("gpsimd", lambda g: g.dma_start(rb_sb[:, :], dram["l0_rb_row"][:, :]),
                 inc=dctr("rcst"), k=16, waits=attn_done)
            P.op("gpsimd", lambda g: g.dma_start(pen_sb[:, :], dram["c_pen"][:, :]),
                 inc=dctr("rcst"), k=16)
            P.op("gpsimd", lambda g: g.dma_start(b1_sb[:, :], dram["l0_b1"][:, :]),
                 inc=dctr("rcst"), k=16)
            rld = dmadone("rcst") + rw_rdy
            for g in range(NG):
                i = g % 2
                lp = slot(i, cols=E0)
                for it in range(NI):
                    P.op("tensor", lambda t, lp=lp, it=it, g=g: t.matmul(
                        lp, xnT[:, xnt_col(it, g): xnt_col(it, g) + 128],
                        rw_sb[:, it * E0:(it + 1) * E0],
                        start=(it == 0), stop=False),
                        waits=(rld + free.get(("lps", i), [])) if it == 0 else ())
                P.op("tensor", lambda t, lp=lp: t.matmul(
                    lp, ones_sb[:, :], rb_sb[:, :], start=False, stop=True), inc=ct)
                P.op("vector", lambda v, lp=lp, i=i: v.reduce_max(
                    lmax[i][:, :], lp, axis=AX.X), inc=cv,
                    waits=[(ct, ct.n)] + free.get(("lmax", i), []))
                P.op("vector", lambda v, lp=lp, i=i: v.tensor_single_scalar(
                    mf_sb[i][:, :], lp, lmax[i][:, :], ALU.is_equal), inc=cv,
                    waits=[(cv, cv.n)] + free.get(("mf", i), []))
                free[("lps", i)] = [(cv, cv.n)]
                free[("lmax", i)] = [(cv, cv.n)]
                mtp = slot(2, rows=E0, cols=128)
                P.op("tensor", lambda t, mtp=mtp, i=i: t.transpose(
                    mtp, mf_sb[i][:, :], ident_sb[:, :]), inc=ct,
                    waits=[(cv, cv.n)] + free.get("mtp", []))
                free[("mf", i)] = [(ct, ct.n)]
                P.op("vector", lambda v, mtp=mtp, g=g: v.tensor_copy(
                    mT_sb[:, g * 128:(g + 1) * 128], mtp), inc=cv,
                    waits=[(ct, ct.n)])
                free["mtp"] = [(cv, cv.n)]
            mtdone = [(cv, cv.n)]

            def load_w1(e):
                i = e % 2
                return emit_wload(
                    [(w1_sb[i][:, a * 1024:(a + 1) * 1024].rearrange("p (i f) -> p i f", f=H0),
                      dram["l0_w1"][e][a * 512:(a + 1) * 512, :].rearrange("(i p) f -> p i f", p=128), 1024)
                     for a in range(2)],
                    dst_free=free.get(("w1", i), []), ring=w1st, rkey="w1st", queue="sync")

            w1rdys = {0: load_w1(0)}
            for e in range(E0):
                i = e % 2
                if e + 1 < E0:
                    w1rdys[e + 1] = load_w1(e + 1)
                w1rdy = w1rdys[e]
                for fh in range(2):
                    for g2 in range(NC2):
                        j = (fh * NC2 + g2) % 2
                        hp = slot(j)
                        for it in range(NI):
                            P.op("tensor", lambda t, hp=hp, it=it, i=i, fh=fh, g2=g2: t.matmul(
                                hp, w1_sb[i][:, it * H0 + fh * 128: it * H0 + fh * 128 + 128],
                                xnT[:, it * XNT_B + g2 * 512: it * XNT_B + (g2 + 1) * 512],
                                start=(it == 0), stop=False),
                                waits=(w1rdy + mtdone + free.get(("hps", j), [])) if it == 0 else ())
                        P.op("tensor", lambda t, hp=hp, e=e, g2=g2: t.matmul(
                            hp, pen_sb[:, e * 128:(e + 1) * 128],
                            mT_sb[:, g2 * 512:(g2 + 1) * 512],
                            start=False, stop=True), inc=ct)
                        k = (fh * NC2 + g2) % 4
                        P.op("scalar", lambda s, hp=hp, k=k, e=e, fh=fh: s.activation(
                            hT_sb[k][:, :], hp, AF.Relu, bias=b1_sb[:, 2 * e + fh:2 * e + fh + 1]),
                            inc=cs, waits=[(ct, ct.n)] + free.get(("hT", k), []))
                        free[("hps", j)] = [(cs, cs.n)]
                        P.op("scalar", lambda s, k=k, e=e, fh=fh, g2=g2: s.dma_start(
                            hb[g2 * 4:(g2 + 1) * 4, :, 2 * e + fh, :].rearrange("a p n -> p a n"),
                            hT_sb[k][:, :].rearrange("p (ti n) -> p ti n", n=128)),
                            inc=dctr(f"hbw{k}"), k=16, waits=[(cs, cs.n)])
                        free[("hT", k)] = dmadone(f"hbw{k}")
                free[("w1", i)] = [(ct, ct.n)]
            hbdone = (dmadone("hbw0") + dmadone("hbw1") + dmadone("hbw2")
                      + dmadone("hbw3") + [(ct, ct.n)])
            sm.close()
            sx.close()

            se = ExitStack()
            Me = se.enter_context
            w2_sb = Me(nc.sbuf_tensor("w2_sb", [128, 2 * E0 * D], BF16, side="right"))
            b2_sb = Me(nc.sbuf_tensor("b2_sb", [E0, D], BF16, side="right"))
            hall = [Me(nc.sbuf_tensor(f"hall{j}", [128, 2 * E0 * 128], BF16, side="right"))
                    for j in range(2)]
            w2lds = []
            for e in range(E0):
                w2lds.append(emit_wload(
                    [(w2_sb[:, (e * 2 + fh) * D:(e * 2 + fh + 1) * D],
                      dram["l0_w2"][e][fh * 128:(fh + 1) * 128, :], D)
                     for fh in range(2)],
                    waits=(hbdone if e == 0 else ()),
                    dst_free=(hbdone if e == 0 else ()),
                    cast_eng="scalar"))
            P.op("gpsimd", lambda g: g.dma_start(b2_sb[:, :], dram["l0_b2"][:, :]),
                 inc=dctr("b2"), k=16, waits=hbdone)
            w2rdy = dmadone("b2")
            def load_hall(g):
                j = g % 2
                P.op("sync", lambda s, g=g, j=j: s.dma_start(
                    hall[j][:, :], hb[g].rearrange("p k n -> p (k n)")),
                    inc=dctr(f"hall{j}"), k=16,
                    waits=(hbdone if g < 2 else []) + free.get(("hall", j), []))
                return dmadone(f"hall{j}")

            hrdys = {0: load_hall(0)}
            for g in range(NG):
                i = g % 2
                if g + 1 < NG:
                    hrdys[g + 1] = load_hall(g + 1)
                hrdy = hrdys[g]
                yp = pool[0] if i == 0 else pool[1]
                pw = free.get(("yps", i), [])
                for half in range(2):
                    for k in range(2 * E0):
                        kw = w2lds[k // 2] if (k % 2 == 0 and half == 0) else []
                        P.op("tensor", lambda t, yp=yp, half=half, k=k, i=i: t.matmul(
                            yp[:, half * 512:(half + 1) * 512],
                            hall[i][:, k * 128:(k + 1) * 128],
                            w2_sb[:, k * D + half * 512: k * D + (half + 1) * 512],
                            start=(k == 0), stop=False),
                            waits=(hrdy + w2rdy + pw + kw) if (half == 0 and k == 0) else kw)
                    P.op("tensor", lambda t, yp=yp, half=half, g=g: t.matmul(
                        yp[:, half * 512:(half + 1) * 512],
                        mT_sb[:, g * 128:(g + 1) * 128],
                        b2_sb[:, half * 512:(half + 1) * 512],
                        start=False, stop=True),
                        inc=(ct if half == 1 else None))
                free[("hall", i)] = [(ct, ct.n)]
                mmdone = [(ct, ct.n)]
                b, t0 = divmod(g, NT)
                P.op("sync", lambda s, i=i, b=b, t0=t0: s.dma_start(
                    xres[i][:, :], x_mid[b, t0 * 128:(t0 + 1) * 128, :]),
                    inc=dctr(f"xres{i}"), k=16, waits=free.get(("xres", i), []))
                xrdy2 = dmadone(f"xres{i}")
                for half in range(2):
                    P.op("vector", lambda v, i=i, yp=yp, half=half: v.scalar_tensor_tensor(
                        res_sb[i][:, half * 512:(half + 1) * 512],
                        yp[:, half * 512:(half + 1) * 512], 0.0,
                        xres[i][:, half * 512:(half + 1) * 512],
                        ALU.max, ALU.add),
                        inc=(cv if half == 1 else None),
                        waits=(mmdone + xrdy2 + free.get(("ressb", i), [])) if half == 0 else ())
                free[("yps", i)] = [(cv, cv.n)]
                free[("xres", i)] = [(cv, cv.n)]
                P.op("sync", lambda s, i=i, b=b, t0=t0: s.dma_start(
                    x_mid[b, t0 * 128:(t0 + 1) * 128, :], res_sb[i][:, :]),
                    inc=dctr(f"res{i}"), k=16, waits=[(cv, cv.n)])
                free[("ressb", i)] = dmadone(f"res{i}")
                if 1 in layers:
                    b6, m2, rs, xo = bn6[i], mvb[i], rsd[i], xn_sb[i]
                    P.op("vector", lambda v, b6=b6, i=i: v.bn_stats(
                        b6[:, 0:6], res_sb[i][:, 0:512]),
                        waits=[(cv, cv.n)] + free.get(("bn", i), []))
                    P.op("vector", lambda v, b6=b6, i=i: v.bn_stats(
                        b6[:, 6:12], res_sb[i][:, 512:1024]), inc=cv)
                    P.op("vector", lambda v, b6=b6, m2=m2: v.bn_aggr(
                        m2[:, :], b6[:, :]), inc=cv,
                        waits=[(cv, cv.n)] + free.get(("mv", i), []))
                    free[("bn", i)] = [(cv, cv.n)]
                    P.op("scalar", lambda s, m2=m2, rs=rs: s.activation(
                        rs[:, 0:1], m2[:, 1:2], AF.Ln, bias=cst[:, 0:1]), inc=cs,
                        waits=[(cv, cv.n)] + free.get(("rstd", i), []))
                    P.op("scalar", lambda s, rs=rs: s.activation(
                        rs[:, 1:2], rs[:, 0:1], AF.Exp, scale=cst[:, 1:2]), inc=cs,
                        waits=[(cs, cs.n)])
                    P.op("vector", lambda v, xo=xo, i=i, m2=m2, rs=rs: v.tensor_scalar(
                        xo[:, :], res_sb[i][:, :], m2[:, 0:1], rs[:, 1:2],
                        ALU.subtract, ALU.mult),
                        inc=cv, waits=[(cs, cs.n)] + free.get(("xn", i), []))
                    free[("mv", i)] = [(cv, cv.n), (cs, cs.n)]
                    free[("rstd", i)] = [(cv, cv.n), (cs, cs.n)]
                    free[("ressb", i)] = dmadone(f"res{i}") + [(cv, cv.n)]
                    P.op("sync", lambda s, xo=xo, b=b, t0=t0: s.dma_start(
                        xn_hbm[b, t0 * 128:(t0 + 1) * 128, :], xo[:, :]),
                        inc=dctr(f"xnst{i}"), k=16, waits=[(cv, cv.n)])
                    free[("xn", i)] = dmadone(f"xnst{i}")
            se.close()

        if 1 in layers:
            if 0 in layers:
                sx1 = ExitStack()
                xnT1 = sx1.enter_context(
                    nc.sbuf_tensor("xnT1", [128, NI * XNT_B], BF16, side="right"))
                stored1 = [(ct, ct.n)] + dmadone("xnst0") + dmadone("xnst1")
                vb_ready = []
                for tb in range(NB):
                    tpc = dctr(f"tp{tb}f")
                    for it in range(NI):
                        off = it * XNT_B + tb * T
                        P.op("sync", lambda s, it=it, tb=tb, off=off: s.dma_start_transpose(
                            xnT1[:, off:off + T], xn_hbm[tb, :, it * 128:(it + 1) * 128]),
                            inc=tpc, k=16, waits=stored1)
                    vb_ready.append([(tpc, tpc.n)])
                xnT_ready1 = vb_ready[0] + vb_ready[1]
            else:
                sx1 = ExitStack()
                xnT1 = sx1.enter_context(
                    nc.sbuf_tensor("xnT1", [128, NI * XNT_B], BF16, side="right"))
                xnT_ready1 = emit_layernorm(x_mid, xnT1)
                vb_ready = [xnT_ready1, xnT_ready1]
            ln1_done = xnT_ready1 + [(ct, ct.n), (cv, cv.n), (cs, cs.n)]
            sln.close()

            sga = ExitStack()
            G = sga.enter_context
            vext1 = G(nc.sbuf_tensor("vext1", [128, NT * H * 66], BF16, side="right"))
            catT = G(nc.sbuf_tensor("catT", [128, NI * T], BF16, side="right"))
            wbuf = G(nc.sbuf_tensor("wbuf", [128, NI * D], BF16, side="right"))
            wqs = [wbuf[:, i * 1024:(i + 1) * 1024] for i in range(2)]
            wks = [wbuf[:, 2048 + i * 1024:2048 + (i + 1) * 1024] for i in range(2)]
            qtp = [G(nc.sbuf_tensor(f"qtp{i}", [128, T], BF16, side="right")) for i in range(2)]
            ktp = [G(nc.sbuf_tensor(f"ktp{i}", [128, T], BF16, side="right")) for i in range(2)]
            probs = [G(nc.sbuf_tensor(f"probs{i}", [128, 512], BF16, side="right")) for i in range(2)]
            ysb = [G(nc.sbuf_tensor("ysb0", [128, 512], BF16, side="right"))] * 2
            tl1 = [G(nc.sbuf_tensor("tl1_0", [1, 512], F32, side="right"))] * 2
            rr1 = [G(nc.sbuf_tensor("rr1_0", [1, 512], BF16, side="right"))] * 2
            abias_sb = G(nc.sbuf_tensor("abias_sb", [128, H * 16], F32, side="right"))
            abias2_sb = G(nc.sbuf_tensor("abias2_sb", [128, 96], F32, side="right"))
            abiasw_sb = G(nc.sbuf_tensor("abiasw_sb", [128, 440], F32, side="right"))
            maskw_sb = G(nc.sbuf_tensor("maskw_sb", [128, 896], BF16, side="right"))
            maskm_sb = G(nc.sbuf_tensor("maskm_sb", [128, 128], BF16, side="right"))
            bqT_sb = G(nc.sbuf_tensor("bqT_sb", [128, NI], F32, side="right"))
            bkT_sb = G(nc.sbuf_tensor("bkT_sb", [128, NI], F32, side="right"))
            bvrow_sb = G(nc.sbuf_tensor("bvrow_sb", [1, D], BF16, side="right"))
            borow_sb = G(nc.sbuf_tensor("borow_sb", [1, D], BF16, side="right"))
            P.op("gpsimd", lambda g: g.dma_start(abias_sb[:, :], dram["c_abias"][:, :]),
                 inc=dctr("l1cst"), k=16, waits=ln1_done)
            P.op("gpsimd", lambda g: g.dma_start(maskm_sb[:, :], dram["c_maskm"][:, :]),
                 inc=dctr("l1cst"), k=16)
            P.op("gpsimd", lambda g: g.dma_start(abias2_sb[:, :], dram["c_abias2"][:, :]),
                 inc=dctr("l1cst"), k=16)
            P.op("gpsimd", lambda g: g.dma_start(abiasw_sb[:, :], dram["c_abiasw"][:, :]),
                 inc=dctr("l1cst"), k=16)
            P.op("gpsimd", lambda g: g.dma_start(maskw_sb[:, :], dram["c_maskw"][:, :]),
                 inc=dctr("l1cst"), k=16)
            P.op("gpsimd", lambda g: g.dma_start(bqT_sb[:, :], dram["l1_bqT"][:, :]),
                 inc=dctr("l1cst"), k=16)
            P.op("gpsimd", lambda g: g.dma_start(bkT_sb[:, :], dram["l1_bkT"][:, :]),
                 inc=dctr("l1cst"), k=16)
            P.op("gpsimd", lambda g: g.dma_start(bvrow_sb[:, :], dram["l1_bv_row"][:, :]),
                 inc=dctr("l1cst"), k=16)
            P.op("gpsimd", lambda g: g.dma_start(borow_sb[:, :], dram["l1_bo_row"][:, :]),
                 inc=dctr("l1cst"), k=16)
            c1ld = dmadone("l1cst")
            P.op("vector", lambda v: v.memset(
                vext1[:, :].rearrange("p (t h c) -> p t h c", h=H, c=66)[:, :, :, 64:66], 1.0),
                inc=cv, waits=ln1_done)
            WOFF = [0, 4, 12, 24]

            for b in range(NB):
                for it in range(NI):
                    wvrdy = emit_wload(
                        [(wbuf[:, it * D:(it + 1) * D],
                          dram["l1_wvT"][it * 128:(it + 1) * 128, :], D)],
                        dst_free=(free.get("wbuf", c1ld) if it == 0 else ()))
                for ti in range(NT):
                    g0 = b * NT + ti
                    vps = pool[2] if ti % 2 == 0 else pool[3]
                    pw = free.get(("vps1", ti % 2), [])
                    for half in range(2):
                        for it in range(NI):
                            P.op("tensor", lambda t, vps=vps, half=half, it=it, g0=g0: t.matmul(
                                vps[:, half * 512:(half + 1) * 512],
                                xnT1[:, xnt_col(it, g0): xnt_col(it, g0) + 128],
                                wbuf[:, it * D + half * 512: it * D + (half + 1) * 512],
                                start=(it == 0), stop=False),
                                waits=(wvrdy + pw + vb_ready[b]) if (half == 0 and it == 0) else ())
                        P.op("tensor", lambda t, vps=vps, half=half: t.matmul(
                            vps[:, half * 512:(half + 1) * 512], ones_sb[:, :],
                            bvrow_sb[:, half * 512:(half + 1) * 512],
                            start=False, stop=True),
                            inc=(ct if half == 1 else None))
                    P.op("vector", lambda v, vps=vps, ti=ti: v.tensor_copy(
                        vext1[:, ti * 1056:ti * 1056 + 528].rearrange("p (h c) -> p h c", c=66)[:, :, 0:64],
                        vps[:, 0:512].rearrange("p (h c) -> p h c", c=64)),
                        waits=[(ct, ct.n)] + free.get("vext1", []))
                    P.op("vector", lambda v, vps=vps, ti=ti: v.tensor_copy(
                        vext1[:, ti * 1056 + 528:ti * 1056 + 1056].rearrange("p (h c) -> p h c", c=66)[:, :, 0:64],
                        vps[:, 512:1024].rearrange("p (h c) -> p h c", c=64)),
                        inc=cv)
                    free[("vps1", ti % 2)] = [(cv, cv.n)]
                vdone1 = [(cv, cv.n)]

                for p in range(8):
                    i = p % 2
                    wqw = free.get(("wqs", i), []) + ([(ct, ct.n)] if p < 2 else [])
                    wrdy = emit_wload(
                        [(wqs[i].rearrange("p (i o) -> p i o", o=128),
                          dram["l1_wqT"][:, p * 128:(p + 1) * 128].rearrange("(i p) o -> p i o", p=128), 1024),
                         (wks[i].rearrange("p (i o) -> p i o", o=128),
                          dram["l1_wkT"][:, p * 128:(p + 1) * 128].rearrange("(i p) o -> p i o", p=128), 1024)],
                        dst_free=wqw)
                    for c in range(4):
                        ps = slot(c % 2)
                        for it in range(NI):
                            P.op("tensor", lambda t, ps=ps, it=it, i=i, c=c, b=b: t.matmul(
                                ps, wqs[i].rearrange("p (i o) -> p i o", o=128)[:, it, :],
                                xnT1[:, it * XNT_B + b * T + c * 512: it * XNT_B + b * T + (c + 1) * 512],
                                start=(it == 0), stop=(it == NI - 1)),
                                inc=(ct if it == NI - 1 else None),
                                waits=(wrdy + vdone1 + free.get(("ps01a", c % 2), [])) if it == 0 else ())
                        P.op("vector", lambda v, ps=ps, i=i, c=c, p=p: v.tensor_scalar(
                            qtp[i][:, c * 512:(c + 1) * 512], ps,
                            bqT_sb[:, p:p + 1], 0.125, ALU.add, ALU.mult),
                            inc=cv, waits=[(ct, ct.n)] + free.get(("qtp", i), []))
                        free[("ps01a", c % 2)] = [(cv, cv.n)]
                    for c in range(4):
                        ps = slot(c % 2)
                        for it in range(NI):
                            P.op("tensor", lambda t, ps=ps, it=it, i=i, c=c, b=b: t.matmul(
                                ps, wks[i].rearrange("p (i o) -> p i o", o=128)[:, it, :],
                                xnT1[:, it * XNT_B + b * T + c * 512: it * XNT_B + b * T + (c + 1) * 512],
                                start=(it == 0), stop=(it == NI - 1)),
                                inc=(ct if it == NI - 1 else None),
                                waits=free.get(("ps01a", c % 2), []) if it == 0 else ())
                        P.op("scalar", lambda s, ps=ps, i=i, c=c, p=p: s.add(
                            ktp[i][:, c * 512:(c + 1) * 512], ps,
                            bkT_sb[:, p:p + 1]),
                            inc=cs, waits=[(ct, ct.n)] + free.get(("ktp", i), []))
                        free[("ps01a", c % 2)] = [(cs, cs.n)]
                    prdone = [(cv, cv.n), (cs, cs.n)]
                    free[("wqs", i)] = [(ct, ct.n)]
                    for hh in range(2):
                        h = 2 * p + hh
                        hb0 = hh * 64
                        pend_div_s = []
                        pend_div_t = []
                        divstate = {}

                        def emit_div_s(w, hh=hh):
                            ysl = pool[2] if w % 2 == 0 else pool[0]
                            nsrc = (ysl[64:65, 0:512] if hh == 0
                                    else ysl[0:1, 512:1024])
                            P.op("scalar", lambda s, nsrc=nsrc: s.activation(
                                tl1[0][:, :], nsrc, AF.Ln), inc=cs,
                                waits=free.get("ydone", []) + free.get("tl1f", []))
                            P.op("scalar", lambda s: s.activation(
                                rr1[0][:, :], tl1[0][:, :], AF.Exp, scale=cst[0:1, 3:4]),
                                inc=cs, waits=[(cs, cs.n)] + free.get("rr1f", []))
                            free["tl1f"] = [(cs, cs.n)]
                            divstate[w] = [(cs, cs.n)]

                        def emit_div(w, hh=hh, hb0=hb0, p=p, i=i):
                            ysl = pool[2] if w % 2 == 0 else pool[0]
                            rdone1 = divstate.pop(w)
                            rbp = pool[3][hb0:hb0 + 64, 0:512]
                            P.op("tensor", lambda t, rbp=rbp: t.matmul(
                                rbp, ones_sb[0:1, 0:64], rr1[0][:, :],
                                start=True, stop=True), inc=ct,
                                waits=rdone1 + free.get("rbps1", []))
                            free["rr1f"] = [(ct, ct.n)]
                            ysrc = ysl[hb0:hb0 + 64, 0:512]
                            P.op("vector", lambda v, ysrc=ysrc, hb0=hb0: v.tensor_copy(
                                ysb[0][hb0:hb0 + 64, :], ysrc), inc=cv,
                                waits=[(ct, ct.n)] + free.get("ysbf", []))
                            free[("yps", w % 2)] = [(cv, cv.n)]
                            P.op("vector", lambda v, hb0=hb0, p=p, w=w, rbp=rbp: v.tensor_tensor(
                                catT[hb0:hb0 + 64, p * T + w * 512:p * T + (w + 1) * 512],
                                ysb[0][hb0:hb0 + 64, :], rbp, ALU.mult),
                                inc=cv, waits=[(cv, cv.n)])
                            free["ysbf"] = [(cv, cv.n)]
                            free["rbps1"] = [(cv, cv.n)]

                        for w in range(4):
                            nkt = 4 * w + 4
                            ysl = pool[2] if w % 2 == 0 else pool[0]
                            tierC = h >= 5
                            pdone = {}

                            def emit_se(kt, w=w, hb0=hb0, h=h, i=i, tierC=tierC, pdone=pdone):
                                sc = slot(2 + kt % 2)
                                P.op("tensor", lambda t, sc=sc, kt=kt, w=w, hb0=hb0, i=i: t.matmul(
                                    sc, ktp[i][hb0:hb0 + 64, kt * 128:(kt + 1) * 128],
                                    qtp[i][hb0:hb0 + 64, w * 512:(w + 1) * 512],
                                    start=True, stop=True),
                                    inc=ct,
                                    waits=prdone + free.get(("scps", kt % 2), []))
                                scdep = [(ct, ct.n)]
                                pfree = free.get(("probs", kt % 2), [])
                                pb = probs[kt % 2]
                                cvd = False
                                if tierC:
                                    if kt >= 4 * w:
                                        j = kt - 4 * w
                                        P.op("vector", lambda v, sc=sc, j=j: v.tensor_tensor(
                                            sc, sc, maskw_sb[:, (3 - j) * 128:(3 - j) * 128 + 512],
                                            ALU.add), inc=cv, waits=scdep)
                                        scdep = [(cv, cv.n)]
                                        cvd = True
                                    col = (h - 5) * 40 + WOFF[w] + kt
                                    P.op("scalar", lambda s, sc=sc, pb=pb, col=col: s.activation(
                                        pb[:, :], sc, AF.Exp,
                                        bias=abiasw_sb[:, col:col + 1]),
                                        inc=cs, waits=scdep + pfree)
                                else:
                                    steep = h < 3
                                    for qt in range(4):
                                        qtg = 4 * w + qt
                                        if kt > qtg:
                                            P.op("vector", lambda v, pb=pb, qt=qt: v.memset(
                                                pb[:, qt * 128:(qt + 1) * 128], 0.0),
                                                inc=cv, waits=scdep + pfree)
                                            cvd = True
                                            continue
                                        if kt == qtg:
                                            P.op("vector", lambda v, sc=sc, qt=qt: v.tensor_tensor(
                                                sc[:, qt * 128:(qt + 1) * 128],
                                                sc[:, qt * 128:(qt + 1) * 128],
                                                maskm_sb[:, :], ALU.add),
                                                inc=cv, waits=scdep)
                                            scdep2 = [(cv, cv.n)]
                                            cvd = True
                                        else:
                                            scdep2 = scdep
                                        if steep:
                                            for q6 in range(2):
                                                jj = (8 * w + 2 * qt + q6) - 2 * kt
                                                c0 = qt * 128 + q6 * 64
                                                if jj < 0:
                                                    P.op("vector", lambda v, pb=pb, c0=c0: v.memset(
                                                        pb[:, c0:c0 + 64], 0.0),
                                                        inc=cv, waits=scdep2 + pfree)
                                                    cvd = True
                                                    continue
                                                col = h * 32 + jj
                                                P.op("scalar", lambda s, sc=sc, pb=pb, c0=c0, col=col: s.activation(
                                                    pb[:, c0:c0 + 64],
                                                    sc[:, c0:c0 + 64], AF.Exp,
                                                    bias=abias2_sb[:, col:col + 1]),
                                                    inc=cs, waits=scdep2 + pfree)
                                        else:
                                            col = h * 16 + (qtg - kt)
                                            P.op("scalar", lambda s, sc=sc, pb=pb, qt=qt, col=col: s.activation(
                                                pb[:, qt * 128:(qt + 1) * 128],
                                                sc[:, qt * 128:(qt + 1) * 128], AF.Exp,
                                                bias=abias_sb[:, col:col + 1]),
                                                inc=cs, waits=scdep2 + pfree)
                                free[("scps", kt % 2)] = [(cs, cs.n)]
                                pdone[kt] = [(cs, cs.n)] + ([(cv, cv.n)] if cvd else [])

                            emit_se(0)
                            for kt in range(nkt):
                                if kt + 1 < nkt:
                                    emit_se(kt + 1)
                                elif pend_div_t:
                                    pend_div_t.pop(0)()
                                if kt == 0 and pend_div_s:
                                    pend_div_s.pop(0)()
                                w0 = (pdone[kt] + prdone + free.get(("yps", w % 2), [])
                                      ) if kt == 0 else pdone[kt]
                                if hh == 0:
                                    P.op("tensor", lambda t, kt=kt, h=h, w=w, nkt=nkt, ysl=ysl: t.matmul(
                                        ysl[0:65, 0:512],
                                        vext1[:, kt * 1056 + h * 66: kt * 1056 + h * 66 + 65],
                                        probs[kt % 2][:, :],
                                        start=(kt == 0), stop=(kt == nkt - 1)),
                                        inc=ct, waits=w0)
                                else:
                                    P.op("tensor", lambda t, kt=kt, h=h, w=w, nkt=nkt, ysl=ysl: t.matmul(
                                        ysl[64:128, 0:512],
                                        vext1[:, kt * 1056 + h * 66: kt * 1056 + h * 66 + 64],
                                        probs[kt % 2][:, :],
                                        start=(kt == 0), stop=(kt == nkt - 1)),
                                        waits=w0)
                                    P.op("tensor", lambda t, kt=kt, h=h, w=w, nkt=nkt, ysl=ysl: t.matmul(
                                        ysl[0:1, 512:1024],
                                        vext1[:, kt * 1056 + h * 66 + 64: kt * 1056 + h * 66 + 65],
                                        probs[kt % 2][:, :],
                                        start=(kt == 0), stop=(kt == nkt - 1)),
                                        inc=ct)
                                free[("probs", kt % 2)] = [(ct, ct.n)]
                            free["ydone"] = [(ct, ct.n)]
                            pend_div_s.append(lambda w=w: emit_div_s(w))
                            pend_div_t.append(lambda w=w: emit_div(w))
                        while pend_div_s:
                            pend_div_s.pop(0)()
                        while pend_div_t:
                            pend_div_t.pop(0)()
                    free[("ps01a", 0)] = [(cv, cv.n), (cs, cs.n)]
                    free[("ps01a", 1)] = [(cv, cv.n), (cs, cs.n)]
                    free[("qtp", i)] = [(ct, ct.n)]
                    free[("ktp", i)] = [(ct, ct.n)]
                catdone = [(cv, cv.n)]
                free["vext1"] = [(ct, ct.n)]

                hd_done = [(ct, ct.n)]
                for it in range(NI):
                    wordy = emit_wload(
                        [(wbuf[:, it * D:(it + 1) * D],
                          dram["l1_woT"][it * 128:(it + 1) * 128, :], D)],
                        dst_free=(hd_done if it == 0 else ()))
                for ti in range(NT):
                    i = ti % 2
                    yp = pool[1] if ti % 2 == 0 else pool[0]
                    P.op("sync", lambda s, i=i, b=b, ti=ti: s.dma_start(
                        xres[i][:, :], x_mid[b, ti * 128:(ti + 1) * 128, :]),
                        inc=dctr(f"xres{i}"), k=16, waits=free.get(("xres", i), []))
                    xrdy = dmadone(f"xres{i}")
                    pw = free.get(("wops1", ti % 2), [])
                    for half in range(2):
                        for it in range(NI):
                            P.op("tensor", lambda t, yp=yp, half=half, it=it, ti=ti: t.matmul(
                                yp[:, half * 512:(half + 1) * 512],
                                catT[:, it * T + ti * 128: it * T + (ti + 1) * 128],
                                wbuf[:, it * D + half * 512: it * D + (half + 1) * 512],
                                start=(it == 0), stop=False),
                                waits=(catdone + pw + wordy) if (half == 0 and it == 0) else ())
                        P.op("tensor", lambda t, yp=yp, half=half: t.matmul(
                            yp[:, half * 512:(half + 1) * 512], ones_sb[:, :],
                            borow_sb[:, half * 512:(half + 1) * 512],
                            start=False, stop=True),
                            inc=(ct if half == 1 else None))
                    for half in range(2):
                        P.op("vector", lambda v, i=i, half=half, yp=yp: v.tensor_tensor(
                            res_sb[i][:, half * 512:(half + 1) * 512],
                            yp[:, half * 512:(half + 1) * 512],
                            xres[i][:, half * 512:(half + 1) * 512], ALU.add),
                            inc=(cv if half == 1 else None),
                            waits=([(ct, ct.n)] + xrdy + free.get(("ressb", i), [])) if half == 0 else ())
                    free[("xres", i)] = [(cv, cv.n)]
                    free[("wops1", ti % 2)] = [(cv, cv.n)]
                    P.op("sync", lambda s, i=i, b=b, ti=ti: s.dma_start(
                        x_att[b, ti * 128:(ti + 1) * 128, :], res_sb[i][:, :]),
                        inc=dctr(f"res{i}"), k=16, waits=[(cv, cv.n)])
                    free[("ressb", i)] = dmadone(f"res{i}")
                free["wbuf"] = [(ct, ct.n)]
            attn1_done = [(ct, ct.n), (cv, cv.n), (cs, cs.n)]
            sga.close()

            sf = ExitStack()
            FT = sf.enter_context
            NFT = DFF // 128
            hT1 = FT(nc.sbuf_tensor("hT1", [128, NFT * 512], BF16, side="right"))
            w1c = [FT(nc.sbuf_tensor(f"w1c{i}", [128, NI * 128], BF16, side="right")) for i in range(4)]
            w3c = [FT(nc.sbuf_tensor(f"w3c{i}", [128, NI * 128], BF16, side="right")) for i in range(4)]
            w2c = [FT(nc.sbuf_tensor(f"w2c{i}", [128, D], BF16, side="right")) for i in range(4)]
            gsb = [FT(nc.sbuf_tensor(f"gsb{i}", [128, 512], BF16, side="right")) for i in range(4)]
            wstf = [FT(nc.sbuf_tensor(f"wstf{i}", [128, 1024], F32, side="right")) for i in range(4)]
            wsts = [FT(nc.sbuf_tensor(f"wsts{i}", [128, 1024], F32, side="right")) for i in range(4)]
            for i in range(4):
                free[("wstf", i)] = attn1_done
                free[("wsts", i)] = attn1_done
            xnv = xnT1[:, :].rearrange("p (i b t) -> p i b t", b=NB, t=T)
            def load_ft(e, ft):
                i = (e * NFT + ft) % 4
                df = (attn1_done if (e == 0 and ft < 4) else [])
                r1 = emit_wload(
                    [(w1c[i][:, :].rearrange("p (i f) -> p i f", f=128),
                      dram["l1_w1"][e][:, ft * 128:(ft + 1) * 128].rearrange("(i p) f -> p i f", p=128), 1024)],
                    dst_free=df + free.get(("w1c", i), []),
                    ring=wstf, rkey="wstf", queue="sync")
                r3 = emit_wload(
                    [(w3c[i][:, :].rearrange("p (i f) -> p i f", f=128),
                      dram["l1_w3"][e][:, ft * 128:(ft + 1) * 128].rearrange("(i p) f -> p i f", p=128), 1024)],
                    dst_free=df + free.get(("w3c", i), []),
                    ring=wsts, rkey="wsts", queue="scalar")
                return (r1, r3)

            wrdys = {}
            for e in range(E1):
                if e == 0:
                    wrdys[0] = load_ft(0, 0)
                    wrdys[1] = load_ft(0, 1)
                for ft in range(NFT):
                    i = (e * NFT + ft) % 4
                    if ft + 2 < NFT:
                        wrdys[ft + 2] = load_ft(e, ft + 2)
                    elif e + 1 < E1:
                        wrdys[ft + 2 - NFT] = load_ft(e + 1, ft + 2 - NFT)
                    wrdy1, wrdy3 = wrdys[ft]
                    par = ft % 2
                    gp = slot(0 + 2 * par)
                    up = slot(1 + 2 * par)
                    for it in range(NI):
                        P.op("tensor", lambda t, gp=gp, it=it, i=i, e=e: t.matmul(
                            gp, w1c[i][:, it * 128:(it + 1) * 128],
                            xnv[:, it, :, e * 256:(e + 1) * 256],
                            start=(it == 0), stop=(it == NI - 1)),
                            inc=(ct if it == NI - 1 else None),
                            waits=(wrdy1 + free.get(("gup", par), [])) if it == 0 else ())
                    for it in range(NI):
                        P.op("tensor", lambda t, up=up, it=it, i=i, e=e: t.matmul(
                            up, w3c[i][:, it * 128:(it + 1) * 128],
                            xnv[:, it, :, e * 256:(e + 1) * 256],
                            start=(it == 0), stop=(it == NI - 1)),
                            inc=(ct if it == NI - 1 else None),
                            waits=wrdy3 if it == 0 else ())
                    free[("w1c", i)] = [(ct, ct.n)]
                    free[("w3c", i)] = [(ct, ct.n)]
                    P.op("scalar", lambda s, gp=gp, i=i: s.activation(
                        gsb[i][:, :], gp, AF.Silu), inc=cs,
                        waits=[(ct, ct.n)] + free.get(("gsb", i), []))
                    P.op("vector", lambda v, up=up, i=i, ft=ft: v.tensor_tensor(
                        hT1[:, ft * 512:(ft + 1) * 512], gsb[i][:, :], up, ALU.mult),
                        inc=cv, waits=[(cs, cs.n)] + free.get(("hT1", ft), []))
                    free[("gsb", i)] = [(cv, cv.n)]
                    free[("gup", par)] = [(cv, cv.n)]
                hdone1 = [(cv, cv.n)]
                ypools = [pool[0], pool[1], pool[2], pool[3]]
                pwy = free.get("ypsf", [])
                w2rdys = {}

                def load_w2ft(ft2):
                    ring, rk, qu = ((wstf, "wstf", "sync") if ft2 % 2 == 0
                                    else (wsts, "wsts", "scalar"))
                    return emit_wload(
                        [(w2c[ft2 % 4][:, :],
                          dram["l1_w2"][e][ft2 * 128:(ft2 + 1) * 128, :], D)],
                        dst_free=free.get(("w2c", ft2 % 4), []),
                        ring=ring, rkey=rk, queue=qu)

                w2rdys[0] = load_w2ft(0)
                w2rdys[1] = load_w2ft(1)
                xrdy3s = {}
                for sl in range(2):
                    i, b, st2 = sl % 2, sl // 2, sl % 2
                    ti = e * 2 + st2
                    P.op("sync", lambda s, i=i, b=b, ti=ti: s.dma_start(
                        xres[i][:, :], x_att[b, ti * 128:(ti + 1) * 128, :]),
                        inc=dctr(f"xres{i}"), k=16, waits=free.get(("xres", i), []))
                    xrdy3s[sl] = dmadone(f"xres{i}")
                for ft in range(NFT):
                    i = ft % 4
                    if ft + 2 < NFT:
                        w2rdys[ft + 2] = load_w2ft(ft + 2)
                    w2rdy1 = w2rdys[ft]
                    for sl in range(4):
                        for half in range(2):
                            P.op("tensor", lambda t, sl=sl, half=half, ft=ft, i=i, e=e: t.matmul(
                                ypools[sl][:, half * 512:(half + 1) * 512],
                                hT1[:, ft * 512 + sl * 128: ft * 512 + (sl + 1) * 128],
                                w2c[i][:, half * 512:(half + 1) * 512],
                                start=(ft == 0), stop=(ft == NFT - 1)),
                                inc=(ct if (sl == 3 and half == 1) else None),
                                waits=(w2rdy1 + ((hdone1 + pwy) if ft == 0 else [])) if (sl == 0 and half == 0) else ())
                    free[("w2c", i)] = [(ct, ct.n)]
                free.update({("hT1", ft): [(ct, ct.n)] for ft in range(NFT)})
                ydone1 = [(ct, ct.n)]
                for sl in range(4):
                    i = sl % 2
                    b, st2 = sl // 2, sl % 2
                    ti = e * 2 + st2
                    xrdy3 = xrdy3s[sl]
                    for half in range(2):
                        P.op("vector", lambda v, i=i, half=half, sl=sl: v.tensor_tensor(
                            res_sb[i][:, half * 512:(half + 1) * 512],
                            ypools[sl][:, half * 512:(half + 1) * 512],
                            xres[i][:, half * 512:(half + 1) * 512], ALU.add),
                            inc=(cv if half == 1 else None),
                            waits=(ydone1 + xrdy3 + free.get(("ressb", i), [])) if half == 0 else ())
                    free[("xres", i)] = [(cv, cv.n)]
                    if sl < 2:
                        free[("gup", sl)] = [(cv, cv.n)]
                    if sl + 2 < 4:
                        sl2 = sl + 2
                        i2, b2_, st22 = sl2 % 2, sl2 // 2, sl2 % 2
                        ti2 = e * 2 + st22
                        P.op("sync", lambda s, i2=i2, b2_=b2_, ti2=ti2: s.dma_start(
                            xres[i2][:, :], x_att[b2_, ti2 * 128:(ti2 + 1) * 128, :]),
                            inc=dctr(f"xres{i2}"), k=16, waits=free.get(("xres", i2), []))
                        xrdy3s[sl2] = dmadone(f"xres{i2}")
                    P.op("sync", lambda s, i=i, b=b, ti=ti: s.dma_start(
                        y_out[b, ti * 128:(ti + 1) * 128, :], res_sb[i][:, :]),
                        inc=dctr(f"res{i}"), k=16, waits=[(cv, cv.n)])
                    free[("ressb", i)] = dmadone(f"res{i}")
                free["ypsf"] = [(cv, cv.n)]
            sf.close()
            sx1.close()

        if 1 not in layers:
            sln.close()
        finals = [(c.h, c.n) for c in ALLC]
        P.raw("sync", lambda e, fl=finals: [e.wait_ge(h, n) for h, n in fl if n > 0])
        P.realize(nc)
    return nc


def _emit_layer1_body(*a, **k):
    raise NotImplementedError("layer 1 not yet implemented")


FUSED = True


def kernel(**inputs):
    W = _host_prep(inputs)
    x = np.ascontiguousarray(np.asarray(inputs["L0_attn_x"], np.float32))
    shards = [x[2 * c:2 * c + 2] for c in range(NCORES)]

    l0_keys = ["l0_wqkv", "l0_wf", "l0_bf_row", "l0_bf_col", "l0_wout",
               "l0_bout_row", "l0_rwT", "l0_rb_row", "l0_w1", "l0_b1",
               "l0_w2", "l0_b2", "c_pen"]
    l1_keys = ["l1_wqT", "l1_wkT", "l1_wvT", "l1_woT", "l1_bqT", "l1_bkT",
               "l1_bv_row", "l1_bo_row", "l1_w1", "l1_w3", "l1_w2",
               "c_abias", "c_abias2", "c_abiasw", "c_maskm", "c_maskw"]

    if FUSED:
        nc = build_program((0, 1))
        maps = [dict({k: W[k] for k in l0_keys + l1_keys},
                     c_ident=W["c_ident"], x0=shards[c]) for c in range(NCORES)]
        res = run_bass_kernel_spmd(nc, maps, core_ids=list(range(NCORES)))
        return np.concatenate([np.asarray(r["y"]).reshape(NB, T, D) for r in res.results], axis=0)
    nc0 = build_program((0,))
    maps = [dict({k: W[k] for k in l0_keys}, c_ident=W["c_ident"], x0=shards[c])
            for c in range(NCORES)]
    res0 = run_bass_kernel_spmd(nc0, maps, core_ids=list(range(NCORES)))
    xmid = [np.ascontiguousarray(np.asarray(r["x_mid"]).reshape(NB, T, D)) for r in res0.results]
    nc1 = build_program((1,))
    maps = [dict({k: W[k] for k in l1_keys}, c_ident=W["c_ident"], x_mid=xmid[c])
            for c in range(NCORES)]
    res1 = run_bass_kernel_spmd(nc1, maps, core_ids=list(range(NCORES)))
    return np.concatenate([np.asarray(r["y"]).reshape(NB, T, D) for r in res1.results], axis=0)


# version 15 (final)
# speedup vs baseline: 1.0531x; 1.0125x over previous
import math
from contextlib import ExitStack

import numpy as np
import ml_dtypes

import concourse.bass as bass
import concourse.mybir as mybir
from concourse.bass_utils import run_bass_kernel_spmd

F32 = mybir.dt.float32
BF16 = mybir.dt.bfloat16
AF = mybir.ActivationFunctionType
ALU = mybir.AluOpType
AX = mybir.AxisListType
BFD = ml_dtypes.bfloat16

NCORES = 8
NB = 2
T = 2048
D = 1024
H = 16
DH = 64
NT = T // 128
NI = D // 128
NG = NB * NT
NC2 = NB * NT // 4
E0, H0 = 32, 256
E1, DFF = 8, 2816
BIG = 1.0e30
SM_SHIFT = 16.0
LN_EPS = 1e-5
XNT_B = NB * T


def _alibi_slopes(n_heads):
    start = 2 ** (-(2 ** (-(math.log2(n_heads) - 3))))
    return [start * start**i for i in range(n_heads)]


def _host_prep(inputs):
    f32 = np.float32
    c = np.ascontiguousarray
    W = {}
    W["l0_wqkv"] = c(np.asarray(inputs["L0_attn_W_qkv"], f32))
    W["l0_wf"] = c(np.asarray(inputs["L0_attn_W_f"], f32))
    bf = np.asarray(inputs["L0_attn_b_f"], f32)
    W["l0_bf_row"] = c(np.tile(bf, H)[None, :])
    W["l0_bf_col"] = c(np.concatenate([bf, bf])[:, None])
    W["l0_wout"] = c(np.asarray(inputs["L0_attn_W_out"], f32))
    W["l0_bout_row"] = c(np.asarray(inputs["L0_attn_b_out"], f32)[None, :])
    W["l0_rwT"] = c(np.asarray(inputs["L0_ffn_router_w"], f32).T)
    W["l0_rb_row"] = c(np.asarray(inputs["L0_ffn_router_b"], f32)[None, :])
    W["l0_w1"] = c(np.asarray(inputs["L0_ffn_weights1"], f32))
    W["l0_b1"] = c(np.asarray(inputs["L0_ffn_bias1"], f32)
                   .reshape(E0, 2, 128).transpose(2, 0, 1).reshape(128, 2 * E0))
    W["l0_w2"] = c(np.asarray(inputs["L0_ffn_weights2"], f32))
    W["l0_b2"] = c(np.asarray(inputs["L0_ffn_bias2"], f32))
    for nm, k in (("q", "Wq"), ("k", "Wk"), ("v", "Wv"), ("o", "Wo")):
        W[f"l1_w{nm}T"] = c(np.asarray(inputs[f"L1_attn_{k}"], f32).T)
    W["l1_bqT"] = c(np.asarray(inputs["L1_attn_bq"], f32).reshape(NI, 128).T)
    W["l1_bkT"] = c(np.asarray(inputs["L1_attn_bk"], f32).reshape(NI, 128).T)
    W["l1_bv_row"] = c(np.asarray(inputs["L1_attn_bv"], f32)[None, :])
    W["l1_bo_row"] = c(np.asarray(inputs["L1_attn_bo"], f32)[None, :])
    W["l1_w1"] = c(np.asarray(inputs["L1_ffn_w1"], f32))
    W["l1_w3"] = c(np.asarray(inputs["L1_ffn_w3"], f32))
    W["l1_w2"] = c(np.asarray(inputs["L1_ffn_w2"], f32))

    W["c_ident"] = np.eye(128, dtype=f32)
    slopes = _alibi_slopes(H)
    abias = np.zeros((128, H * 16), np.float64)
    p = np.arange(128)
    for h in range(H):
        for m2 in range(16):
            abias[:, h * 16 + m2] = slopes[h] * (p - m2 * 128 - 64)
    W["c_abias"] = abias.astype(f32)
    abias2 = np.zeros((128, 3 * 32), np.float64)
    for h in range(3):
        for j in range(32):
            abias2[:, h * 32 + j] = slopes[h] * (p - 64 * j - 32)
    W["c_abias2"] = abias2.astype(f32)
    woff = [0, 4, 12, 24]
    abw = np.zeros((128, 11 * 40), np.float64)
    for h in range(5, H):
        for w in range(4):
            for kt in range(4 * w + 4):
                abw[:, (h - 5) * 40 + woff[w] + kt] = slopes[h] * (kt * 128 + p - (w * 512 + 384))
    W["c_abiasw"] = abw.astype(f32)
    pen = np.full((E0, E0, 128), -BIG, f32)
    for e in range(E0):
        pen[e, e, :] = 0.0
    W["c_pen"] = np.ascontiguousarray(pen.reshape(E0, E0 * 128)).astype(BFD)
    kk = np.arange(128)[:, None]
    qr = np.arange(128)[None, :]
    tri = np.where(kk > qr, -BIG, 0.0).astype(f32)
    W["c_maskm"] = tri.astype(BFD)
    mw = np.zeros((128, 896), f32)
    mw[:, 0:384] = -BIG
    mw[:, 384:512] = tri
    W["c_maskw"] = mw.astype(BFD)
    return W


class Ctr:
    __slots__ = ("h", "n", "name")

    def __init__(self, stack, nc, name):
        self.h = stack.enter_context(nc.semaphore(name))
        self.n = 0
        self.name = name


class Prog:
    ENGS = ("tensor", "vector", "scalar", "gpsimd", "sync")

    def __init__(self):
        self.q = {e: [] for e in self.ENGS}
        self.last_wait = {e: {} for e in self.ENGS}

    def op(self, eng, fn, inc=None, k=1, waits=()):
        lw = self.last_wait[eng]
        wl = []
        for ctr, v in waits:
            if v <= 0 or lw.get(ctr.name, -1) >= v:
                continue
            lw[ctr.name] = v
            wl.append((ctr.h, v))
        tgt = None
        if inc is not None:
            inc.n += k
            tgt = (inc.h, k)

        def emit(e, wl=wl, fn=fn, tgt=tgt):
            for h, v in wl:
                e.wait_ge(h, v)
            ins = fn(e)
            if tgt is not None:
                ins.then_inc(tgt[0], tgt[1])
        self.q[eng].append(emit)

    def raw(self, eng, fn):
        self.q[eng].append(fn)



    def realize(self, nc):
        with nc.Block() as blk:
            for eng in self.ENGS:
                ops = self.q[eng]
                if not ops:
                    continue
                def make(ops):
                    def run(e):
                        for f in ops:
                            f(e)
                    return run
                getattr(blk, eng)(make(ops))


def build_program(layers=(0, 1)):
    nc = bass.Bass("TRN2", target_bir_lowering=False)
    P = Prog()
    dram = {}

    def din(name, shape, dt=F32):
        dram[name] = nc.dram_tensor(name, list(shape), dt, kind="ExternalInput").ap()
        return dram[name]

    def dint(name, shape, dt=F32):
        dram[name] = nc.dram_tensor(name, list(shape), dt, kind="Internal").ap()
        return dram[name]

    def dout(name, shape, dt=F32):
        dram[name] = nc.dram_tensor(name, list(shape), dt, kind="ExternalOutput").ap()
        return dram[name]

    if 0 in layers:
        x0 = din("x0", (NB, T, D))
        din("l0_wqkv", (D, 3 * D)); din("l0_wf", (DH, DH))
        din("l0_bf_row", (1, D)); din("l0_bf_col", (128, 1))
        din("l0_wout", (D, D)); din("l0_bout_row", (1, D))
        din("l0_rwT", (D, E0)); din("l0_rb_row", (1, E0))
        din("l0_w1", (E0, D, H0)); din("l0_b1", (128, 2 * E0))
        din("l0_w2", (E0, H0, D)); din("l0_b2", (E0, D))
        din("c_pen", (E0, E0 * 128), BF16)
        hb = dint("hbounce", (NG, 128, 2 * E0, 128), BF16)
    if 1 in layers:
        for nm in "qkvo":
            din(f"l1_w{nm}T", (D, D))
        din("l1_bqT", (128, NI)); din("l1_bkT", (128, NI))
        din("l1_bv_row", (1, D)); din("l1_bo_row", (1, D))
        din("l1_w1", (E1, D, DFF)); din("l1_w3", (E1, D, DFF))
        din("l1_w2", (E1, DFF, D))
        din("c_abias", (128, H * 16)); din("c_abias2", (128, 96))
        din("c_abiasw", (128, 440))
        din("c_maskm", (128, 128), BF16); din("c_maskw", (128, 896), BF16)
    din("c_ident", (128, 128))

    if layers == (0,):
        x_mid = dout("x_mid", (NB, T, D))
    elif 0 in layers:
        x_mid = dint("x_mid", (NB, T, D))
    else:
        x_mid = din("x_mid", (NB, T, D))
    if 1 in layers:
        y_out = dout("y", (NB, T, D))
        x_att = dint("x_att", (NB, T, D))
    xn_hbm = dint("xn_hbm", (NB, T, D), BF16)

    st = ExitStack()
    with st:
        E = st.enter_context
        ct = Ctr(st, nc, "ct")
        cv = Ctr(st, nc, "cv")
        cs = Ctr(st, nc, "cs")
        cg = Ctr(st, nc, "cg")
        ch = Ctr(st, nc, "ch")
        cq = Ctr(st, nc, "cq")
        ALLC = [ct, cv, cs, cg, ch, cq]
        dctrs = {}

        def dctr(name):
            if name not in dctrs:
                dctrs[name] = Ctr(st, nc, "d_" + name)
                ALLC.append(dctrs[name])
            return dctrs[name]

        def dmadone(name):
            c = dctr(name)
            return [(c, c.n)]

        def allw():
            return [(c, c.n) for c in ALLC]

        pool = [E(nc.psum_tensor(f"ps{i}", [128, 1024], F32)) for i in range(4)]

        def slot(i, rows=128, cols=512, r0=0):
            return pool[i // 2][r0:r0 + rows, (i % 2) * 512:(i % 2) * 512 + cols]

        ident_sb = E(nc.sbuf_tensor("ident_sb", [128, 128], F32))
        ones_sb = E(nc.sbuf_tensor("ones_sb", [1, 128], BF16))
        P.op("gpsimd", lambda g: g.dma_start(ident_sb[:, :], dram["c_ident"][:, :]),
             inc=cg, k=16)
        P.op("vector", lambda v: v.memset(ones_sb[:, :], 1.0), inc=cv)
        cst = E(nc.sbuf_tensor("cst", [128, 4], F32))
        P.op("vector", lambda v: v.memset(cst[:, 0:1], LN_EPS))
        P.op("vector", lambda v: v.memset(cst[:, 1:2], -0.5))
        P.op("vector", lambda v: v.memset(cst[:, 2:3], 1e-6))
        P.op("vector", lambda v: v.memset(cst[:, 3:4], -1.0), inc=cv)
        wst = [E(nc.sbuf_tensor(f"wst{i}", [128, 1024], F32)) for i in range(2)]
        wsti = {}

        def emit_wload(pairs, waits=(), dst_free=(), ring=None, rkey="wst", queue="scalar",
                       cast_eng="vector"):
            cc = cv if cast_eng == "vector" else cs
            bufs = wst if ring is None else ring
            nb = len(bufs)
            groups, cur, acc = [], [], 0
            for pr in pairs:
                if acc + pr[2] > 1024:
                    groups.append(cur); cur, acc = [], 0
                cur.append(pr); acc += pr[2]
            if cur:
                groups.append(cur)
            firstall = True
            for grp in groups:
                i = wsti.get(rkey, 0) % nb
                wsti[rkey] = wsti.get(rkey, 0) + 1
                wc = dctr(f"{rkey}{i}")
                off = 0
                first = True
                for dst, srcap, cols in grp:
                    w = list(free.get((rkey, i), []))
                    if firstall and first:
                        w = list(waits) + w
                    P.op(queue, lambda s, i=i, off=off, srcap=srcap, cols=cols: s.dma_start(
                        bufs[i][:, off:off + cols], srcap), inc=wc, k=16,
                        waits=(w if first else ()))
                    first = False
                    off += cols
                ms = [(wc, wc.n)]
                off = 0
                first = True
                for dst, srcap, cols in grp:
                    w = (ms + (list(dst_free) if firstall else [])) if first else ()
                    if cast_eng == "vector":
                        P.op("vector", lambda v, i=i, off=off, dst=dst, cols=cols: v.tensor_copy(
                            dst, bufs[i][:, off:off + cols]), inc=cc, waits=w)
                    else:
                        P.op("scalar", lambda s, i=i, off=off, dst=dst, cols=cols: s.copy(
                            dst, bufs[i][:, off:off + cols]), inc=cc, waits=w)
                    first = False
                    off += cols
                free[(rkey, i)] = [(cc, cc.n)]
                firstall = False
            return [(cc, cc.n)]

        xres = [E(nc.sbuf_tensor(f"xres{i}", [128, D], F32)) for i in range(2)]
        res_sb = [E(nc.sbuf_tensor(f"res{i}", [128, D], F32)) for i in range(2)]
        sln = ExitStack()
        L = sln.enter_context
        xin = [L(nc.sbuf_tensor(f"xin{i}", [128, D], F32)) for i in range(2)]
        xn_sb = [L(nc.sbuf_tensor(f"xnsb{i}", [128, D], BF16)) for i in range(2)]
        bn6 = [L(nc.sbuf_tensor(f"bn6_{i}", [128, 12], F32)) for i in range(2)]
        mvb = [L(nc.sbuf_tensor(f"mv{i}", [128, 2], F32)) for i in range(2)]
        rsd = [L(nc.sbuf_tensor(f"rstd{i}", [128, 2], F32)) for i in range(2)]

        free = {}

        def emit_layernorm(src_dram, xnT):
            base = allw()
            for i in range(2):
                for key in (("xin", i), ("bn", i), ("mv", i), ("rstd", i), ("xn", i)):
                    free.pop(key, None)
            for g in range(NG):
                b, t = divmod(g, NT)
                i = g % 2
                xi, xo, b6, m2, rs = xin[i], xn_sb[i], bn6[i], mvb[i], rsd[i]
                P.op("sync", lambda s, xi=xi, b=b, t=t: s.dma_start(
                    xi[:, :], src_dram[b, t * 128:(t + 1) * 128, :]),
                    inc=dctr(f"xin{i}"), k=16, waits=free.get(("xin", i), base))
                ld = dmadone(f"xin{i}")
                P.op("vector", lambda v, b6=b6, xi=xi: v.bn_stats(b6[:, 0:6], xi[:, 0:512]),
                     waits=ld + free.get(("bn", i), base))
                P.op("vector", lambda v, b6=b6, xi=xi: v.bn_stats(b6[:, 6:12], xi[:, 512:1024]),
                     inc=cv)
                P.op("vector", lambda v, b6=b6, m2=m2: v.bn_aggr(m2[:, :], b6[:, :]),
                     inc=cv, waits=[(cv, cv.n)] + free.get(("mv", i), base))
                free[("bn", i)] = [(cv, cv.n)]
                P.op("scalar", lambda s, m2=m2, rs=rs: s.activation(
                    rs[:, 0:1], m2[:, 1:2], AF.Ln, bias=cst[:, 0:1]), inc=cs,
                    waits=[(cv, cv.n)] + free.get(("rstd", i), base))
                P.op("scalar", lambda s, rs=rs: s.activation(
                    rs[:, 1:2], rs[:, 0:1], AF.Exp, scale=cst[:, 1:2]), inc=cs,
                    waits=[(cs, cs.n)])
                P.op("vector", lambda v, xo=xo, xi=xi, m2=m2, rs=rs: v.tensor_scalar(
                    xo[:, :], xi[:, :], m2[:, 0:1], rs[:, 1:2], ALU.subtract, ALU.mult),
                    inc=cv, waits=[(cs, cs.n)] + free.get(("xn", i), base))
                free[("xin", i)] = [(cv, cv.n)]
                free[("mv", i)] = [(cv, cv.n), (cs, cs.n)]
                free[("rstd", i)] = [(cv, cv.n), (cs, cs.n)]
                P.op("sync", lambda s, xo=xo, b=b, t=t: s.dma_start(
                    xn_hbm[b, t * 128:(t + 1) * 128, :], xo[:, :]),
                    inc=dctr(f"xnst{i}"), k=16, waits=[(cv, cv.n)])
                free[("xn", i)] = dmadone(f"xnst{i}")
            stored = dmadone("xnst0") + dmadone("xnst1") + base
            for it in range(NI):
                for b in range(NB):
                    off = it * XNT_B + b * T
                    P.op("sync", lambda s, it=it, b=b, off=off: s.dma_start_transpose(
                        xnT[:, off:off + T], xn_hbm[b, :, it * 128:(it + 1) * 128]),
                        inc=dctr("tp"), k=16, waits=stored)
            return dmadone("tp")

        def xnt_col(it, g):
            return it * XNT_B + g * 128

        if 0 in layers:
            sx = ExitStack()
            xnT = sx.enter_context(nc.sbuf_tensor("xnT", [128, NI * XNT_B], BF16, side="right"))
            sa0 = ExitStack()
            A = sa0.enter_context
            wf_sb = A(nc.sbuf_tensor("wf_sb", [128, DH], BF16, side="right"))
            bfrow_sb = A(nc.sbuf_tensor("bfrow_sb", [1, D], BF16, side="right"))
            bfcol_sb = A(nc.sbuf_tensor("bfcol_sb", [128, 1], F32, side="right"))
            boutrow_sb = A(nc.sbuf_tensor("boutrow_sb", [1, D], BF16, side="right"))
            kt_sb = A(nc.sbuf_tensor("kt_sb", [128, 8 * 512], BF16, side="right"))
            fk_sb = [A(nc.sbuf_tensor(f"fk{i}", [128, D], BF16, side="right")) for i in range(2)]
            vext = [A(nc.sbuf_tensor(f"vex0_{i}", [128, H * 66], BF16, side="right")) for i in range(2)]
            kv_sb = [A(nc.sbuf_tensor(f"kv{i}", [128, 8 * 65], BF16, side="right")) for i in range(2)]
            kvacc = A(nc.sbuf_tensor("kvacc", [128, 2 * 260], F32, side="right"))
            qt_sb = [A(nc.sbuf_tensor(f"qt{i}", [128, 512], BF16, side="right")) for i in range(2)]
            fq_sb = [A(nc.sbuf_tensor(f"fq{i}", [128, 512], BF16, side="right")) for i in range(2)]
            oe_sb = [A(nc.sbuf_tensor(f"oe{i}", [128, 512], BF16, side="right")) for i in range(2)]
            tln = [A(nc.sbuf_tensor(f"tln{i}", [1, 512], F32, side="right")) for i in range(2)]
            rrow = [A(nc.sbuf_tensor(f"rrow{i}", [1, 512], BF16, side="right")) for i in range(2)]
            apT = A(nc.sbuf_tensor("apT", [128, NI * T], BF16, side="right"))

            P.op("gpsimd", lambda g: g.dma_start(wf_sb[0:64, :], dram["l0_wf"][:, :]),
                 inc=cg, k=16)
            P.op("gpsimd", lambda g: g.dma_start(wf_sb[64:128, :], dram["l0_wf"][:, :]),
                 inc=cg, k=16)
            P.op("gpsimd", lambda g: g.dma_start(bfrow_sb[:, :], dram["l0_bf_row"][:, :]),
                 inc=cg, k=16)
            P.op("gpsimd", lambda g: g.dma_start(bfcol_sb[:, :], dram["l0_bf_col"][:, :]),
                 inc=cg, k=16)
            P.op("gpsimd", lambda g: g.dma_start(boutrow_sb[:, :], dram["l0_bout_row"][:, :]),
                 inc=cg, k=16)
            for i in range(2):
                P.op("vector", lambda v, i=i: v.memset(
                    vext[i][:, :].rearrange("p (h c) -> p h c", c=66)[:, :, 64:66], 1.0),
                    inc=cv)

            sa1 = ExitStack()
            wk_sb = sa1.enter_context(nc.sbuf_tensor("wk_sb", [128, NI * D], BF16, side="right"))
            wv_sb = sa1.enter_context(nc.sbuf_tensor("wv_sb", [128, NI * D], BF16, side="right"))
            for it in range(NI):
                wload = emit_wload([
                    (wk_sb[:, it * D:(it + 1) * D],
                     dram["l0_wqkv"][it * 128:(it + 1) * 128, D:2 * D], D),
                    (wv_sb[:, it * D:(it + 1) * D],
                     dram["l0_wqkv"][it * 128:(it + 1) * 128, 2 * D:3 * D], D)])

            wload = wload + [(cg, cg.n)]
            xnT_ready = emit_layernorm(x0, xnT)

            for b in range(NB):
                P.op("vector", lambda v: v.memset(kvacc[:, :], 0.0), inc=cv,
                     waits=free.get("kvacc", []))
                for c in range(4):
                    for kt in range(8):
                        ps = slot(kt % 2)
                        pw = free.get(("ps01", kt % 2), [])
                        for it in range(NI):
                            P.op("tensor", lambda t, ps=ps, it=it, kt=kt, c=c, b=b: t.matmul(
                                ps, wk_sb[:, it * D + kt * 128: it * D + (kt + 1) * 128],
                                xnT[:, xnt_col(it, b * NT) + c * 512: xnt_col(it, b * NT) + (c + 1) * 512],
                                start=(it == 0), stop=(it == NI - 1)),
                                inc=(ct if it == NI - 1 else None),
                                waits=(xnT_ready + wload + pw) if it == 0 else ())
                        P.op("scalar", lambda s, ps=ps, kt=kt: s.copy(
                            kt_sb[:, kt * 512:(kt + 1) * 512], ps), inc=cs,
                            waits=[(ct, ct.n)] + free.get(("ktsb", kt), []))
                        free[("ps01", kt % 2)] = [(cs, cs.n)]
                    ktev = [(cs, cs.n)]
                    for ti in range(4):
                        g = b * NT + c * 4 + ti
                        i = ti % 2
                        fkps0 = slot(2, cols=512)
                        fkps1 = slot(3, cols=512)
                        pw = free.get("fkps", [])
                        for h in range(H):
                            ps = fkps0 if h < 8 else fkps1
                            co = (h % 8) * 64
                            hb0 = (h % 2) * 64
                            P.op("tensor", lambda t, ps=ps, co=co, h=h, ti=ti, hb0=hb0: t.matmul(
                                ps[:, co:co + 64],
                                kt_sb[hb0:hb0 + 64,
                                      (h // 2) * 512 + ti * 128:(h // 2) * 512 + (ti + 1) * 128],
                                wf_sb[hb0:hb0 + 64, :], start=True, stop=False),
                                waits=(ktev + pw + wload) if h == 0 else ())
                            P.op("tensor", lambda t, ps=ps, co=co, h=h: t.matmul(
                                ps[:, co:co + 64], ones_sb[:, :],
                                bfrow_sb[:, h * 64:(h + 1) * 64], start=False, stop=True),
                                inc=(ct if h == H - 1 else None))
                        P.op("scalar", lambda s, i=i: s.activation(
                            fk_sb[i][:, :], pool[1][:, :], AF.Relu), inc=cs,
                            waits=[(ct, ct.n)] + free.get(("fksb", i), []))
                        free["fkps"] = [(cs, cs.n)]
                        fkdone = [(cs, cs.n)]
                        vps = pool[2]
                        pw = free.get("vps", [])
                        for half in range(2):
                            for it in range(NI):
                                P.op("tensor", lambda t, vps=vps, half=half, it=it, g=g: t.matmul(
                                    vps[:, half * 512:(half + 1) * 512],
                                    xnT[:, xnt_col(it, g): xnt_col(it, g) + 128],
                                    wv_sb[:, it * D + half * 512: it * D + (half + 1) * 512],
                                    start=(it == 0), stop=(it == NI - 1)),
                                    inc=(ct if it == NI - 1 and half == 1 else None),
                                    waits=pw if (half == 0 and it == 0) else ())
                        P.op("vector", lambda v, i=i: v.tensor_copy(
                            vext[i][:, 0:528].rearrange("p (h c) -> p h c", c=66)[:, :, 0:64],
                            pool[2][:, 0:512].rearrange("p (h c) -> p h c", c=64)),
                            waits=[(ct, ct.n)] + free.get(("vext", i), []))
                        P.op("vector", lambda v, i=i: v.tensor_copy(
                            vext[i][:, 528:1056].rearrange("p (h c) -> p h c", c=66)[:, :, 0:64],
                            pool[2][:, 512:1024].rearrange("p (h c) -> p h c", c=64)),
                            inc=cv)
                        free["vps"] = [(cv, cv.n)]
                        vdone = [(cv, cv.n)]
                        for h in range(H):
                            kvp = pool[3][(h % 2) * 64:(h % 2) * 64 + 64,
                                          (h // 2 % 4) * 65 + (h // 8) * 512:
                                          (h // 2 % 4) * 65 + (h // 8) * 512 + 65]
                            P.op("tensor", lambda t, kvp=kvp, h=h, i=i: t.matmul(
                                kvp, fk_sb[i][:, h * 64:(h + 1) * 64],
                                vext[i][:, h * 66:h * 66 + 65],
                                start=True, stop=True),
                                inc=(ct if h == H - 1 else None),
                                waits=(fkdone + vdone + free.get("kvps", [])) if h == 0 else ())
                        free[("fksb", i)] = [(ct, ct.n)]
                        free[("vext", i)] = [(ct, ct.n)]
                        P.op("vector", lambda v: v.tensor_tensor(
                            kvacc[:, 0:260], kvacc[:, 0:260], pool[3][:, 0:260], ALU.add),
                            waits=[(ct, ct.n), (cv, cv.n)])
                        P.op("vector", lambda v: v.tensor_tensor(
                            kvacc[:, 260:520], kvacc[:, 260:520], pool[3][:, 512:772], ALU.add),
                            inc=cv)
                        free["kvps"] = [(cv, cv.n)]
                    for kt in range(8):
                        free[("ktsb", kt)] = [(ct, ct.n)]
                P.op("vector", lambda v, b=b: v.tensor_copy(
                    kv_sb[b][:, :], kvacc[:, :]),
                    inc=cv, waits=[(cv, cv.n)] + free.get(("kvsb", b), []))
                free["kvacc"] = [(cv, cv.n)]
            kvdone = [(cv, cv.n)]

            sa1.close()
            sa2 = ExitStack()
            wq_sb = sa2.enter_context(nc.sbuf_tensor("wq_sb", [128, NI * D], BF16, side="right"))
            wout_sb = sa2.enter_context(nc.sbuf_tensor("wout_sb", [128, NI * D], BF16, side="right"))
            pa_done = [(ct, ct.n)]
            for it in range(NI):
                wload2 = emit_wload([
                    (wq_sb[:, it * D:(it + 1) * D],
                     dram["l0_wqkv"][it * 128:(it + 1) * 128, 0:D], D),
                    (wout_sb[:, it * D:(it + 1) * D],
                     dram["l0_wout"][it * 128:(it + 1) * 128, :], D)],
                    dst_free=pa_done)

            free[("ps01", 0)] = kvdone
            free[("ps01", 1)] = kvdone
            free["wops"] = kvdone
            pend_op = []
            for b in range(NB):
                for c in range(4):
                    for p in range(8):
                        i = p % 2
                        ps = slot(i)
                        for it in range(NI):
                            P.op("tensor", lambda t, ps=ps, it=it, p=p, c=c, b=b: t.matmul(
                                ps, wq_sb[:, it * D + p * 128: it * D + (p + 1) * 128],
                                xnT[:, xnt_col(it, b * NT) + c * 512: xnt_col(it, b * NT) + (c + 1) * 512],
                                start=(it == 0), stop=(it == NI - 1)),
                                inc=(ct if it == NI - 1 else None),
                                waits=(kvdone + wload2 + free.get(("ps01", i), [])) if it == 0 else ())
                        P.op("scalar", lambda s, ps=ps, i=i: s.copy(qt_sb[i][:, :], ps),
                             inc=cs, waits=[(ct, ct.n)] + free.get(("qtsb", i), []))
                        qtdone = [(cs, cs.n)]
                        fqp = slot(i)
                        P.op("tensor", lambda t, fqp=fqp, i=i: t.matmul(
                            fqp[0:64, :], wf_sb[0:64, :], qt_sb[i][0:64, :],
                            start=True, stop=True),
                            waits=qtdone)
                        P.op("tensor", lambda t, fqp=fqp, i=i: t.matmul(
                            fqp[64:128, :], wf_sb[64:128, :], qt_sb[i][64:128, :],
                            start=True, stop=True), inc=ct)
                        free[("qtsb", i)] = [(ct, ct.n)]
                        P.op("scalar", lambda s, fqp=fqp, i=i: s.activation(
                            fq_sb[i][:, :], fqp, AF.Relu, bias=bfcol_sb[:, 0:1]),
                            inc=cs, waits=[(ct, ct.n)] + free.get(("fqsb", i), []))
                        free[("ps01", i)] = [(cs, cs.n)]
                        fqdone = [(cs, cs.n)]
                        oep = slot(4)
                        n0 = slot(5, rows=1)
                        n1 = slot(6, rows=1)
                        pw = free.get("oeps", [])
                        for hh in range(2):
                            h = 2 * p + hh
                            kcol = (h // 2 % 4) * 65 + (h // 8) * 260
                            kbase = (h % 2) * 64
                            P.op("tensor", lambda t, oep=oep, hh=hh, kcol=kcol, kbase=kbase, i=i, b=b: t.matmul(
                                oep[hh * 64:(hh + 1) * 64, :],
                                kv_sb[b][kbase:kbase + 64, kcol:kcol + 64],
                                fq_sb[i][hh * 64:(hh + 1) * 64, :],
                                start=True, stop=True),
                                waits=(fqdone + pw) if hh == 0 else ())
                            nps = n0 if hh == 0 else n1
                            P.op("tensor", lambda t, nps=nps, hh=hh, kcol=kcol, kbase=kbase, i=i, b=b: t.matmul(
                                nps, kv_sb[b][kbase:kbase + 64, kcol + 64:kcol + 65],
                                fq_sb[i][hh * 64:(hh + 1) * 64, :],
                                start=True, stop=True),
                                inc=(ct if hh == 1 else None))
                        oedone = [(ct, ct.n)]
                        free[("fqsb", i)] = oedone
                        rbp = slot(7)
                        for hh in range(2):
                            nps = n0 if hh == 0 else n1
                            P.op("scalar", lambda s, nps=nps, hh=hh: s.activation(
                                tln[hh][:, :], nps, AF.Ln, bias=cst[0:1, 2:3]), inc=cs,
                                waits=oedone + free.get(("tln", hh), []))
                            P.op("scalar", lambda s, hh=hh: s.activation(
                                rrow[hh][:, :], tln[hh][:, :], AF.Exp, scale=cst[0:1, 3:4]),
                                inc=cs, waits=[(cs, cs.n)] + free.get(("rrow", hh), []))
                            free[("tln", hh)] = [(cs, cs.n)]
                        rdone = [(cs, cs.n)]
                        if pend_op:
                            pend_op.pop(0)()
                        pwrb = free.get("rbps", [])
                        for hh in range(2):
                            P.op("tensor", lambda t, rbp=rbp, hh=hh: t.matmul(
                                rbp[hh * 64:(hh + 1) * 64, :], ones_sb[0:1, 0:64],
                                rrow[hh][:, :], start=True, stop=True),
                                inc=(ct if hh == 1 else None),
                                waits=(rdone + pwrb) if hh == 0 else ())
                        free[("rrow", 0)] = [(ct, ct.n)]
                        free[("rrow", 1)] = [(ct, ct.n)]
                        P.op("scalar", lambda s, oep=oep, i=i: s.copy(oe_sb[i][:, :], oep),
                             inc=cs, waits=[(ct, ct.n)] + free.get(("oesb", i), []))
                        free["oeps"] = [(cs, cs.n)]
                        P.op("vector", lambda v, i=i, p=p, c=c, rbp=rbp: v.tensor_tensor(
                            apT[:, p * T + c * 512:p * T + (c + 1) * 512],
                            oe_sb[i][:, :], rbp[:, :], ALU.mult),
                            inc=cv, waits=[(cs, cs.n)] + free.get(("apT", p, c), []))
                        free[("oesb", i)] = [(cv, cv.n)]
                        free["rbps"] = [(cv, cv.n)]
                    apdone = [(cv, cv.n)]

                    def mk_op0(b, c, ti, apdone):
                        def run():
                            i = ti % 2
                            yp = pool[1]
                            P.op("sync", lambda s: s.dma_start(
                                xres[i][:, :], x0[b, (c * 4 + ti) * 128:(c * 4 + ti + 1) * 128, :]),
                                inc=dctr(f"xres{i}"), k=16, waits=free.get(("xres", i), []))
                            pw = free.get("wops", [])
                            for it in range(NI):
                                P.op("tensor", lambda t, it=it: t.matmul(
                                    yp[:, 0:512],
                                    apT[:, it * T + (c * 4 + ti) * 128: it * T + (c * 4 + ti + 1) * 128],
                                    wout_sb[:, it * D: it * D + 512],
                                    start=(it == 0), stop=False),
                                    waits=(apdone + pw + wload2) if it == 0 else ())
                            P.op("tensor", lambda t: t.matmul(
                                yp[:, 0:512], ones_sb[:, :], boutrow_sb[:, 0:512],
                                start=False, stop=True))
                        return run

                    def mk_op1(b, c, ti, apdone):
                        def run():
                            i = ti % 2
                            yp = pool[1]
                            xrdy = dmadone(f"xres{i}")
                            for it in range(NI):
                                P.op("tensor", lambda t, it=it: t.matmul(
                                    yp[:, 512:1024],
                                    apT[:, it * T + (c * 4 + ti) * 128: it * T + (c * 4 + ti + 1) * 128],
                                    wout_sb[:, it * D + 512: it * D + 1024],
                                    start=(it == 0), stop=False))
                            P.op("tensor", lambda t: t.matmul(
                                yp[:, 512:1024], ones_sb[:, :], boutrow_sb[:, 512:1024],
                                start=False, stop=True), inc=ct)
                            if ti == 3:
                                free.update({("apT", pp, c): [(ct, ct.n)] for pp in range(8)})
                            for half in range(2):
                                P.op("vector", lambda v, half=half: v.tensor_tensor(
                                    res_sb[i][:, half * 512:(half + 1) * 512],
                                    yp[:, half * 512:(half + 1) * 512],
                                    xres[i][:, half * 512:(half + 1) * 512], ALU.add),
                                    inc=(cv if half == 1 else None),
                                    waits=([(ct, ct.n)] + xrdy + free.get(("ressb", i), [])) if half == 0 else ())
                            free[("xres", i)] = [(cv, cv.n)]
                            free["wops"] = [(cv, cv.n)]
                            P.op("sync", lambda s: s.dma_start(
                                x_mid[b, (c * 4 + ti) * 128:(c * 4 + ti + 1) * 128, :],
                                res_sb[i][:, :]), inc=dctr(f"res{i}"), k=16, waits=[(cv, cv.n)])
                            free[("ressb", i)] = dmadone(f"res{i}")
                        return run

                    for ti in range(4):
                        pend_op.append(mk_op0(b, c, ti, apdone))
                        pend_op.append(mk_op1(b, c, ti, apdone))
            while pend_op:
                pend_op.pop(0)()
            attn_done = [(ct, ct.n), (cv, cv.n), (cs, cs.n)]
            sa2.close()
            sa0.close()

            mT_sb = L(nc.sbuf_tensor("mT_sb", [E0, NG * 128], BF16))
            sm = ExitStack()
            M = sm.enter_context
            rw_sb = M(nc.sbuf_tensor("rw_sb", [128, NI * E0], BF16, side="right"))
            rb_sb = M(nc.sbuf_tensor("rb_sb", [1, E0], BF16, side="right"))
            pen_sb = M(nc.sbuf_tensor("pen_sb", [E0, E0 * 128], BF16, side="right"))
            b1_sb = M(nc.sbuf_tensor("b1_sb", [128, 2 * E0], F32, side="right"))
            lmax = [M(nc.sbuf_tensor(f"lmax{i}", [128, 1], F32, side="right")) for i in range(2)]
            mf_sb = [M(nc.sbuf_tensor(f"mf{i}", [128, E0], F32, side="right")) for i in range(2)]
            w1_sb = [M(nc.sbuf_tensor(f"w1sb{i}", [128, NI * H0], BF16, side="right")) for i in range(2)]
            hT_sb = [M(nc.sbuf_tensor(f"hT0_{i}", [128, 512], BF16, side="right")) for i in range(4)]
            w1st = [M(nc.sbuf_tensor(f"w1st{i}", [128, 1024], F32, side="right")) for i in range(2)]
            for i in range(2):
                free[("w1st", i)] = attn_done
                free[("w1", i)] = attn_done
            rw_rdy = emit_wload([(rw_sb[:, :].rearrange("p (i e) -> p i e", e=E0),
                                  dram["l0_rwT"].rearrange("(i p) e -> p i e", p=128), NI * E0)],
                                dst_free=attn_done)
            P.op("gpsimd", lambda g: g.dma_start(rb_sb[:, :], dram["l0_rb_row"][:, :]),
                 inc=dctr("rcst"), k=16, waits=attn_done)
            P.op("gpsimd", lambda g: g.dma_start(pen_sb[:, :], dram["c_pen"][:, :]),
                 inc=dctr("rcst"), k=16)
            P.op("gpsimd", lambda g: g.dma_start(b1_sb[:, :], dram["l0_b1"][:, :]),
                 inc=dctr("rcst"), k=16)
            rld = dmadone("rcst") + rw_rdy
            for g in range(NG):
                i = g % 2
                lp = slot(i, cols=E0)
                for it in range(NI):
                    P.op("tensor", lambda t, lp=lp, it=it, g=g: t.matmul(
                        lp, xnT[:, xnt_col(it, g): xnt_col(it, g) + 128],
                        rw_sb[:, it * E0:(it + 1) * E0],
                        start=(it == 0), stop=False),
                        waits=(rld + free.get(("lps", i), [])) if it == 0 else ())
                P.op("tensor", lambda t, lp=lp: t.matmul(
                    lp, ones_sb[:, :], rb_sb[:, :], start=False, stop=True), inc=ct)
                P.op("vector", lambda v, lp=lp, i=i: v.reduce_max(
                    lmax[i][:, :], lp, axis=AX.X), inc=cv,
                    waits=[(ct, ct.n)] + free.get(("lmax", i), []))
                P.op("vector", lambda v, lp=lp, i=i: v.tensor_single_scalar(
                    mf_sb[i][:, :], lp, lmax[i][:, :], ALU.is_equal), inc=cv,
                    waits=[(cv, cv.n)] + free.get(("mf", i), []))
                free[("lps", i)] = [(cv, cv.n)]
                free[("lmax", i)] = [(cv, cv.n)]
                mtp = slot(2, rows=E0, cols=128)
                P.op("tensor", lambda t, mtp=mtp, i=i: t.transpose(
                    mtp, mf_sb[i][:, :], ident_sb[:, :]), inc=ct,
                    waits=[(cv, cv.n)] + free.get("mtp", []))
                free[("mf", i)] = [(ct, ct.n)]
                P.op("vector", lambda v, mtp=mtp, g=g: v.tensor_copy(
                    mT_sb[:, g * 128:(g + 1) * 128], mtp), inc=cv,
                    waits=[(ct, ct.n)])
                free["mtp"] = [(cv, cv.n)]
            mtdone = [(cv, cv.n)]

            def load_w1(e):
                i = e % 2
                return emit_wload(
                    [(w1_sb[i][:, a * 1024:(a + 1) * 1024].rearrange("p (i f) -> p i f", f=H0),
                      dram["l0_w1"][e][a * 512:(a + 1) * 512, :].rearrange("(i p) f -> p i f", p=128), 1024)
                     for a in range(2)],
                    dst_free=free.get(("w1", i), []), ring=w1st, rkey="w1st", queue="sync")

            w1rdys = {0: load_w1(0)}
            for e in range(E0):
                i = e % 2
                if e + 1 < E0:
                    w1rdys[e + 1] = load_w1(e + 1)
                w1rdy = w1rdys[e]
                for fh in range(2):
                    for g2 in range(NC2):
                        j = (fh * NC2 + g2) % 2
                        hp = slot(j)
                        for it in range(NI):
                            P.op("tensor", lambda t, hp=hp, it=it, i=i, fh=fh, g2=g2: t.matmul(
                                hp, w1_sb[i][:, it * H0 + fh * 128: it * H0 + fh * 128 + 128],
                                xnT[:, it * XNT_B + g2 * 512: it * XNT_B + (g2 + 1) * 512],
                                start=(it == 0), stop=False),
                                waits=(w1rdy + mtdone + free.get(("hps", j), [])) if it == 0 else ())
                        P.op("tensor", lambda t, hp=hp, e=e, g2=g2: t.matmul(
                            hp, pen_sb[:, e * 128:(e + 1) * 128],
                            mT_sb[:, g2 * 512:(g2 + 1) * 512],
                            start=False, stop=True), inc=ct)
                        k = (fh * NC2 + g2) % 4
                        P.op("scalar", lambda s, hp=hp, k=k, e=e, fh=fh: s.activation(
                            hT_sb[k][:, :], hp, AF.Relu, bias=b1_sb[:, 2 * e + fh:2 * e + fh + 1]),
                            inc=cs, waits=[(ct, ct.n)] + free.get(("hT", k), []))
                        free[("hps", j)] = [(cs, cs.n)]
                        P.op("scalar", lambda s, k=k, e=e, fh=fh, g2=g2: s.dma_start(
                            hb[g2 * 4:(g2 + 1) * 4, :, 2 * e + fh, :].rearrange("a p n -> p a n"),
                            hT_sb[k][:, :].rearrange("p (ti n) -> p ti n", n=128)),
                            inc=dctr(f"hbw{k}"), k=16, waits=[(cs, cs.n)])
                        free[("hT", k)] = dmadone(f"hbw{k}")
                free[("w1", i)] = [(ct, ct.n)]
            hbdone = (dmadone("hbw0") + dmadone("hbw1") + dmadone("hbw2")
                      + dmadone("hbw3") + [(ct, ct.n)])
            sm.close()
            sx.close()

            se = ExitStack()
            Me = se.enter_context
            w2_sb = Me(nc.sbuf_tensor("w2_sb", [128, 2 * E0 * D], BF16, side="right"))
            b2_sb = Me(nc.sbuf_tensor("b2_sb", [E0, D], BF16, side="right"))
            hall = [Me(nc.sbuf_tensor(f"hall{j}", [128, 2 * E0 * 128], BF16, side="right"))
                    for j in range(2)]
            w2lds = []
            for e in range(E0):
                w2lds.append(emit_wload(
                    [(w2_sb[:, (e * 2 + fh) * D:(e * 2 + fh + 1) * D],
                      dram["l0_w2"][e][fh * 128:(fh + 1) * 128, :], D)
                     for fh in range(2)],
                    waits=(hbdone if e == 0 else ()),
                    dst_free=(hbdone if e == 0 else ()),
                    cast_eng="scalar"))
            P.op("gpsimd", lambda g: g.dma_start(b2_sb[:, :], dram["l0_b2"][:, :]),
                 inc=dctr("b2"), k=16, waits=hbdone)
            w2rdy = dmadone("b2")
            def load_hall(g):
                j = g % 2
                P.op("sync", lambda s, g=g, j=j: s.dma_start(
                    hall[j][:, :], hb[g].rearrange("p k n -> p (k n)")),
                    inc=dctr(f"hall{j}"), k=16,
                    waits=(hbdone if g < 2 else []) + free.get(("hall", j), []))
                return dmadone(f"hall{j}")

            hrdys = {0: load_hall(0)}
            for g in range(NG):
                i = g % 2
                if g + 1 < NG:
                    hrdys[g + 1] = load_hall(g + 1)
                hrdy = hrdys[g]
                yp = pool[0] if i == 0 else pool[1]
                pw = free.get(("yps", i), [])
                for half in range(2):
                    for k in range(2 * E0):
                        kw = w2lds[k // 2] if (k % 2 == 0 and half == 0) else []
                        P.op("tensor", lambda t, yp=yp, half=half, k=k, i=i: t.matmul(
                            yp[:, half * 512:(half + 1) * 512],
                            hall[i][:, k * 128:(k + 1) * 128],
                            w2_sb[:, k * D + half * 512: k * D + (half + 1) * 512],
                            start=(k == 0), stop=False),
                            waits=(hrdy + w2rdy + pw + kw) if (half == 0 and k == 0) else kw)
                    P.op("tensor", lambda t, yp=yp, half=half, g=g: t.matmul(
                        yp[:, half * 512:(half + 1) * 512],
                        mT_sb[:, g * 128:(g + 1) * 128],
                        b2_sb[:, half * 512:(half + 1) * 512],
                        start=False, stop=True),
                        inc=(ct if half == 1 else None))
                free[("hall", i)] = [(ct, ct.n)]
                mmdone = [(ct, ct.n)]
                b, t0 = divmod(g, NT)
                P.op("sync", lambda s, i=i, b=b, t0=t0: s.dma_start(
                    xres[i][:, :], x_mid[b, t0 * 128:(t0 + 1) * 128, :]),
                    inc=dctr(f"xres{i}"), k=16, waits=free.get(("xres", i), []))
                xrdy2 = dmadone(f"xres{i}")
                for half in range(2):
                    P.op("vector", lambda v, i=i, yp=yp, half=half: v.scalar_tensor_tensor(
                        res_sb[i][:, half * 512:(half + 1) * 512],
                        yp[:, half * 512:(half + 1) * 512], 0.0,
                        xres[i][:, half * 512:(half + 1) * 512],
                        ALU.max, ALU.add),
                        inc=(cv if half == 1 else None),
                        waits=(mmdone + xrdy2 + free.get(("ressb", i), [])) if half == 0 else ())
                free[("yps", i)] = [(cv, cv.n)]
                free[("xres", i)] = [(cv, cv.n)]
                P.op("sync", lambda s, i=i, b=b, t0=t0: s.dma_start(
                    x_mid[b, t0 * 128:(t0 + 1) * 128, :], res_sb[i][:, :]),
                    inc=dctr(f"res{i}"), k=16, waits=[(cv, cv.n)])
                free[("ressb", i)] = dmadone(f"res{i}")
                if 1 in layers:
                    b6, m2, rs, xo = bn6[i], mvb[i], rsd[i], xn_sb[i]
                    P.op("vector", lambda v, b6=b6, i=i: v.bn_stats(
                        b6[:, 0:6], res_sb[i][:, 0:512]),
                        waits=[(cv, cv.n)] + free.get(("bn", i), []))
                    P.op("vector", lambda v, b6=b6, i=i: v.bn_stats(
                        b6[:, 6:12], res_sb[i][:, 512:1024]), inc=cv)
                    P.op("vector", lambda v, b6=b6, m2=m2: v.bn_aggr(
                        m2[:, :], b6[:, :]), inc=cv,
                        waits=[(cv, cv.n)] + free.get(("mv", i), []))
                    free[("bn", i)] = [(cv, cv.n)]
                    P.op("scalar", lambda s, m2=m2, rs=rs: s.activation(
                        rs[:, 0:1], m2[:, 1:2], AF.Ln, bias=cst[:, 0:1]), inc=cs,
                        waits=[(cv, cv.n)] + free.get(("rstd", i), []))
                    P.op("scalar", lambda s, rs=rs: s.activation(
                        rs[:, 1:2], rs[:, 0:1], AF.Exp, scale=cst[:, 1:2]), inc=cs,
                        waits=[(cs, cs.n)])
                    P.op("vector", lambda v, xo=xo, i=i, m2=m2, rs=rs: v.tensor_scalar(
                        xo[:, :], res_sb[i][:, :], m2[:, 0:1], rs[:, 1:2],
                        ALU.subtract, ALU.mult),
                        inc=cv, waits=[(cs, cs.n)] + free.get(("xn", i), []))
                    free[("mv", i)] = [(cv, cv.n), (cs, cs.n)]
                    free[("rstd", i)] = [(cv, cv.n), (cs, cs.n)]
                    free[("ressb", i)] = dmadone(f"res{i}") + [(cv, cv.n)]
                    P.op("sync", lambda s, xo=xo, b=b, t0=t0: s.dma_start(
                        xn_hbm[b, t0 * 128:(t0 + 1) * 128, :], xo[:, :]),
                        inc=dctr(f"xnst{i}"), k=16, waits=[(cv, cv.n)])
                    free[("xn", i)] = dmadone(f"xnst{i}")
            se.close()

        if 1 in layers:
            if 0 in layers:
                sx1 = ExitStack()
                xnT1 = sx1.enter_context(
                    nc.sbuf_tensor("xnT1", [128, NI * XNT_B], BF16, side="right"))
                stored1 = [(ct, ct.n)] + dmadone("xnst0") + dmadone("xnst1")
                vb_ready = []
                for tb in range(NB):
                    tpc = dctr(f"tp{tb}f")
                    for it in range(NI):
                        off = it * XNT_B + tb * T
                        P.op("sync", lambda s, it=it, tb=tb, off=off: s.dma_start_transpose(
                            xnT1[:, off:off + T], xn_hbm[tb, :, it * 128:(it + 1) * 128]),
                            inc=tpc, k=16, waits=stored1)
                    vb_ready.append([(tpc, tpc.n)])
                xnT_ready1 = vb_ready[0] + vb_ready[1]
            else:
                sx1 = ExitStack()
                xnT1 = sx1.enter_context(
                    nc.sbuf_tensor("xnT1", [128, NI * XNT_B], BF16, side="right"))
                xnT_ready1 = emit_layernorm(x_mid, xnT1)
                vb_ready = [xnT_ready1, xnT_ready1]
            ln1_done = xnT_ready1 + [(ct, ct.n), (cv, cv.n), (cs, cs.n)]
            sln.close()

            sga = ExitStack()
            G = sga.enter_context
            vext1 = G(nc.sbuf_tensor("vext1", [128, NT * H * 66], BF16, side="right"))
            catT = G(nc.sbuf_tensor("catT", [128, NI * T], BF16, side="right"))
            wbuf = G(nc.sbuf_tensor("wbuf", [128, NI * D], BF16, side="right"))
            wqs = [wbuf[:, i * 1024:(i + 1) * 1024] for i in range(2)]
            wks = [wbuf[:, 2048 + i * 1024:2048 + (i + 1) * 1024] for i in range(2)]
            qtp = [G(nc.sbuf_tensor(f"qtp{i}", [128, T], BF16, side="right")) for i in range(2)]
            ktp = [G(nc.sbuf_tensor(f"ktp{i}", [128, T], BF16, side="right")) for i in range(2)]
            probs = [G(nc.sbuf_tensor(f"probs{i}", [128, 512], BF16, side="right")) for i in range(2)]
            ysb = [G(nc.sbuf_tensor("ysb0", [128, 512], BF16, side="right"))] * 2
            tl1 = [G(nc.sbuf_tensor("tl1_0", [1, 512], F32, side="right"))] * 2
            rr1 = [G(nc.sbuf_tensor("rr1_0", [1, 512], BF16, side="right"))] * 2
            abias_sb = G(nc.sbuf_tensor("abias_sb", [128, H * 16], F32, side="right"))
            abias2_sb = G(nc.sbuf_tensor("abias2_sb", [128, 96], F32, side="right"))
            abiasw_sb = G(nc.sbuf_tensor("abiasw_sb", [128, 440], F32, side="right"))
            maskw_sb = G(nc.sbuf_tensor("maskw_sb", [128, 896], BF16, side="right"))
            maskm_sb = G(nc.sbuf_tensor("maskm_sb", [128, 128], BF16, side="right"))
            bqT_sb = G(nc.sbuf_tensor("bqT_sb", [128, NI], F32, side="right"))
            bkT_sb = G(nc.sbuf_tensor("bkT_sb", [128, NI], F32, side="right"))
            bvrow_sb = G(nc.sbuf_tensor("bvrow_sb", [1, D], BF16, side="right"))
            borow_sb = G(nc.sbuf_tensor("borow_sb", [1, D], BF16, side="right"))
            P.op("gpsimd", lambda g: g.dma_start(abias_sb[:, :], dram["c_abias"][:, :]),
                 inc=dctr("l1cst"), k=16, waits=ln1_done)
            P.op("gpsimd", lambda g: g.dma_start(maskm_sb[:, :], dram["c_maskm"][:, :]),
                 inc=dctr("l1cst"), k=16)
            P.op("gpsimd", lambda g: g.dma_start(abias2_sb[:, :], dram["c_abias2"][:, :]),
                 inc=dctr("l1cst"), k=16)
            P.op("gpsimd", lambda g: g.dma_start(abiasw_sb[:, :], dram["c_abiasw"][:, :]),
                 inc=dctr("l1cst"), k=16)
            P.op("gpsimd", lambda g: g.dma_start(maskw_sb[:, :], dram["c_maskw"][:, :]),
                 inc=dctr("l1cst"), k=16)
            P.op("gpsimd", lambda g: g.dma_start(bqT_sb[:, :], dram["l1_bqT"][:, :]),
                 inc=dctr("l1cst"), k=16)
            P.op("gpsimd", lambda g: g.dma_start(bkT_sb[:, :], dram["l1_bkT"][:, :]),
                 inc=dctr("l1cst"), k=16)
            P.op("gpsimd", lambda g: g.dma_start(bvrow_sb[:, :], dram["l1_bv_row"][:, :]),
                 inc=dctr("l1cst"), k=16)
            P.op("gpsimd", lambda g: g.dma_start(borow_sb[:, :], dram["l1_bo_row"][:, :]),
                 inc=dctr("l1cst"), k=16)
            c1ld = dmadone("l1cst")
            P.op("vector", lambda v: v.memset(
                vext1[:, :].rearrange("p (t h c) -> p t h c", h=H, c=66)[:, :, :, 64:66], 1.0),
                inc=cv, waits=ln1_done)
            WOFF = [0, 4, 12, 24]

            for b in range(NB):
                for it in range(NI):
                    wvrdy = emit_wload(
                        [(wbuf[:, it * D:(it + 1) * D],
                          dram["l1_wvT"][it * 128:(it + 1) * 128, :], D)],
                        dst_free=(free.get("wbuf", c1ld) if it == 0 else ()))
                for ti in range(NT):
                    g0 = b * NT + ti
                    vps = pool[2] if ti % 2 == 0 else pool[3]
                    pw = free.get(("vps1", ti % 2), [])
                    for half in range(2):
                        for it in range(NI):
                            P.op("tensor", lambda t, vps=vps, half=half, it=it, g0=g0: t.matmul(
                                vps[:, half * 512:(half + 1) * 512],
                                xnT1[:, xnt_col(it, g0): xnt_col(it, g0) + 128],
                                wbuf[:, it * D + half * 512: it * D + (half + 1) * 512],
                                start=(it == 0), stop=False),
                                waits=(wvrdy + pw + vb_ready[b]) if (half == 0 and it == 0) else ())
                        P.op("tensor", lambda t, vps=vps, half=half: t.matmul(
                            vps[:, half * 512:(half + 1) * 512], ones_sb[:, :],
                            bvrow_sb[:, half * 512:(half + 1) * 512],
                            start=False, stop=True),
                            inc=(ct if half == 1 else None))
                    P.op("vector", lambda v, vps=vps, ti=ti: v.tensor_copy(
                        vext1[:, ti * 1056:ti * 1056 + 528].rearrange("p (h c) -> p h c", c=66)[:, :, 0:64],
                        vps[:, 0:512].rearrange("p (h c) -> p h c", c=64)),
                        waits=[(ct, ct.n)] + free.get("vext1", []))
                    P.op("vector", lambda v, vps=vps, ti=ti: v.tensor_copy(
                        vext1[:, ti * 1056 + 528:ti * 1056 + 1056].rearrange("p (h c) -> p h c", c=66)[:, :, 0:64],
                        vps[:, 512:1024].rearrange("p (h c) -> p h c", c=64)),
                        inc=cv)
                    free[("vps1", ti % 2)] = [(cv, cv.n)]
                vdone1 = [(cv, cv.n)]

                def load_wqk(p):
                    i = p % 2
                    wqw = free.get(("wqs", i), []) + ([(ct, ct.n)] if p < 2 else [])
                    return emit_wload(
                        [(wqs[i].rearrange("p (i o) -> p i o", o=128),
                          dram["l1_wqT"][:, p * 128:(p + 1) * 128].rearrange("(i p) o -> p i o", p=128), 1024),
                         (wks[i].rearrange("p (i o) -> p i o", o=128),
                          dram["l1_wkT"][:, p * 128:(p + 1) * 128].rearrange("(i p) o -> p i o", p=128), 1024)],
                        dst_free=wqw)

                wqrdys = {0: load_wqk(0)}
                for p in range(8):
                    i = p % 2
                    if p + 1 < 8:
                        wqrdys[p + 1] = load_wqk(p + 1)
                    wrdy = wqrdys[p]
                    for c in range(4):
                        ps = slot(c % 2)
                        for it in range(NI):
                            P.op("tensor", lambda t, ps=ps, it=it, i=i, c=c, b=b: t.matmul(
                                ps, wqs[i].rearrange("p (i o) -> p i o", o=128)[:, it, :],
                                xnT1[:, it * XNT_B + b * T + c * 512: it * XNT_B + b * T + (c + 1) * 512],
                                start=(it == 0), stop=(it == NI - 1)),
                                inc=(ct if it == NI - 1 else None),
                                waits=(wrdy + vdone1 + free.get(("ps01a", c % 2), [])) if it == 0 else ())
                        P.op("vector", lambda v, ps=ps, i=i, c=c, p=p: v.tensor_scalar(
                            qtp[i][:, c * 512:(c + 1) * 512], ps,
                            bqT_sb[:, p:p + 1], 0.125, ALU.add, ALU.mult),
                            inc=cv, waits=[(ct, ct.n)] + free.get(("qtp", i), []))
                        free[("ps01a", c % 2)] = [(cv, cv.n)]
                    for c in range(4):
                        ps = slot(c % 2)
                        for it in range(NI):
                            P.op("tensor", lambda t, ps=ps, it=it, i=i, c=c, b=b: t.matmul(
                                ps, wks[i].rearrange("p (i o) -> p i o", o=128)[:, it, :],
                                xnT1[:, it * XNT_B + b * T + c * 512: it * XNT_B + b * T + (c + 1) * 512],
                                start=(it == 0), stop=(it == NI - 1)),
                                inc=(ct if it == NI - 1 else None),
                                waits=free.get(("ps01a", c % 2), []) if it == 0 else ())
                        P.op("scalar", lambda s, ps=ps, i=i, c=c, p=p: s.add(
                            ktp[i][:, c * 512:(c + 1) * 512], ps,
                            bkT_sb[:, p:p + 1]),
                            inc=cs, waits=[(ct, ct.n)] + free.get(("ktp", i), []))
                        free[("ps01a", c % 2)] = [(cs, cs.n)]
                    prdone = [(cv, cv.n), (cs, cs.n)]
                    free[("wqs", i)] = [(ct, ct.n)]
                    for hh in range(2):
                        h = 2 * p + hh
                        hb0 = hh * 64
                        pend_div_s = []
                        pend_div_t = []
                        divstate = {}

                        def emit_div_s(w, hh=hh):
                            ysl = pool[2] if w % 2 == 0 else pool[0]
                            nsrc = (ysl[64:65, 0:512] if hh == 0
                                    else ysl[0:1, 512:1024])
                            P.op("scalar", lambda s, nsrc=nsrc: s.activation(
                                tl1[0][:, :], nsrc, AF.Ln), inc=cs,
                                waits=free.get("ydone", []) + free.get("tl1f", []))
                            P.op("scalar", lambda s: s.activation(
                                rr1[0][:, :], tl1[0][:, :], AF.Exp, scale=cst[0:1, 3:4]),
                                inc=cs, waits=[(cs, cs.n)] + free.get("rr1f", []))
                            free["tl1f"] = [(cs, cs.n)]
                            divstate[w] = [(cs, cs.n)]

                        def emit_div(w, hh=hh, hb0=hb0, p=p, i=i):
                            ysl = pool[2] if w % 2 == 0 else pool[0]
                            rdone1 = divstate.pop(w)
                            rbp = pool[3][hb0:hb0 + 64, 0:512]
                            P.op("tensor", lambda t, rbp=rbp: t.matmul(
                                rbp, ones_sb[0:1, 0:64], rr1[0][:, :],
                                start=True, stop=True), inc=ct,
                                waits=rdone1 + free.get("rbps1", []))
                            free["rr1f"] = [(ct, ct.n)]
                            ysrc = ysl[hb0:hb0 + 64, 0:512]
                            P.op("vector", lambda v, ysrc=ysrc, hb0=hb0: v.tensor_copy(
                                ysb[0][hb0:hb0 + 64, :], ysrc), inc=cv,
                                waits=[(ct, ct.n)] + free.get("ysbf", []))
                            free[("yps", w % 2)] = [(cv, cv.n)]
                            P.op("vector", lambda v, hb0=hb0, p=p, w=w, rbp=rbp: v.tensor_tensor(
                                catT[hb0:hb0 + 64, p * T + w * 512:p * T + (w + 1) * 512],
                                ysb[0][hb0:hb0 + 64, :], rbp, ALU.mult),
                                inc=cv, waits=[(cv, cv.n)])
                            free["ysbf"] = [(cv, cv.n)]
                            free["rbps1"] = [(cv, cv.n)]

                        for w in range(4):
                            nkt = 4 * w + 4
                            ysl = pool[2] if w % 2 == 0 else pool[0]
                            tierC = h >= 5
                            pdone = {}

                            def emit_se(kt, w=w, hb0=hb0, h=h, i=i, tierC=tierC, pdone=pdone):
                                sc = slot(2 + kt % 2)
                                P.op("tensor", lambda t, sc=sc, kt=kt, w=w, hb0=hb0, i=i: t.matmul(
                                    sc, ktp[i][hb0:hb0 + 64, kt * 128:(kt + 1) * 128],
                                    qtp[i][hb0:hb0 + 64, w * 512:(w + 1) * 512],
                                    start=True, stop=True),
                                    inc=ct,
                                    waits=prdone + free.get(("scps", kt % 2), []))
                                scdep = [(ct, ct.n)]
                                pfree = free.get(("probs", kt % 2), [])
                                pb = probs[kt % 2]
                                cvd = False
                                if tierC:
                                    if kt >= 4 * w:
                                        j = kt - 4 * w
                                        P.op("vector", lambda v, sc=sc, j=j: v.tensor_tensor(
                                            sc, sc, maskw_sb[:, (3 - j) * 128:(3 - j) * 128 + 512],
                                            ALU.add), inc=cv, waits=scdep)
                                        scdep = [(cv, cv.n)]
                                        cvd = True
                                    col = (h - 5) * 40 + WOFF[w] + kt
                                    P.op("scalar", lambda s, sc=sc, pb=pb, col=col: s.activation(
                                        pb[:, :], sc, AF.Exp,
                                        bias=abiasw_sb[:, col:col + 1]),
                                        inc=cs, waits=scdep + pfree)
                                else:
                                    steep = h < 3
                                    for qt in range(4):
                                        qtg = 4 * w + qt
                                        if kt > qtg:
                                            P.op("vector", lambda v, pb=pb, qt=qt: v.memset(
                                                pb[:, qt * 128:(qt + 1) * 128], 0.0),
                                                inc=cv, waits=scdep + pfree)
                                            cvd = True
                                            continue
                                        if kt == qtg:
                                            P.op("vector", lambda v, sc=sc, qt=qt: v.tensor_tensor(
                                                sc[:, qt * 128:(qt + 1) * 128],
                                                sc[:, qt * 128:(qt + 1) * 128],
                                                maskm_sb[:, :], ALU.add),
                                                inc=cv, waits=scdep)
                                            scdep2 = [(cv, cv.n)]
                                            cvd = True
                                        else:
                                            scdep2 = scdep
                                        if steep:
                                            for q6 in range(2):
                                                jj = (8 * w + 2 * qt + q6) - 2 * kt
                                                c0 = qt * 128 + q6 * 64
                                                if jj < 0:
                                                    P.op("vector", lambda v, pb=pb, c0=c0: v.memset(
                                                        pb[:, c0:c0 + 64], 0.0),
                                                        inc=cv, waits=scdep2 + pfree)
                                                    cvd = True
                                                    continue
                                                col = h * 32 + jj
                                                P.op("scalar", lambda s, sc=sc, pb=pb, c0=c0, col=col: s.activation(
                                                    pb[:, c0:c0 + 64],
                                                    sc[:, c0:c0 + 64], AF.Exp,
                                                    bias=abias2_sb[:, col:col + 1]),
                                                    inc=cs, waits=scdep2 + pfree)
                                        else:
                                            col = h * 16 + (qtg - kt)
                                            P.op("scalar", lambda s, sc=sc, pb=pb, qt=qt, col=col: s.activation(
                                                pb[:, qt * 128:(qt + 1) * 128],
                                                sc[:, qt * 128:(qt + 1) * 128], AF.Exp,
                                                bias=abias_sb[:, col:col + 1]),
                                                inc=cs, waits=scdep2 + pfree)
                                free[("scps", kt % 2)] = [(cs, cs.n)]
                                pdone[kt] = [(cs, cs.n)] + ([(cv, cv.n)] if cvd else [])

                            emit_se(0)
                            for kt in range(nkt):
                                if kt + 1 < nkt:
                                    emit_se(kt + 1)
                                elif pend_div_t:
                                    pend_div_t.pop(0)()
                                if kt == 0 and pend_div_s:
                                    pend_div_s.pop(0)()
                                w0 = (pdone[kt] + prdone + free.get(("yps", w % 2), [])
                                      ) if kt == 0 else pdone[kt]
                                if hh == 0:
                                    P.op("tensor", lambda t, kt=kt, h=h, w=w, nkt=nkt, ysl=ysl: t.matmul(
                                        ysl[0:65, 0:512],
                                        vext1[:, kt * 1056 + h * 66: kt * 1056 + h * 66 + 65],
                                        probs[kt % 2][:, :],
                                        start=(kt == 0), stop=(kt == nkt - 1)),
                                        inc=ct, waits=w0)
                                else:
                                    P.op("tensor", lambda t, kt=kt, h=h, w=w, nkt=nkt, ysl=ysl: t.matmul(
                                        ysl[64:128, 0:512],
                                        vext1[:, kt * 1056 + h * 66: kt * 1056 + h * 66 + 64],
                                        probs[kt % 2][:, :],
                                        start=(kt == 0), stop=(kt == nkt - 1)),
                                        waits=w0)
                                    P.op("tensor", lambda t, kt=kt, h=h, w=w, nkt=nkt, ysl=ysl: t.matmul(
                                        ysl[0:1, 512:1024],
                                        vext1[:, kt * 1056 + h * 66 + 64: kt * 1056 + h * 66 + 65],
                                        probs[kt % 2][:, :],
                                        start=(kt == 0), stop=(kt == nkt - 1)),
                                        inc=ct)
                                free[("probs", kt % 2)] = [(ct, ct.n)]
                            free["ydone"] = [(ct, ct.n)]
                            pend_div_s.append(lambda w=w: emit_div_s(w))
                            pend_div_t.append(lambda w=w: emit_div(w))
                        while pend_div_s:
                            pend_div_s.pop(0)()
                        while pend_div_t:
                            pend_div_t.pop(0)()
                    free[("ps01a", 0)] = [(cv, cv.n), (cs, cs.n)]
                    free[("ps01a", 1)] = [(cv, cv.n), (cs, cs.n)]
                    free[("qtp", i)] = [(ct, ct.n)]
                    free[("ktp", i)] = [(ct, ct.n)]
                catdone = [(cv, cv.n)]
                free["vext1"] = [(ct, ct.n)]

                hd_done = [(ct, ct.n)]
                for it in range(NI):
                    wordy = emit_wload(
                        [(wbuf[:, it * D:(it + 1) * D],
                          dram["l1_woT"][it * 128:(it + 1) * 128, :], D)],
                        dst_free=(hd_done if it == 0 else ()))
                for ti in range(NT):
                    i = ti % 2
                    yp = pool[1] if ti % 2 == 0 else pool[0]
                    P.op("sync", lambda s, i=i, b=b, ti=ti: s.dma_start(
                        xres[i][:, :], x_mid[b, ti * 128:(ti + 1) * 128, :]),
                        inc=dctr(f"xres{i}"), k=16, waits=free.get(("xres", i), []))
                    xrdy = dmadone(f"xres{i}")
                    pw = free.get(("wops1", ti % 2), [])
                    for half in range(2):
                        for it in range(NI):
                            P.op("tensor", lambda t, yp=yp, half=half, it=it, ti=ti: t.matmul(
                                yp[:, half * 512:(half + 1) * 512],
                                catT[:, it * T + ti * 128: it * T + (ti + 1) * 128],
                                wbuf[:, it * D + half * 512: it * D + (half + 1) * 512],
                                start=(it == 0), stop=False),
                                waits=(catdone + pw + wordy) if (half == 0 and it == 0) else ())
                        P.op("tensor", lambda t, yp=yp, half=half: t.matmul(
                            yp[:, half * 512:(half + 1) * 512], ones_sb[:, :],
                            borow_sb[:, half * 512:(half + 1) * 512],
                            start=False, stop=True),
                            inc=(ct if half == 1 else None))
                    for half in range(2):
                        P.op("vector", lambda v, i=i, half=half, yp=yp: v.tensor_tensor(
                            res_sb[i][:, half * 512:(half + 1) * 512],
                            yp[:, half * 512:(half + 1) * 512],
                            xres[i][:, half * 512:(half + 1) * 512], ALU.add),
                            inc=(cv if half == 1 else None),
                            waits=([(ct, ct.n)] + xrdy + free.get(("ressb", i), [])) if half == 0 else ())
                    free[("xres", i)] = [(cv, cv.n)]
                    free[("wops1", ti % 2)] = [(cv, cv.n)]
                    P.op("sync", lambda s, i=i, b=b, ti=ti: s.dma_start(
                        x_att[b, ti * 128:(ti + 1) * 128, :], res_sb[i][:, :]),
                        inc=dctr(f"res{i}"), k=16, waits=[(cv, cv.n)])
                    free[("ressb", i)] = dmadone(f"res{i}")
                free["wbuf"] = [(ct, ct.n)]
            attn1_done = [(ct, ct.n), (cv, cv.n), (cs, cs.n)]
            sga.close()

            sf = ExitStack()
            FT = sf.enter_context
            NFT = DFF // 128
            hT1 = FT(nc.sbuf_tensor("hT1", [128, NFT * 512], BF16, side="right"))
            w1c = [FT(nc.sbuf_tensor(f"w1c{i}", [128, NI * 128], BF16, side="right")) for i in range(4)]
            w3c = [FT(nc.sbuf_tensor(f"w3c{i}", [128, NI * 128], BF16, side="right")) for i in range(4)]
            w2c = [FT(nc.sbuf_tensor(f"w2c{i}", [128, D], BF16, side="right")) for i in range(4)]
            gsb = [FT(nc.sbuf_tensor(f"gsb{i}", [128, 512], BF16, side="right")) for i in range(4)]
            wstf = [FT(nc.sbuf_tensor(f"wstf{i}", [128, 1024], F32, side="right")) for i in range(4)]
            wsts = [FT(nc.sbuf_tensor(f"wsts{i}", [128, 1024], F32, side="right")) for i in range(4)]
            for i in range(4):
                free[("wstf", i)] = attn1_done
                free[("wsts", i)] = attn1_done
            xnv = xnT1[:, :].rearrange("p (i b t) -> p i b t", b=NB, t=T)
            def load_ft(e, ft):
                i = (e * NFT + ft) % 4
                df = (attn1_done if (e == 0 and ft < 4) else [])
                r1 = emit_wload(
                    [(w1c[i][:, :].rearrange("p (i f) -> p i f", f=128),
                      dram["l1_w1"][e][:, ft * 128:(ft + 1) * 128].rearrange("(i p) f -> p i f", p=128), 1024)],
                    dst_free=df + free.get(("w1c", i), []),
                    ring=wstf, rkey="wstf", queue="sync")
                r3 = emit_wload(
                    [(w3c[i][:, :].rearrange("p (i f) -> p i f", f=128),
                      dram["l1_w3"][e][:, ft * 128:(ft + 1) * 128].rearrange("(i p) f -> p i f", p=128), 1024)],
                    dst_free=df + free.get(("w3c", i), []),
                    ring=wsts, rkey="wsts", queue="scalar")
                return (r1, r3)

            wrdys = {}
            for e in range(E1):
                if e == 0:
                    wrdys[0] = load_ft(0, 0)
                    wrdys[1] = load_ft(0, 1)
                for ft in range(NFT):
                    i = (e * NFT + ft) % 4
                    if ft + 2 < NFT:
                        wrdys[ft + 2] = load_ft(e, ft + 2)
                    elif e + 1 < E1:
                        wrdys[ft + 2 - NFT] = load_ft(e + 1, ft + 2 - NFT)
                    wrdy1, wrdy3 = wrdys[ft]
                    par = ft % 2
                    gp = slot(0 + 2 * par)
                    up = slot(1 + 2 * par)
                    for it in range(NI):
                        P.op("tensor", lambda t, gp=gp, it=it, i=i, e=e: t.matmul(
                            gp, w1c[i][:, it * 128:(it + 1) * 128],
                            xnv[:, it, :, e * 256:(e + 1) * 256],
                            start=(it == 0), stop=(it == NI - 1)),
                            inc=(ct if it == NI - 1 else None),
                            waits=(wrdy1 + free.get(("gup", par), [])) if it == 0 else ())
                    for it in range(NI):
                        P.op("tensor", lambda t, up=up, it=it, i=i, e=e: t.matmul(
                            up, w3c[i][:, it * 128:(it + 1) * 128],
                            xnv[:, it, :, e * 256:(e + 1) * 256],
                            start=(it == 0), stop=(it == NI - 1)),
                            inc=(ct if it == NI - 1 else None),
                            waits=wrdy3 if it == 0 else ())
                    free[("w1c", i)] = [(ct, ct.n)]
                    free[("w3c", i)] = [(ct, ct.n)]
                    P.op("scalar", lambda s, gp=gp, i=i: s.activation(
                        gsb[i][:, :], gp, AF.Silu), inc=cs,
                        waits=[(ct, ct.n)] + free.get(("gsb", i), []))
                    P.op("vector", lambda v, up=up, i=i, ft=ft: v.tensor_tensor(
                        hT1[:, ft * 512:(ft + 1) * 512], gsb[i][:, :], up, ALU.mult),
                        inc=cv, waits=[(cs, cs.n)] + free.get(("hT1", ft), []))
                    free[("gsb", i)] = [(cv, cv.n)]
                    free[("gup", par)] = [(cv, cv.n)]
                hdone1 = [(cv, cv.n)]
                ypools = [pool[0], pool[1], pool[2], pool[3]]
                pwy = free.get("ypsf", [])
                w2rdys = {}

                def load_w2ft(ft2):
                    ring, rk, qu = ((wstf, "wstf", "sync") if ft2 % 2 == 0
                                    else (wsts, "wsts", "scalar"))
                    return emit_wload(
                        [(w2c[ft2 % 4][:, :],
                          dram["l1_w2"][e][ft2 * 128:(ft2 + 1) * 128, :], D)],
                        dst_free=free.get(("w2c", ft2 % 4), []),
                        ring=ring, rkey=rk, queue=qu)

                w2rdys[0] = load_w2ft(0)
                w2rdys[1] = load_w2ft(1)
                xrdy3s = {}
                for sl in range(2):
                    i, b, st2 = sl % 2, sl // 2, sl % 2
                    ti = e * 2 + st2
                    P.op("sync", lambda s, i=i, b=b, ti=ti: s.dma_start(
                        xres[i][:, :], x_att[b, ti * 128:(ti + 1) * 128, :]),
                        inc=dctr(f"xres{i}"), k=16, waits=free.get(("xres", i), []))
                    xrdy3s[sl] = dmadone(f"xres{i}")
                for ft in range(NFT):
                    i = ft % 4
                    if ft + 2 < NFT:
                        w2rdys[ft + 2] = load_w2ft(ft + 2)
                    w2rdy1 = w2rdys[ft]
                    for sl in range(4):
                        for half in range(2):
                            P.op("tensor", lambda t, sl=sl, half=half, ft=ft, i=i, e=e: t.matmul(
                                ypools[sl][:, half * 512:(half + 1) * 512],
                                hT1[:, ft * 512 + sl * 128: ft * 512 + (sl + 1) * 128],
                                w2c[i][:, half * 512:(half + 1) * 512],
                                start=(ft == 0), stop=(ft == NFT - 1)),
                                inc=(ct if (sl == 3 and half == 1) else None),
                                waits=(w2rdy1 + ((hdone1 + pwy) if ft == 0 else [])) if (sl == 0 and half == 0) else ())
                    free[("w2c", i)] = [(ct, ct.n)]
                free.update({("hT1", ft): [(ct, ct.n)] for ft in range(NFT)})
                ydone1 = [(ct, ct.n)]
                for sl in range(4):
                    i = sl % 2
                    b, st2 = sl // 2, sl % 2
                    ti = e * 2 + st2
                    xrdy3 = xrdy3s[sl]
                    for half in range(2):
                        P.op("vector", lambda v, i=i, half=half, sl=sl: v.tensor_tensor(
                            res_sb[i][:, half * 512:(half + 1) * 512],
                            ypools[sl][:, half * 512:(half + 1) * 512],
                            xres[i][:, half * 512:(half + 1) * 512], ALU.add),
                            inc=(cv if half == 1 else None),
                            waits=(ydone1 + xrdy3 + free.get(("ressb", i), [])) if half == 0 else ())
                    free[("xres", i)] = [(cv, cv.n)]
                    if sl < 2:
                        free[("gup", sl)] = [(cv, cv.n)]
                    if sl + 2 < 4:
                        sl2 = sl + 2
                        i2, b2_, st22 = sl2 % 2, sl2 // 2, sl2 % 2
                        ti2 = e * 2 + st22
                        P.op("sync", lambda s, i2=i2, b2_=b2_, ti2=ti2: s.dma_start(
                            xres[i2][:, :], x_att[b2_, ti2 * 128:(ti2 + 1) * 128, :]),
                            inc=dctr(f"xres{i2}"), k=16, waits=free.get(("xres", i2), []))
                        xrdy3s[sl2] = dmadone(f"xres{i2}")
                    P.op("sync", lambda s, i=i, b=b, ti=ti: s.dma_start(
                        y_out[b, ti * 128:(ti + 1) * 128, :], res_sb[i][:, :]),
                        inc=dctr(f"res{i}"), k=16, waits=[(cv, cv.n)])
                    free[("ressb", i)] = dmadone(f"res{i}")
                free["ypsf"] = [(cv, cv.n)]
            sf.close()
            sx1.close()

        if 1 not in layers:
            sln.close()
        finals = [(c.h, c.n) for c in ALLC]
        P.raw("sync", lambda e, fl=finals: [e.wait_ge(h, n) for h, n in fl if n > 0])
        P.realize(nc)
    return nc


def _emit_layer1_body(*a, **k):
    raise NotImplementedError("layer 1 not yet implemented")


FUSED = True


def kernel(**inputs):
    W = _host_prep(inputs)
    x = np.ascontiguousarray(np.asarray(inputs["L0_attn_x"], np.float32))
    shards = [x[2 * c:2 * c + 2] for c in range(NCORES)]

    l0_keys = ["l0_wqkv", "l0_wf", "l0_bf_row", "l0_bf_col", "l0_wout",
               "l0_bout_row", "l0_rwT", "l0_rb_row", "l0_w1", "l0_b1",
               "l0_w2", "l0_b2", "c_pen"]
    l1_keys = ["l1_wqT", "l1_wkT", "l1_wvT", "l1_woT", "l1_bqT", "l1_bkT",
               "l1_bv_row", "l1_bo_row", "l1_w1", "l1_w3", "l1_w2",
               "c_abias", "c_abias2", "c_abiasw", "c_maskm", "c_maskw"]

    if FUSED:
        nc = build_program((0, 1))
        maps = [dict({k: W[k] for k in l0_keys + l1_keys},
                     c_ident=W["c_ident"], x0=shards[c]) for c in range(NCORES)]
        res = run_bass_kernel_spmd(nc, maps, core_ids=list(range(NCORES)))
        return np.concatenate([np.asarray(r["y"]).reshape(NB, T, D) for r in res.results], axis=0)
    nc0 = build_program((0,))
    maps = [dict({k: W[k] for k in l0_keys}, c_ident=W["c_ident"], x0=shards[c])
            for c in range(NCORES)]
    res0 = run_bass_kernel_spmd(nc0, maps, core_ids=list(range(NCORES)))
    xmid = [np.ascontiguousarray(np.asarray(r["x_mid"]).reshape(NB, T, D)) for r in res0.results]
    nc1 = build_program((1,))
    maps = [dict({k: W[k] for k in l1_keys}, c_ident=W["c_ident"], x_mid=xmid[c])
            for c in range(NCORES)]
    res1 = run_bass_kernel_spmd(nc1, maps, core_ids=list(range(NCORES)))
    return np.concatenate([np.asarray(r["y"]).reshape(NB, T, D) for r in res1.results], axis=0)
